# version 01
# baseline (speedup 1.0000x reference)
import numpy as np
from contextlib import ExitStack

import concourse.bass as bass
import concourse.mybir as mybir

F32 = mybir.dt.float32
BF16 = mybir.dt.bfloat16
I32 = mybir.dt.int32
U32 = mybir.dt.uint32
FP8 = mybir.dt.float8e4
AF = mybir.ActivationFunctionType
ALU = mybir.AluOpType
AX = mybir.AxisListType

ENGINES = ("sync", "scalar", "vector", "tensor", "gpsimd")
S = 2048
DM = 2048
NTILE = S // 128


class Op:
    __slots__ = ("eng", "emit", "ins", "outs", "dma", "tag", "waits", "inc", "tick")

    def __init__(self, eng, emit, ins, outs, dma, tag):
        self.eng, self.emit, self.ins, self.outs = eng, emit, list(ins), list(outs)
        self.dma, self.tag = dma, tag
        self.inc = 16 if dma else 1
        self.waits = None
        self.tick = None


def _key(x):
    if isinstance(x, str):
        return x
    if hasattr(x, "tensor"):
        return x.tensor.name
    if hasattr(x, "name"):
        return x.name
    raise TypeError(f"bad buffer key {x!r}")


class Prog:

    def __init__(self, nc, dma_slots=6):
        self.nc = nc
        self.ops = []
        self.dma_slots = dma_slots

    def op(self, eng, emit, ins=(), outs=(), dma=False, tag=""):
        assert eng in ENGINES, eng
        self.ops.append(Op(eng, emit, [_key(i) for i in ins], [_key(o) for o in outs], dma, tag))

    def plan(self):
        state = {}
        counts = {}
        memo = {e: {} for e in ENGINES}
        dma_idx = {}
        for op in self.ops:
            if op.dma:
                i = dma_idx.get(op.eng, 0)
                dma_idx[op.eng] = i + 1
                sk = f"dma_{op.eng}{i % self.dma_slots}"
            else:
                sk = "tick_" + op.eng
            needs = {}
            for b in op.ins:
                st = state.get(b)
                if st and st["w"]:
                    wsk, wv = st["w"]
                    needs[wsk] = max(needs.get(wsk, 0), wv)
            for b in op.outs:
                st = state.get(b)
                if st:
                    if st["w"]:
                        wsk, wv = st["w"]
                        needs[wsk] = max(needs.get(wsk, 0), wv)
                    for rsk, rv in st["r"].items():
                        needs[rsk] = max(needs.get(rsk, 0), rv)
            if op.dma:
                prev = counts.get(sk, 0)
                if prev:
                    needs[sk] = max(needs.get(sk, 0), prev)
            if op.eng == "tensor" and not op.dma:
                needs.pop("tick_tensor", None)
            waits = []
            m = memo[op.eng]
            for wsk, wv in sorted(needs.items()):
                if m.get(wsk, 0) >= wv:
                    continue
                waits.append((wsk, wv))
                m[wsk] = wv
            op.waits = waits
            counts[sk] = counts.get(sk, 0) + op.inc
            op.tick = counts[sk]
            for b in op.ins:
                st = state.setdefault(b, {"w": None, "r": {}})
                st["r"][sk] = op.tick
            for b in op.outs:
                state[b] = {"w": (sk, op.tick), "r": {}}
        self.final_counts = counts

    def build(self):
        self.plan()
        nc = self.nc
        import contextlib
        self._es = contextlib.ExitStack()
        sems = {k: self._es.enter_context(nc.semaphore(k)) for k in sorted(self.final_counts)}
        self.sems = sems
        ops = self.ops
        dma_slots = self.dma_slots

        with nc.Block() as block:
            def section(eng_name, eng_obj):
                dma_i = 0
                for op in ops:
                    if op.eng != eng_name:
                        continue
                    if op.dma:
                        sk = f"dma_{eng_name}{dma_i % dma_slots}"
                        dma_i += 1
                    else:
                        sk = "tick_" + eng_name
                    for wsk, wv in op.waits:
                        eng_obj.wait_ge(sems[wsk], wv)
                    ins = op.emit(eng_obj)
                    if ins is None:
                        raise RuntimeError(f"op {op.tag} returned None")
                    ins.then_inc(sems[sk], op.inc)

            engs = sorted({o.eng for o in ops})
            if "sync" in engs:
                @block.sync
                def _(e):
                    section("sync", e)
            if "scalar" in engs:
                @block.scalar
                def _(e):
                    section("scalar", e)
            if "vector" in engs:
                @block.vector
                def _(e):
                    section("vector", e)
            if "tensor" in engs:
                @block.tensor
                def _(e):
                    section("tensor", e)
            if "gpsimd" in engs:
                @block.gpsimd
                def _(e):
                    section("gpsimd", e)


def ap3(t, offset, dims):
    return bass.AP(t, offset, dims)


class B:

    def __init__(self, nc):
        self.nc = nc
        self.p = Prog(nc)
        self.es = ExitStack()
        self._uid = 0

    def uid(self):
        self._uid += 1
        return self._uid

    def sb(self, name, shape, dt=F32):
        return self.es.enter_context(self.nc.sbuf_tensor(name, shape, dt))

    def ps(self, name, shape, dt=F32):
        return self.es.enter_context(self.nc.psum_tensor(name, shape, dt))

    def dram(self, name, shape, dt=F32, kind=None):
        if kind:
            return self.nc.dram_tensor(name, shape, dt, kind=kind)
        return self.nc.dram_tensor(name, shape, dt)

    def dma(self, q, dst, src, tag=""):
        self.p.op(q, lambda e, d=dst, s=src: e.dma_start(d, s), ins=[src], outs=[dst], dma=True, tag=tag)

    def mm(self, psum_ap, pairs, ins, tag="", extra=None):
        def emit(e):
            last = None
            n = len(pairs)
            for i, (l, r) in enumerate(pairs):
                last = e.matmul(psum_ap, l, r, start=(i == 0),
                                stop=(i == n - 1 and extra is None))
            if extra is not None:
                last = e.matmul(psum_ap, extra[0], extra[1], start=(n == 0), stop=True)
            return last
        self.p.op("tensor", emit, ins=ins, outs=[psum_ap], tag=tag)

    def transpose_blocks(self, src_tile, cols, psum_bf, dst_writer, ins, tag=""):
        raise NotImplementedError


def qdq_emit(b, src_ap, dst_ap, w, scr, ins, tag, mode="pow2"):
    p = b.p
    nb = w // 32
    amax, ibits, scl, inv, t, q8 = scr["amax"], scr["ibits"], scr["scl"], scr["inv"], scr["t"], scr["q8"]
    src3 = src_ap.rearrange("p (b t) -> p b t", t=32)
    p.op("vector", lambda e: e.tensor_reduce(amax[:, :nb, None], src3, AX.X, ALU.max,
                                             apply_absolute_value=True),
         ins=ins, outs=[amax], tag=tag + ".amax")
    if mode == "pow2":
        p.op("vector", lambda e: e.tensor_single_scalar(ibits[:, :nb], amax[:, :nb].bitcast(I32),
                                                        23, ALU.logical_shift_right),
             ins=[amax], outs=[ibits], tag=tag + ".e")
        p.op("vector", lambda e: e.tensor_single_scalar(ibits[:, :nb], ibits[:, :nb], 9, ALU.max),
             ins=[ibits], outs=[ibits], tag=tag + ".clampe")
        p.op("vector", lambda e: e.tensor_single_scalar(scl[:, :nb].bitcast(I32), ibits[:, :nb],
                                                        -8, ALU.add),
             ins=[ibits], outs=[scl], tag=tag + ".scl0")
        p.op("vector", lambda e: e.tensor_single_scalar(scl[:, :nb].bitcast(I32), scl[:, :nb].bitcast(I32),
                                                        23, ALU.logical_shift_left),
             ins=[scl], outs=[scl], tag=tag + ".scl")
        p.op("vector", lambda e: e.tensor_scalar(inv[:, :nb].bitcast(I32), ibits[:, :nb], -1, 262,
                                                 ALU.mult, ALU.add),
             ins=[ibits], outs=[inv], tag=tag + ".inv0")
        p.op("vector", lambda e: e.tensor_single_scalar(inv[:, :nb].bitcast(I32), inv[:, :nb].bitcast(I32),
                                                        23, ALU.logical_shift_left),
             ins=[inv], outs=[inv], tag=tag + ".inv")
    else:
        p.op("vector", lambda e: e.tensor_scalar(scl[:, :nb], amax[:, :nb], 1.0 / 448.0, 1e-30,
                                                 ALU.mult, ALU.max),
             ins=[amax], outs=[scl], tag=tag + ".scl")
        p.op("vector", lambda e: e.reciprocal(inv[:, :nb], scl[:, :nb]),
             ins=[scl], outs=[inv], tag=tag + ".inv")
    t3 = t[:, :w].rearrange("p (b t) -> p b t", t=32)
    p.op("vector", lambda e: e.tensor_tensor(t3, src3, inv[:, :nb, None].broadcast_to([128, nb, 32]),
                                             ALU.mult),
         ins=ins + [inv], outs=[t], tag=tag + ".mul")
    if mode == "pow2":
        p.op("vector", lambda e: e.tensor_scalar(t[:, :w], t[:, :w], 448.0, -448.0, ALU.min, ALU.max),
             ins=[t], outs=[t], tag=tag + ".clip")
    p.op("scalar", lambda e: e.activation(q8[:, :w], t[:, :w], AF.Copy, scale=0.5),
         ins=[t], outs=[q8], tag=tag + ".q8")
    p.op("vector", lambda e: e.tensor_scalar_mul(scl[:, :nb], scl[:, :nb], 2.0),
         ins=[scl], outs=[scl], tag=tag + ".s2")
    q83 = q8[:, :w].rearrange("p (b t) -> p b t", t=32)
    d3 = dst_ap.rearrange("p (b t) -> p b t", t=32)
    p.op("vector", lambda e: e.tensor_tensor(d3, q83, scl[:, :nb, None].broadcast_to([128, nb, 32]),
                                             ALU.mult),
         ins=[q8, scl], outs=[dst_ap], tag=tag + ".dq")


def mk_qdq_scratch(b, w, name):
    nb = w // 32
    return dict(
        amax=b.sb(f"{name}_amax", [128, nb]),
        ibits=b.sb(f"{name}_ib", [128, nb], I32),
        scl=b.sb(f"{name}_scl", [128, nb]),
        inv=b.sb(f"{name}_inv", [128, nb]),
        t=b.sb(f"{name}_t", [128, w]),
        q8=b.sb(f"{name}_q8", [128, w], FP8),
    )


def emit_weight_qdqT(b, w_d, qT_d, O, K, scr, wrow, wbf, ptr_bf, stage, ident_bf, tag, mode="pow2"):
    p = b.p
    KS = 2048 if K >= 2048 else K
    for r in range(O // 128):
        for ks in range(K // KS):
            b.dma("sync", wrow[:, :KS], w_d[r * 128:(r + 1) * 128, ks * KS:(ks + 1) * KS],
                  tag=f"{tag}.ld{r}.{ks}")
            qdq_emit(b, wrow[:, :KS], wbf[:, :KS], KS, scr, ins=[wrow], tag=f"{tag}.q{r}.{ks}", mode=mode)
            for g in range(KS // 1024):
                def emit_tr(e, g=g):
                    last = None
                    for t in range(8):
                        c = g * 8 + t
                        last = e.transpose(ptr_bf[:, t * 128:(t + 1) * 128],
                                           wbf[:, c * 128:(c + 1) * 128], ident_bf[:, :])
                    return last
                p.op("tensor", emit_tr, ins=[wbf, ident_bf], outs=[ptr_bf], tag=f"{tag}.tr{r}.{ks}.{g}")
                p.op("vector", lambda e: e.tensor_copy(stage[:, :1024], ptr_bf[:, :1024]),
                     ins=[ptr_bf], outs=[stage], tag=f"{tag}.ev{r}.{ks}.{g}")
                c0 = ks * (KS // 128) + g * 8
                dst = ap3(qT_d, c0 * 128 * O + r * 128, [[O, 128], [128 * O, 8], [1, 128]])
                b.dma("scalar", dst, stage[:, :1024].rearrange("p (b c) -> p b c", c=128),
                      tag=f"{tag}.st{r}.{ks}.{g}")


def emit_ln_xnT(b, x_in, xnT_d, xt, xn, xnb, sq, stats, ptr_bf, stage, ident_bf,
                xn_d=None, tag="ln", extra_tile_cb=None, qdq_scr=None, qdq_mode="div448"):
    p = b.p
    m, ssq, var, rstd = stats["m"], stats["ssq"], stats["var"], stats["rstd"]
    eps = stats["eps"]
    p.op("vector", lambda e: e.memset(eps[:, :], 1e-5), ins=[], outs=[eps], tag=tag + ".eps")
    for si in range(NTILE):
        b.dma("sync", xt[:, :], x_in[si * 128:(si + 1) * 128, :], tag=f"{tag}.ld{si}")
        p.op("vector", lambda e: e.reduce_sum(m[:, :], xt[:, :], AX.X),
             ins=[xt], outs=[m], tag=f"{tag}.sum{si}")
        p.op("scalar", lambda e: e.activation(sq[:, :], xt[:, :], AF.Square, accum_out=ssq[:, :]),
             ins=[xt], outs=[sq, ssq], tag=f"{tag}.sq{si}")
        p.op("vector", lambda e: e.tensor_scalar_mul(m[:, :], m[:, :], 1.0 / DM),
             ins=[m], outs=[m], tag=f"{tag}.m{si}")
        p.op("vector", lambda e: e.tensor_tensor(var[:, :], m[:, :], m[:, :], ALU.mult),
             ins=[m], outs=[var], tag=f"{tag}.m2{si}")
        p.op("vector", lambda e: e.scalar_tensor_tensor(var[:, :], ssq[:, :], 1.0 / DM, var[:, :],
                                                        ALU.mult, ALU.subtract),
             ins=[ssq, var], outs=[var], tag=f"{tag}.var{si}")
        p.op("scalar", lambda e: e.activation(rstd[:, :], var[:, :], AF.Sqrt, bias=eps[:, :]),
             ins=[var, eps], outs=[rstd], tag=f"{tag}.sd{si}")
        p.op("vector", lambda e: e.reciprocal(rstd[:, :], rstd[:, :]),
             ins=[rstd], outs=[rstd], tag=f"{tag}.rs{si}")
        p.op("vector", lambda e: e.tensor_scalar(xn[:, :], xt[:, :], m[:, :], rstd[:, :],
                                                 ALU.subtract, ALU.mult),
             ins=[xt, m, rstd], outs=[xn], tag=f"{tag}.xn{si}")
        if xn_d is not None:
            b.dma("scalar", xn_d[si * 128:(si + 1) * 128, :], xn[:, :], tag=f"{tag}.stxn{si}")
        if qdq_scr is not None:
            qdq_emit(b, xn[:, :], xnb[:, :], DM, qdq_scr, ins=[xn], tag=f"{tag}.qdq{si}",
                     mode=qdq_mode)
        else:
            p.op("vector", lambda e: e.tensor_copy(xnb[:, :], xn[:, :]),
                 ins=[xn], outs=[xnb], tag=f"{tag}.bf{si}")
        for g in range(2):
            def emit_tr(e, g=g):
                last = None
                for t in range(8):
                    c = g * 8 + t
                    last = e.transpose(ptr_bf[:, t * 128:(t + 1) * 128],
                                       xnb[:, c * 128:(c + 1) * 128], ident_bf[:, :])
                return last
            p.op("tensor", emit_tr, ins=[xnb, ident_bf], outs=[ptr_bf], tag=f"{tag}.tr{si}.{g}")
            p.op("vector", lambda e: e.tensor_copy(stage[:, :1024], ptr_bf[:, :1024]),
                 ins=[ptr_bf], outs=[stage], tag=f"{tag}.ev{si}.{g}")
            dst = ap3(xnT_d, (g * 8) * 128 * S + si * 128, [[S, 128], [128 * S, 8], [1, 128]])
            b.dma("scalar", dst, stage[:, :1024].rearrange("p (b c) -> p b c", c=128),
                  tag=f"{tag}.st{si}.{g}")
        if extra_tile_cb is not None:
            extra_tile_cb(si)


def build_l1ffn():
    I = 8192
    SUP, NSUB = 256, 2
    NSUP = S // SUP
    nc = bass.Bass("TRN2", target_bir_lowering=False)
    b = B(nc)
    x_in = b.dram("x_in", [S, DM], kind="ExternalInput")
    gate_w = b.dram("gate_w", [I, DM], kind="ExternalInput")
    up_w = b.dram("up_w", [I, DM], kind="ExternalInput")
    down_w = b.dram("down_w", [DM, I], kind="ExternalInput")
    ident_in = b.dram("ident_bf", [128, 128], BF16, kind="ExternalInput")
    x_out = b.dram("x_out", [S, DM], kind="ExternalOutput")

    xnT_d = b.dram("xnT_d", [DM, S], BF16)
    gq_d = b.dram("gq_d", [DM, I], BF16)
    uq_d = b.dram("uq_d", [DM, I], BF16)
    dq_d = b.dram("dq_d", [I, DM], BF16)

    p = b.p
    ident_bf = b.sb("identb", [128, 128], BF16)
    b.dma("sync", ident_bf[:, :], ident_in[:, :], tag="ident")

    wrow = b.sb("wrow", [128, DM])
    wbf = b.sb("wbf", [128, DM], BF16)
    ptr_bf = b.ps("ptr_bf", [128, 1024], BF16)
    stage = b.sb("stage", [128, 1024], BF16)
    scrW = mk_qdq_scratch(b, DM, "scrW")
    emit_weight_qdqT(b, gate_w, gq_d, I, DM, scrW, wrow, wbf, ptr_bf, stage, ident_bf, "gw", mode="div448")
    emit_weight_qdqT(b, up_w, uq_d, I, DM, scrW, wrow, wbf, ptr_bf, stage, ident_bf, "uw", mode="div448")
    emit_weight_qdqT(b, down_w, dq_d, DM, I, scrW, wrow, wbf, ptr_bf, stage, ident_bf, "dw", mode="div448")

    xt = b.sb("xt", [128, DM])
    xn = b.sb("xn", [128, DM])
    xnb = b.sb("xnb", [128, DM], BF16)
    sq = b.sb("sq", [128, DM])
    stats = dict(m=b.sb("m", [128, 1]), ssq=b.sb("ssq", [128, 1]),
                 var=b.sb("var", [128, 1]), rstd=b.sb("rstd", [128, 1]),
                 eps=b.sb("epsc", [128, 1]))
    emit_ln_xnT(b, x_in, xnT_d, xt, xn, xnb, sq, stats, ptr_bf, stage, ident_bf,
                qdq_scr=scrW, qdq_mode="div448")

    xnT_s = b.sb("xnT_s", [128, NTILE * SUP], BF16)
    gsl = [b.sb(f"gsl{i}", [128, NTILE * 512], BF16) for i in range(2)]
    usl = [b.sb(f"usl{i}", [128, NTILE * 512], BF16) for i in range(2)]
    hh = b.sb("hh", [128, 512])
    t1 = b.sb("t1", [128, 512])
    hqb = b.sb("hqb", [128, 512], BF16)
    hqT = b.sb("hqT", [128, (I // 128) * SUP], BF16)
    dsl = [b.sb(f"dsl{i}", [128, DM], BF16) for i in range(2)]
    acc = b.sb("acc", [128, NSUB * DM])
    scrH = mk_qdq_scratch(b, 512, "scrH")
    pA = b.ps("pA", [128, 512])
    pB = b.ps("pB", [128, 512])
    pQ = b.ps("pQ", [128, DM])
    xres = b.sb("xres", [128, DM])
    outt = b.sb("outt", [128, DM])

    for u in range(NSUP):
        src = ap3(xnT_d, u * SUP, [[S, 128], [128 * S, NTILE], [1, SUP]])
        b.dma("sync", xnT_s[:, :].rearrange("p (b c) -> p b c", c=SUP), src, tag=f"u{u}.xnT")
        for j in range(I // 512):
            g_b, u_b = gsl[j % 2], usl[j % 2]
            gsrc = ap3(gq_d, j * 512, [[I, 128], [128 * I, NTILE], [1, 512]])
            usrc = ap3(uq_d, j * 512, [[I, 128], [128 * I, NTILE], [1, 512]])
            b.dma("sync", g_b[:, :].rearrange("p (b c) -> p b c", c=512), gsrc, tag=f"u{u}.g{j}")
            b.dma("sync", u_b[:, :].rearrange("p (b c) -> p b c", c=512), usrc, tag=f"u{u}.u{j}")
            for sub in range(NSUB):
                pairs_g = [(xnT_s[:, kc * SUP + sub * 128: kc * SUP + sub * 128 + 128],
                            g_b[:, kc * 512:(kc + 1) * 512]) for kc in range(NTILE)]
                pairs_u = [(xnT_s[:, kc * SUP + sub * 128: kc * SUP + sub * 128 + 128],
                            u_b[:, kc * 512:(kc + 1) * 512]) for kc in range(NTILE)]
                b.mm(pA[:, :], pairs_g, ins=[xnT_s, g_b], tag=f"u{u}.j{j}.s{sub}.g")
                b.mm(pB[:, :], pairs_u, ins=[xnT_s, u_b], tag=f"u{u}.j{j}.s{sub}.u")
                p.op("scalar", lambda e: e.activation(t1[:, :], pA[:, :], AF.Silu),
                     ins=[pA], outs=[t1], tag=f"u{u}.j{j}.s{sub}.si")
                p.op("vector", lambda e: e.tensor_tensor(hh[:, :], t1[:, :], pB[:, :], ALU.mult),
                     ins=[t1, pB], outs=[hh], tag=f"u{u}.j{j}.s{sub}.h")
                qdq_emit(b, hh[:, :], hqb[:, :], 512, scrH, ins=[hh],
                         tag=f"u{u}.j{j}.s{sub}.q", mode="div448")
                def emit_tr(e):
                    last = None
                    for t in range(4):
                        last = e.transpose(ptr_bf[:, t * 128:(t + 1) * 128],
                                           hqb[:, t * 128:(t + 1) * 128], ident_bf[:, :])
                    return last
                p.op("tensor", emit_tr, ins=[hqb, ident_bf], outs=[ptr_bf], tag=f"u{u}.j{j}.s{sub}.tr")
                dst = ap3(hqT, (4 * j) * SUP + sub * 128, [[(I // 128) * SUP, 128], [SUP, 4], [1, 128]])
                p.op("vector", lambda e, dst=dst: e.tensor_copy(
                    dst, ptr_bf[:, :512].rearrange("p (b c) -> p b c", c=128)),
                    ins=[ptr_bf], outs=[hqT], tag=f"u{u}.j{j}.s{sub}.sc")
        for sub in range(NSUB):
            for kc in range(I // 128):
                d_b = dsl[kc % 2]
                b.dma("sync", d_b[:, :], dq_d[kc * 128:(kc + 1) * 128, :], tag=f"u{u}.s{sub}.d{kc}")
                def emit_mm(e, kc=kc, sub=sub, d_b=d_b):
                    last = None
                    for n in range(4):
                        last = e.matmul(pQ[:, n * 512:(n + 1) * 512],
                                        hqT[:, kc * SUP + sub * 128: kc * SUP + sub * 128 + 128],
                                        d_b[:, n * 512:(n + 1) * 512],
                                        start=(kc == 0), stop=(kc == I // 128 - 1))
                    return last
                p.op("tensor", emit_mm, ins=[hqT, d_b, pQ], outs=[pQ], tag=f"u{u}.s{sub}.mm{kc}")
            p.op("vector", lambda e, sub=sub: e.tensor_copy(acc[:, sub * DM:(sub + 1) * DM], pQ[:, :]),
                 ins=[pQ], outs=[acc], tag=f"u{u}.s{sub}.acc")
        for sub in range(NSUB):
            r0 = u * SUP + sub * 128
            b.dma("sync", xres[:, :], x_in[r0:r0 + 128, :], tag=f"u{u}.s{sub}.lx")
            p.op("vector", lambda e, sub=sub: e.scalar_tensor_tensor(
                outt[:, :], acc[:, sub * DM:(sub + 1) * DM], 0.5, xres[:, :], ALU.mult, ALU.add),
                ins=[acc, xres], outs=[outt], tag=f"u{u}.s{sub}.res")
            b.dma("scalar", x_out[r0:r0 + 128, :], outt[:, :], tag=f"u{u}.s{sub}.st")
    p.op("sync", lambda e: e.nop(), ins=[x_out], tag="final")
    p.build()
    return nc, b


def l1ffn_inputs(x_full_core, gate_w, up_w, down_w):
    import ml_dtypes
    return {
        "x_in": np.ascontiguousarray(x_full_core, np.float32),
        "gate_w": np.ascontiguousarray(gate_w, np.float32),
        "up_w": np.ascontiguousarray(up_w, np.float32),
        "down_w": np.ascontiguousarray(down_w, np.float32),
        "ident_bf": np.eye(128, dtype=ml_dtypes.bfloat16),
    }


def build_l0ffn():
    E, HID = 8, 2048
    SUP, NSUB = 512, 4
    NSUP = S // SUP
    nc = bass.Bass("TRN2", target_bir_lowering=False)
    b = B(nc)
    x_in = b.dram("x_in", [S, DM], kind="ExternalInput")
    gate_wT = b.dram("gate_wT", [DM, E], kind="ExternalInput")
    ebias = b.dram("ebias", [1, E], kind="ExternalInput")
    w13 = b.dram("w13", [E * 2 * HID, DM], kind="ExternalInput")
    w2 = b.dram("w2", [E * DM, HID], kind="ExternalInput")
    sw1T = b.dram("sw1T", [DM, HID], kind="ExternalInput")
    sw3T = b.dram("sw3T", [DM, HID], kind="ExternalInput")
    sw2T = b.dram("sw2T", [HID, DM], kind="ExternalInput")
    ident_in = b.dram("ident_bf", [128, 128], BF16, kind="ExternalInput")
    identf_in = b.dram("ident_f32", [128, 128], F32, kind="ExternalInput")
    ones_in = b.dram("ones1", [1, 128], kind="ExternalInput")
    iota_in = b.dram("iota8", [1, E], kind="ExternalInput")
    x_out = b.dram("x_out", [S, DM], kind="ExternalOutput")

    xnT_d = b.dram("xnT_d", [DM, S], BF16)
    A_d = b.dram("A_d", [S, DM], BF16)
    Bq_d = b.dram("B_d", [S, DM], BF16)
    w13qT_d = b.dram("w13qT_d", [E * DM, 2 * HID], BF16)
    w2qT_d = b.dram("w2qT_d", [E * HID, DM], BF16)
    sw1b_d = b.dram("sw1b_d", [DM, HID], BF16)
    sw3b_d = b.dram("sw3b_d", [DM, HID], BF16)
    sw2b_d = b.dram("sw2b_d", [HID, DM], BF16)

    p = b.p
    ident_bf = b.sb("identb", [128, 128], BF16)
    ident_f = b.sb("identf", [128, 128], F32)
    ones1 = b.sb("ones1s", [1, 128])
    iota8 = b.sb("iota8s", [1, E])
    ebias_s = b.sb("ebias_s", [1, E])
    b.dma("sync", ident_bf[:, :], ident_in[:, :], tag="c0")
    b.dma("sync", ident_f[:, :], identf_in[:, :], tag="c1")
    b.dma("sync", ones1[:, :], ones_in[:, :], tag="c2")
    b.dma("sync", iota8[:, :], iota_in[:, :], tag="c3")
    b.dma("sync", ebias_s[:, :], ebias[:, :], tag="c4")

    ptr_bf = b.ps("ptr_bf", [128, 1024], BF16)
    pA = b.ps("pA", [128, 512])
    pB = b.ps("pB", [128, 512])
    pQ = b.ps("pQ", [128, DM])
    stage = b.sb("stage", [128, 1024], BF16)

    bias_b = b.sb("bias_b", [128, E])
    iota_b = b.sb("iota_b", [128, E])
    b.mm(pA[:, :E], [(ones1[:1, :], ebias_s[:1, :])], ins=[ones1, ebias_s], tag="bb")
    p.op("vector", lambda e: e.tensor_copy(bias_b[:, :], pA[:, :E]), ins=[pA], outs=[bias_b], tag="bbe")
    b.mm(pB[:, :E], [(ones1[:1, :], iota8[:1, :])], ins=[ones1, iota8], tag="ib")
    p.op("vector", lambda e: e.tensor_copy(iota_b[:, :], pB[:, :E]), ins=[pB], outs=[iota_b], tag="ibe")

    acc = b.sb("acc", [128, NSUB * DM])
    wrow = acc[:, :DM]
    wbf = b.sb("wbf", [128, DM], BF16)
    scrW = mk_qdq_scratch(b, DM, "scrW")
    for e in range(E):
        emit_weight_qdqT(b, w13[e * 2 * HID:(e + 1) * 2 * HID, :],
                         ap3(w13qT_d, e * DM * 2 * HID, [[1, 0]]).tensor if False else None,
                         2 * HID, DM, scrW, wrow, wbf, ptr_bf, stage, ident_bf,
                         f"w13.{e}", mode="pow2") if False else None
    def wprep(w_src, q_dst, row0, O, K, tag):
        KS = 2048 if K >= 2048 else K
        for r in range(O // 128):
            for ks in range(K // KS):
                b.dma("sync", wrow[:, :KS], w_src[row0 + r * 128: row0 + (r + 1) * 128,
                                                  ks * KS:(ks + 1) * KS], tag=f"{tag}.ld{r}.{ks}")
                qdq_emit(b, wrow[:, :KS], wbf[:, :KS], KS, scrW, ins=[wrow],
                         tag=f"{tag}.q{r}.{ks}", mode="pow2")
                for g in range(KS // 1024):
                    def emit_tr(e2, g=g):
                        last = None
                        for t in range(8):
                            c = g * 8 + t
                            last = e2.transpose(ptr_bf[:, t * 128:(t + 1) * 128],
                                                wbf[:, c * 128:(c + 1) * 128], ident_bf[:, :])
                        return last
                    p.op("tensor", emit_tr, ins=[wbf, ident_bf], outs=[ptr_bf], tag=f"{tag}.tr{r}.{g}")
                    p.op("vector", lambda e2: e2.tensor_copy(stage[:, :1024], ptr_bf[:, :1024]),
                         ins=[ptr_bf], outs=[stage], tag=f"{tag}.ev{r}.{g}")
                    c0 = ks * (KS // 128) + g * 8
                    dst = ap3(q_dst[0], q_dst[1] + c0 * 128 * O + r * 128,
                              [[O, 128], [128 * O, 8], [1, 128]])
                    b.dma("scalar", dst, stage[:, :1024].rearrange("p (b c) -> p b c", c=128),
                          tag=f"{tag}.st{r}.{g}")

    for e in range(E):
        wprep(w13, (w13qT_d, e * DM * 2 * HID), e * 2 * HID, 2 * HID, DM, f"w13.{e}")
        wprep(w2, (w2qT_d, e * HID * DM), e * DM, DM, HID, f"w2.{e}")
    for (src, dst, R, C, tg) in [(sw1T, sw1b_d, DM, HID, "s1"), (sw3T, sw3b_d, DM, HID, "s3"),
                                 (sw2T, sw2b_d, HID, DM, "s2")]:
        for r in range(R // 128):
            b.dma("sync", wrow[:, :C], src[r * 128:(r + 1) * 128, :], tag=f"{tg}.ld{r}")
            p.op("vector", lambda e2: e2.tensor_copy(wbf[:, :C], wrow[:, :C]),
                 ins=[wrow], outs=[wbf], tag=f"{tg}.c{r}")
            b.dma("scalar", dst[r * 128:(r + 1) * 128, :], wbf[:, :C], tag=f"{tg}.st{r}")

    xt = b.sb("xt", [128, DM])
    xn = b.sb("xn", [128, DM])
    ab_raw = b.sb("ab_raw", [128, DM])
    sq = ab_raw
    stats = dict(m=b.sb("m", [128, 1]), ssq=b.sb("ssq", [128, 1]),
                 var=b.sb("var", [128, 1]), rstd=b.sb("rstd", [128, 1]),
                 eps=b.sb("epsc", [128, 1]))
    xnTf = b.sb("xnTf", [128, NTILE * 128])
    xnTb = b.sb("xnTb", [128, NTILE * 128], BF16)
    sc = b.sb("sc", [128, E])
    scb = b.sb("scb", [128, E])
    mx8 = b.sb("mx8", [128, E])
    mi8 = b.sb("mi8", [128, E], U32)
    mif = b.sb("mif", [128, E])
    eqt = b.sb("eqt", [128, E])
    tvals = b.sb("tvals", [128, 4])
    wts = b.sb("wts", [128, 2])
    eq_sb = b.sb("eq_sb", [128, NTILE * 2 * E])
    ab_bf = b.sb("ab_bf", [128, DM], BF16)
    m_, ssq_, var_, rstd_, eps_ = stats["m"], stats["ssq"], stats["var"], stats["rstd"], stats["eps"]
    p.op("vector", lambda e: e.memset(eps_[:, :], 1e-5), ins=[], outs=[eps_], tag="eps")
    for si in range(NTILE):
        b.dma("sync", xt[:, :], x_in[si * 128:(si + 1) * 128, :], tag=f"r.ld{si}")
        p.op("vector", lambda e: e.reduce_sum(m_[:, :], xt[:, :], AX.X), ins=[xt], outs=[m_], tag=f"r.s{si}")
        p.op("scalar", lambda e: e.activation(sq[:, :], xt[:, :], AF.Square, accum_out=ssq_[:, :]),
             ins=[xt], outs=[sq, ssq_], tag=f"r.q{si}")
        p.op("vector", lambda e: e.tensor_scalar_mul(m_[:, :], m_[:, :], 1.0 / DM),
             ins=[m_], outs=[m_], tag=f"r.m{si}")
        p.op("vector", lambda e: e.tensor_tensor(var_[:, :], m_[:, :], m_[:, :], ALU.mult),
             ins=[m_], outs=[var_], tag=f"r.m2{si}")
        p.op("vector", lambda e: e.scalar_tensor_tensor(var_[:, :], ssq_[:, :], 1.0 / DM, var_[:, :],
                                                        ALU.mult, ALU.subtract),
             ins=[ssq_, var_], outs=[var_], tag=f"r.v{si}")
        p.op("scalar", lambda e: e.activation(rstd_[:, :], var_[:, :], AF.Sqrt, bias=eps_[:, :]),
             ins=[var_, eps_], outs=[rstd_], tag=f"r.sd{si}")
        p.op("vector", lambda e: e.reciprocal(rstd_[:, :], rstd_[:, :]), ins=[rstd_], outs=[rstd_], tag=f"r.r{si}")
        p.op("vector", lambda e: e.tensor_scalar(xn[:, :], xt[:, :], m_[:, :], rstd_[:, :],
                                                 ALU.subtract, ALU.mult),
             ins=[xt, m_, rstd_], outs=[xn], tag=f"r.xn{si}")
        for g in range(4):
            def emit_tr(e2, g=g):
                last = None
                for t in range(4):
                    c = g * 4 + t
                    last = e2.transpose(pA[:, t * 128:(t + 1) * 128],
                                        xn[:, c * 128:(c + 1) * 128], ident_f[:, :])
                return last
            p.op("tensor", emit_tr, ins=[xn, ident_f], outs=[pA], tag=f"r.tr{si}.{g}")
            p.op("vector", lambda e2, g=g: e2.tensor_copy(xnTf[:, g * 512:(g + 1) * 512], pA[:, :]),
                 ins=[pA], outs=[xnTf], tag=f"r.ev{si}.{g}")
        p.op("vector", lambda e2: e2.tensor_copy(xnTb[:, :], xnTf[:, :]),
             ins=[xnTf], outs=[xnTb], tag=f"r.bf{si}")
        dst = ap3(xnT_d, si * 128, [[S, 128], [128 * S, NTILE], [1, 128]])
        b.dma("scalar", dst, xnTb[:, :].rearrange("p (b c) -> p b c", c=128), tag=f"r.stT{si}")
        pairs = [(xnTf[:, kc * 128:(kc + 1) * 128], None) for kc in range(NTILE)]
        def emit_gate(e2):
            last = None
            for kc in range(NTILE):
                last = e2.matmul(pB[:, :E], xnTf[:, kc * 128:(kc + 1) * 128],
                                 gwT_sb[:, kc * E:(kc + 1) * E], start=(kc == 0), stop=(kc == NTILE - 1))
            return last
        p.op("tensor", emit_gate, ins=[xnTf, "gwT_sb"], outs=[pB], tag=f"r.g{si}")
        p.op("scalar", lambda e2: e2.activation(sc[:, :], pB[:, :E], AF.Sigmoid),
             ins=[pB], outs=[sc], tag=f"r.sig{si}")
        p.op("vector", lambda e2: e2.tensor_tensor(scb[:, :], sc[:, :], bias_b[:, :], ALU.add),
             ins=[sc, bias_b], outs=[scb], tag=f"r.sb{si}")
        p.op("vector", lambda e2: e2.max(mx8[:, :], scb[:, :]), ins=[scb], outs=[mx8], tag=f"r.mx{si}")
        p.op("vector", lambda e2: e2.max_index(mi8[:, :], mx8[:, :], scb[:, :]),
             ins=[mx8, scb], outs=[mi8], tag=f"r.mi{si}")
        p.op("vector", lambda e2: e2.tensor_copy(mif[:, :], mi8[:, :]), ins=[mi8], outs=[mif], tag=f"r.mf{si}")
        for k2, col in ((0, 0), (1, 1)):
            eqdst = eq_sb[:, si * 2 * E + k2 * E: si * 2 * E + (k2 + 1) * E]
            p.op("vector", lambda e2, col=col, eqdst=eqdst: e2.tensor_single_scalar(
                eqdst, iota_b[:, :], mif[:, col:col + 1], ALU.is_equal),
                ins=[iota_b, mif], outs=[eq_sb], tag=f"r.eq{k2}.{si}")
            p.op("vector", lambda e2, k2=k2, eqdst=eqdst: e2.tensor_tensor(
                eqt[:, :], eqdst, sc[:, :], ALU.mult),
                ins=[eq_sb, sc], outs=[eqt], tag=f"r.eqm{k2}.{si}")
            p.op("vector", lambda e2, k2=k2: e2.reduce_sum(tvals[:, k2:k2 + 1], eqt[:, :], AX.X),
                 ins=[eqt], outs=[tvals], tag=f"r.t{k2}.{si}")
        p.op("vector", lambda e2: e2.tensor_tensor(tvals[:, 2:3], tvals[:, 0:1], tvals[:, 1:2], ALU.add),
             ins=[tvals], outs=[tvals], tag=f"r.su{si}")
        p.op("vector", lambda e2: e2.tensor_single_scalar(tvals[:, 2:3], tvals[:, 2:3], 1e-20, ALU.add),
             ins=[tvals], outs=[tvals], tag=f"r.su2{si}")
        p.op("vector", lambda e2: e2.reciprocal(tvals[:, 3:4], tvals[:, 2:3]),
             ins=[tvals], outs=[tvals], tag=f"r.rc{si}")
        p.op("vector", lambda e2: e2.tensor_tensor(wts[:, 0:2], tvals[:, 0:2],
                                                   tvals[:, 3:4].broadcast_to([128, 2]), ALU.mult),
             ins=[tvals], outs=[wts], tag=f"r.w{si}")
        for k2, (src_d, wcol) in enumerate(((A_d, 0), (Bq_d, 1))):
            p.op("vector", lambda e2, wcol=wcol: e2.tensor_scalar_mul(ab_raw[:, :], xn[:, :],
                                                                      wts[:, wcol:wcol + 1]),
                 ins=[xn, wts], outs=[ab_raw], tag=f"r.ab{k2}.{si}")
            qdq_emit(b, ab_raw[:, :], ab_bf[:, :], DM, scrW, ins=[ab_raw],
                     tag=f"r.abq{k2}.{si}", mode="pow2")
            b.dma("scalar", src_d[si * 128:(si + 1) * 128, :], ab_bf[:, :], tag=f"r.abs{k2}.{si}")

    xnT_s = b.sb("xnT_s", [128, NTILE * SUP], BF16)
    wA = [b.sb("wA0", [128, NTILE * 512], BF16)] * 2
    wB = [b.sb("wB0", [128, NTILE * 512], BF16)] * 2
    w2sl = [b.sb("w2sl0", [128, 2 * DM], BF16)] * 2
    at = b.sb("at", [128, DM], BF16)
    bt = b.sb("bt", [128, DM], BF16)
    ct = b.sb("ct", [128, DM], BF16)
    hqT = b.sb("hqT", [128, NTILE * SUP], BF16)
    ceT = b.sb("ceT", [128, NTILE * SUP], BF16)
    hh = b.sb("hh", [128, 512])
    t1 = b.sb("t1", [128, 512])
    hqb = b.sb("hqb", [128, 512], BF16)
    scrH = dict(amax=scrW["amax"], ibits=scrW["ibits"], scl=scrW["scl"], inv=scrW["inv"],
                t=scrW["t"][:, :512], q8=scrW["q8"][:, :512])
    xres = xt
    outt = xn
    gwT_sb = b.sb("gwT_sb", [128, NTILE * E])

    for u in range(NSUP):
        src = ap3(xnT_d, u * SUP, [[S, 128], [128 * S, NTILE], [1, SUP]])
        b.dma("sync", xnT_s[:, :].rearrange("p (b c) -> p b c", c=SUP), src, tag=f"u{u}.xnT")
        p.op("vector", lambda e2: e2.memset(acc[:, :], 0.0), ins=[], outs=[acc], tag=f"u{u}.z")
        for j in range(HID // 512):
            a_b, b_b = wA[j % 2], wB[j % 2]
            b.dma("sync", a_b[:, :].rearrange("p (b c) -> p b c", c=512),
                  ap3(sw1b_d, j * 512, [[HID, 128], [128 * HID, NTILE], [1, 512]]), tag=f"u{u}.s1.{j}")
            b.dma("sync", b_b[:, :].rearrange("p (b c) -> p b c", c=512),
                  ap3(sw3b_d, j * 512, [[HID, 128], [128 * HID, NTILE], [1, 512]]), tag=f"u{u}.s3.{j}")
            for sub in range(NSUB):
                pairs_a = [(xnT_s[:, kc * SUP + sub * 128: kc * SUP + sub * 128 + 128],
                            a_b[:, kc * 512:(kc + 1) * 512]) for kc in range(NTILE)]
                pairs_b = [(xnT_s[:, kc * SUP + sub * 128: kc * SUP + sub * 128 + 128],
                            b_b[:, kc * 512:(kc + 1) * 512]) for kc in range(NTILE)]
                b.mm(pA[:, :], pairs_a, ins=[xnT_s, a_b], tag=f"u{u}.sh{j}.{sub}.a")
                b.mm(pB[:, :], pairs_b, ins=[xnT_s, b_b], tag=f"u{u}.sh{j}.{sub}.b")
                p.op("scalar", lambda e2: e2.activation(t1[:, :], pA[:, :], AF.Silu),
                     ins=[pA], outs=[t1], tag=f"u{u}.sh{j}.{sub}.si")
                p.op("vector", lambda e2: e2.tensor_tensor(hh[:, :], t1[:, :], pB[:, :], ALU.mult),
                     ins=[t1, pB], outs=[hh], tag=f"u{u}.sh{j}.{sub}.h")
                p.op("vector", lambda e2: e2.tensor_copy(hqb[:, :], hh[:, :]),
                     ins=[hh], outs=[hqb], tag=f"u{u}.sh{j}.{sub}.c")
                def emit_tr(e2):
                    last = None
                    for t in range(4):
                        last = e2.transpose(ptr_bf[:, t * 128:(t + 1) * 128],
                                            hqb[:, t * 128:(t + 1) * 128], ident_bf[:, :])
                    return last
                p.op("tensor", emit_tr, ins=[hqb, ident_bf], outs=[ptr_bf], tag=f"u{u}.sh{j}.{sub}.tr")
                dst = ap3(hqT, (4 * j) * SUP + sub * 128, [[NTILE * SUP, 128], [SUP, 4], [1, 128]])
                p.op("vector", lambda e2, dst=dst: e2.tensor_copy(
                    dst, ptr_bf[:, :512].rearrange("p (b c) -> p b c", c=128)),
                    ins=[ptr_bf], outs=[hqT], tag=f"u{u}.sh{j}.{sub}.sc")
        for sub in range(NSUB):
            for kg in range(8):
                wsl = w2sl[kg % 2]
                b.dma("sync", wsl[:, :].rearrange("p (b c) -> p b c", c=DM),
                      ap3(sw2b_d, kg * 2 * 128 * DM, [[DM, 128], [128 * DM, 2], [1, DM]]),
                      tag=f"u{u}.w2s.{sub}.{kg}")
                def emit_mm(e2, kg=kg, sub=sub, wsl=wsl):
                    last = None
                    for h2 in range(2):
                        kc = kg * 2 + h2
                        for n in range(4):
                            last = e2.matmul(pQ[:, n * 512:(n + 1) * 512],
                                             hqT[:, kc * SUP + sub * 128: kc * SUP + sub * 128 + 128],
                                             wsl[:, h2 * DM + n * 512: h2 * DM + (n + 1) * 512],
                                             start=(kc == 0), stop=(kc == 15))
                    return last
                p.op("tensor", emit_mm, ins=[hqT, wsl, pQ], outs=[pQ], tag=f"u{u}.w2m.{sub}.{kg}")
            p.op("vector", lambda e2, sub=sub: e2.tensor_tensor(
                acc[:, sub * DM:(sub + 1) * DM], acc[:, sub * DM:(sub + 1) * DM], pQ[:, :], ALU.add),
                ins=[acc, pQ], outs=[acc], tag=f"u{u}.w2acc{sub}")
        for e in range(E):
            for sub in range(NSUB):
                r0 = u * SUP + sub * 128
                b.dma("sync", at[:, :], A_d[r0:r0 + 128, :], tag=f"u{u}.e{e}.la{sub}")
                b.dma("sync", bt[:, :], Bq_d[r0:r0 + 128, :], tag=f"u{u}.e{e}.lb{sub}")
                si = (r0 // 128)
                eq1 = eq_sb[:, si * 2 * E + e: si * 2 * E + e + 1]
                eq2 = eq_sb[:, si * 2 * E + E + e: si * 2 * E + E + e + 1]
                p.op("vector", lambda e2, eq1=eq1: e2.tensor_scalar_mul(ct[:, :], at[:, :], eq1),
                     ins=[at, eq_sb], outs=[ct], tag=f"u{u}.e{e}.c1.{sub}")
                p.op("vector", lambda e2, eq2=eq2: e2.scalar_tensor_tensor(
                    ct[:, :], bt[:, :], eq2, ct[:, :], ALU.mult, ALU.add),
                    ins=[bt, eq_sb, ct], outs=[ct], tag=f"u{u}.e{e}.c2.{sub}")
                for g in range(2):
                    def emit_tr(e2, g=g):
                        last = None
                        for t in range(8):
                            c = g * 8 + t
                            last = e2.transpose(ptr_bf[:, t * 128:(t + 1) * 128],
                                                ct[:, c * 128:(c + 1) * 128], ident_bf[:, :])
                        return last
                    p.op("tensor", emit_tr, ins=[ct, ident_bf], outs=[ptr_bf], tag=f"u{u}.e{e}.tr{sub}.{g}")
                    dst = ap3(ceT, (g * 8) * SUP + sub * 128, [[NTILE * SUP, 128], [SUP, 8], [1, 128]])
                    p.op("vector", lambda e2, dst=dst: e2.tensor_copy(
                        dst, ptr_bf[:, :1024].rearrange("p (b c) -> p b c", c=128)),
                        ins=[ptr_bf], outs=[ceT], tag=f"u{u}.e{e}.sc{sub}.{g}")
            for j in range(HID // 512):
                a_b, b_b = wA[j % 2], wB[j % 2]
                base = e * DM * 2 * HID
                b.dma("sync", a_b[:, :].rearrange("p (b c) -> p b c", c=512),
                      ap3(w13qT_d, base + j * 512, [[2 * HID, 128], [128 * 2 * HID, NTILE], [1, 512]]),
                      tag=f"u{u}.e{e}.w1.{j}")
                b.dma("sync", b_b[:, :].rearrange("p (b c) -> p b c", c=512),
                      ap3(w13qT_d, base + HID + j * 512, [[2 * HID, 128], [128 * 2 * HID, NTILE], [1, 512]]),
                      tag=f"u{u}.e{e}.w3.{j}")
                for sub in range(NSUB):
                    pairs_a = [(ceT[:, kc * SUP + sub * 128: kc * SUP + sub * 128 + 128],
                                a_b[:, kc * 512:(kc + 1) * 512]) for kc in range(NTILE)]
                    pairs_b = [(ceT[:, kc * SUP + sub * 128: kc * SUP + sub * 128 + 128],
                                b_b[:, kc * 512:(kc + 1) * 512]) for kc in range(NTILE)]
                    b.mm(pA[:, :], pairs_a, ins=[ceT, a_b], tag=f"u{u}.e{e}.j{j}.{sub}.a")
                    b.mm(pB[:, :], pairs_b, ins=[ceT, b_b], tag=f"u{u}.e{e}.j{j}.{sub}.b")
                    p.op("scalar", lambda e2: e2.activation(t1[:, :], pA[:, :], AF.Silu),
                         ins=[pA], outs=[t1], tag=f"u{u}.e{e}.j{j}.{sub}.si")
                    p.op("vector", lambda e2: e2.tensor_tensor(hh[:, :], t1[:, :], pB[:, :], ALU.mult),
                         ins=[t1, pB], outs=[hh], tag=f"u{u}.e{e}.j{j}.{sub}.h")
                    qdq_emit(b, hh[:, :], hqb[:, :], 512, scrH, ins=[hh],
                             tag=f"u{u}.e{e}.j{j}.{sub}.q", mode="pow2")
                    def emit_tr(e2):
                        last = None
                        for t in range(4):
                            last = e2.transpose(ptr_bf[:, t * 128:(t + 1) * 128],
                                                hqb[:, t * 128:(t + 1) * 128], ident_bf[:, :])
                        return last
                    p.op("tensor", emit_tr, ins=[hqb, ident_bf], outs=[ptr_bf], tag=f"u{u}.e{e}.j{j}.{sub}.tr")
                    dst = ap3(hqT, (4 * j) * SUP + sub * 128, [[NTILE * SUP, 128], [SUP, 4], [1, 128]])
                    p.op("vector", lambda e2, dst=dst: e2.tensor_copy(
                        dst, ptr_bf[:, :512].rearrange("p (b c) -> p b c", c=128)),
                        ins=[ptr_bf], outs=[hqT], tag=f"u{u}.e{e}.j{j}.{sub}.sc")
            for sub in range(NSUB):
                for kg in range(8):
                    wsl = w2sl[kg % 2]
                    b.dma("sync", wsl[:, :].rearrange("p (b c) -> p b c", c=DM),
                          ap3(w2qT_d, e * HID * DM + kg * 2 * 128 * DM, [[DM, 128], [128 * DM, 2], [1, DM]]),
                          tag=f"u{u}.e{e}.w2.{sub}.{kg}")
                    def emit_mm(e2, kg=kg, sub=sub, wsl=wsl):
                        last = None
                        for h2 in range(2):
                            kc = kg * 2 + h2
                            for n in range(4):
                                last = e2.matmul(pQ[:, n * 512:(n + 1) * 512],
                                                 hqT[:, kc * SUP + sub * 128: kc * SUP + sub * 128 + 128],
                                                 wsl[:, h2 * DM + n * 512: h2 * DM + (n + 1) * 512],
                                                 start=(kc == 0), stop=(kc == 15))
                        return last
                    p.op("tensor", emit_mm, ins=[hqT, wsl, pQ], outs=[pQ], tag=f"u{u}.e{e}.m{sub}.{kg}")
                p.op("vector", lambda e2, sub=sub: e2.tensor_tensor(
                    acc[:, sub * DM:(sub + 1) * DM], acc[:, sub * DM:(sub + 1) * DM], pQ[:, :], ALU.add),
                    ins=[acc, pQ], outs=[acc], tag=f"u{u}.e{e}.acc{sub}")
        for sub in range(NSUB):
            r0 = u * SUP + sub * 128
            b.dma("sync", xres[:, :], x_in[r0:r0 + 128, :], tag=f"u{u}.lx{sub}")
            p.op("vector", lambda e2, sub=sub: e2.scalar_tensor_tensor(
                outt[:, :], acc[:, sub * DM:(sub + 1) * DM], 0.5, xres[:, :], ALU.mult, ALU.add),
                ins=[acc, xres], outs=[outt], tag=f"u{u}.res{sub}")
            b.dma("scalar", x_out[r0:r0 + 128, :], outt[:, :], tag=f"u{u}.st{sub}")

    p.op("sync", lambda e2: e2.nop(), ins=[x_out], tag="final")

    gload = Op("sync", lambda e2: e2.dma_start(
        gwT_sb[:, :].rearrange("p (b c) -> p b c", c=E),
        ap3(gate_wT, 0, [[E, 128], [128 * E, NTILE], [1, E]])), ["gate_wT"], ["gwT_sb"], True, "gw.ld")
    idx = next(i for i, o in enumerate(p.ops) if o.tag == "c4") + 1
    p.ops.insert(idx, gload)

    p.build()
    return nc, b


def l0ffn_inputs(x_core, gate_w, ebias, w13, w2, sw1, sw2, sw3):
    import ml_dtypes
    E, HID = 8, 2048
    return {
        "x_in": np.ascontiguousarray(x_core, np.float32),
        "gate_wT": np.ascontiguousarray(gate_w.T, np.float32),
        "ebias": np.ascontiguousarray(ebias.reshape(1, E), np.float32),
        "w13": np.ascontiguousarray(w13.reshape(E * 2 * HID, DM), np.float32),
        "w2": np.ascontiguousarray(w2.reshape(E * DM, HID), np.float32),
        "sw1T": np.ascontiguousarray(sw1.T, np.float32),
        "sw3T": np.ascontiguousarray(sw3.T, np.float32),
        "sw2T": np.ascontiguousarray(sw2.T, np.float32),
        "ident_bf": np.eye(128, dtype=ml_dtypes.bfloat16),
        "ident_f32": np.eye(128, dtype=np.float32),
        "ones1": np.ones((1, 128), np.float32),
        "iota8": np.arange(E, dtype=np.float32).reshape(1, E),
    }


def build_l0att():
    H, G, DH, F = 16, 4, 128, 768
    nc = bass.Bass("TRN2", target_bir_lowering=False)
    b = B(nc)
    p = b.p
    x_in = b.dram("x_in", [S, DM], kind="ExternalInput")
    WqT = b.dram("WqT", [DM, 2048], kind="ExternalInput")
    WkT = b.dram("WkT", [DM, 512], kind="ExternalInput")
    WvT = b.dram("WvT", [DM, 512], kind="ExternalInput")
    WgT = b.dram("WgT", [DM, 2048], kind="ExternalInput")
    WmbT = b.dram("WmbT", [DM, 2048], kind="ExternalInput")
    bias_all = b.dram("bias_all", [1, 2048 * 3 + 512 * 2], kind="ExternalInput")
    memnorm = b.dram("memnorm", [H * F, 129], kind="ExternalInput")
    ident_in = b.dram("ident_bf", [128, 128], BF16, kind="ExternalInput")
    ident_fin = b.dram("ident_f32", [128, 128], F32, kind="ExternalInput")
    ones_in = b.dram("ones1", [1, 128], kind="ExternalInput")
    x_out = b.dram("x_out", [S, DM], kind="ExternalOutput")

    xnT_d = b.dram("xnT_d", [DM, S], BF16)
    xn_d = b.dram("xn_d", [S, DM])
    q_d = b.dram("q_d", [S, 2048], BF16)
    k_d = b.dram("k_d", [S, 512], BF16)
    v_d = b.dram("v_d", [S, 512], BF16)
    mb_d = b.dram("mb_d", [S, 2048], BF16)
    g_d = b.dram("g_d", [S, 2048], BF16)
    wq_b = b.dram("wq_b", [DM, 2048], BF16)
    wk_b = b.dram("wk_b", [DM, 512], BF16)
    wv_b = b.dram("wv_b", [DM, 512], BF16)
    wg_b = b.dram("wg_b", [DM, 2048], BF16)
    wmb_b = b.dram("wmb_b", [DM, 2048], BF16)
    cc_in = b.dram("cc_in", [H * F, 129])
    cc_out = b.dram("cc_out", [H * F, 129])

    ident_bf = b.sb("identb", [128, 128], BF16)
    ident_f = b.sb("identf", [128, 128], F32)
    ones_bf = b.sb("ones_bf", [1, 128], BF16)
    ones_f = b.sb("ones_f", [1, 128])
    brow = b.sb("brow", [1, 2048 * 3 + 512 * 2])
    brow_bf = b.sb("brow_bf", [1, 2048 * 3 + 512 * 2], BF16)
    b.dma("sync", ident_bf[:, :], ident_in[:, :], tag="c0")
    b.dma("sync", ident_f[:, :], ident_fin[:, :], tag="c1")
    b.dma("sync", ones_f[:, :], ones_in[:, :], tag="c2")
    b.dma("sync", brow[:, :], bias_all[:, :], tag="c3")
    p.op("vector", lambda e: e.tensor_copy(ones_bf[:1, :], ones_f[:1, :]),
         ins=[ones_f], outs=[ones_bf], tag="c4")
    p.op("vector", lambda e: e.tensor_copy(brow_bf[:1, :], brow[:1, :]),
         ins=[brow], outs=[brow_bf], tag="c5")
    OFF_BQ, OFF_BG, OFF_BMB, OFF_BK, OFF_BV = 0, 2048, 4096, 6144, 6656

    ptr_bf = b.ps("ptr_bf", [128, 1024], BF16)
    pP = b.ps("pP", [128, 512])
    pN = b.ps("pN", [128, 512])
    pD = b.ps("pD", [128, 1024])

    wrow = b.sb("wrow", [128, DM])
    wbf = b.sb("wbf", [128, DM], BF16)
    for (src, dst, O, tg) in [(WqT, wq_b, 2048, "wq"), (WkT, wk_b, 512, "wk"),
                              (WvT, wv_b, 512, "wv"), (WgT, wg_b, 2048, "wg"),
                              (WmbT, wmb_b, 2048, "wmb")]:
        for r in range(DM // 128):
            b.dma("sync", wrow[:, :O], src[r * 128:(r + 1) * 128, :], tag=f"{tg}.l{r}")
            p.op("vector", lambda e: e.tensor_copy(wbf[:, :O], wrow[:, :O]),
                 ins=[wrow], outs=[wbf], tag=f"{tg}.c{r}")
            b.dma("scalar", dst[r * 128:(r + 1) * 128, :], wbf[:, :O], tag=f"{tg}.s{r}")

    xt = b.sb("xt", [128, DM])
    xn = b.sb("xn", [128, DM])
    xnb = b.sb("xnb", [128, DM], BF16)
    sq = b.sb("sq", [128, DM])
    stage = b.sb("stage", [128, 1024], BF16)
    stats = dict(m=b.sb("m", [128, 1]), ssq=b.sb("ssq", [128, 1]),
                 var=b.sb("var", [128, 1]), rstd=b.sb("rstd", [128, 1]),
                 eps=b.sb("epsc", [128, 1]))
    emit_ln_xnT(b, x_in, xnT_d, xt, xn, xnb, sq, stats, ptr_bf, stage, ident_bf, xn_d=xn_d)

    xnTt = b.sb("xnTt", [128, NTILE * 128], BF16)
    wsl = b.sb("wsl", [128, NTILE * 512], BF16)
    pout = b.sb("pout", [128, 512])
    poutb = b.sb("poutb", [128, 512], BF16)
    projs = [
        (wk_b, 512, OFF_BK, k_d, None, "pk"),
        (wv_b, 512, OFF_BV, v_d, None, "pv"),
        (wmb_b, 2048, OFF_BMB, mb_d, AF.Sigmoid, "pmb"),
        (wq_b, 2048, OFF_BQ, q_d, None, "pq"),
        (wg_b, 2048, OFF_BG, g_d, AF.Sigmoid, "pg"),
    ]
    for (w_b, O, boff, out_d, act, tg) in projs:
        for os_ in range(O // 512):
            wsrc = ap3(w_b, os_ * 512, [[O, 128], [128 * O, NTILE], [1, 512]])
            b.dma("sync", wsl[:, :].rearrange("p (b c) -> p b c", c=512), wsrc, tag=f"{tg}.w{os_}")
            for si in range(NTILE):
                xsrc = ap3(xnT_d, si * 128, [[S, 128], [128 * S, NTILE], [1, 128]])
                b.dma("sync", xnTt[:, :].rearrange("p (b c) -> p b c", c=128), xsrc, tag=f"{tg}.x{os_}.{si}")
                pairs = [(xnTt[:, kc * 128:(kc + 1) * 128], wsl[:, kc * 512:(kc + 1) * 512])
                         for kc in range(NTILE)]
                b.mm(pP[:, :], pairs, ins=[xnTt, wsl, ones_bf, brow_bf], tag=f"{tg}.m{os_}.{si}",
                     extra=(ones_bf[:1, :], brow_bf[:1, boff + os_ * 512: boff + (os_ + 1) * 512]))
                if act is None:
                    p.op("vector", lambda e: e.tensor_copy(poutb[:, :], pP[:, :]),
                         ins=[pP], outs=[poutb], tag=f"{tg}.e{os_}.{si}")
                else:
                    p.op("scalar", lambda e, act=act: e.activation(poutb[:, :], pP[:, :], act),
                         ins=[pP], outs=[poutb], tag=f"{tg}.e{os_}.{si}")
                b.dma("scalar", out_d[si * 128:(si + 1) * 128, os_ * 512:(os_ + 1) * 512],
                      poutb[:, :], tag=f"{tg}.s{os_}.{si}")

    cbuf = b.sb("cbuf", [128, 256])
    mkraw = b.sb("mkraw", [128, F])
    dpsq = b.sb("dpsq", [128, F])
    dnorm = b.sb("dnorm", [128, 1])
    drr = b.sb("drr", [128, 1])
    mkt_bf = b.sb("mkt_bf", [128, F], BF16)

    def emit_dpfp(src_ap, ins_src, dst_bf_ap, dst_bufs, mksq_ap, mksq_bufs, tg):
        p.op("scalar", lambda e: e.activation(cbuf[:, :128], src_ap, AF.Relu),
             ins=ins_src, outs=[cbuf], tag=f"{tg}.r0")
        p.op("scalar", lambda e: e.activation(cbuf[:, 128:], src_ap, AF.Relu, scale=-1.0),
             ins=ins_src, outs=[cbuf], tag=f"{tg}.r1")
        for r in range(3):
            j = r + 1
            p.op("vector", lambda e, r=r, j=j: e.tensor_tensor(
                mkraw[:, r * 256 + j: r * 256 + 256], cbuf[:, j:256], cbuf[:, 0:256 - j], ALU.mult),
                ins=[cbuf], outs=[mkraw], tag=f"{tg}.ro{r}a")
            p.op("vector", lambda e, r=r, j=j: e.tensor_tensor(
                mkraw[:, r * 256: r * 256 + j], cbuf[:, 0:j], cbuf[:, 256 - j:256], ALU.mult),
                ins=[cbuf], outs=[mkraw], tag=f"{tg}.ro{r}b")
        p.op("scalar", lambda e: e.activation(dpsq[:, :], mkraw[:, :], AF.Square, accum_out=dnorm[:, :]),
             ins=[mkraw], outs=[dpsq, dnorm], tag=f"{tg}.ss")
        p.op("scalar", lambda e: e.activation(dnorm[:, :], dnorm[:, :], AF.Sqrt),
             ins=[dnorm], outs=[dnorm], tag=f"{tg}.sq")
        p.op("vector", lambda e: e.tensor_single_scalar(dnorm[:, :], dnorm[:, :], 1e-12, ALU.max),
             ins=[dnorm], outs=[dnorm], tag=f"{tg}.mx")
        p.op("vector", lambda e: e.reciprocal(drr[:, :], dnorm[:, :]),
             ins=[dnorm], outs=[drr], tag=f"{tg}.rc")
        p.op("vector", lambda e: e.tensor_scalar_mul(mkraw[:, :], mkraw[:, :], drr[:, :]),
             ins=[mkraw, drr], outs=[mkraw], tag=f"{tg}.nm")
        if mksq_ap is not None:
            p.op("scalar", lambda e: e.activation(dpsq[:, :], mkraw[:, :], AF.Square,
                                                  accum_out=mksq_ap),
                 ins=[mkraw], outs=[dpsq] + mksq_bufs, tag=f"{tg}.s2")
        p.op("vector", lambda e: e.tensor_copy(dst_bf_ap, mkraw[:, :]),
             ins=[mkraw], outs=dst_bufs, tag=f"{tg}.bf")

    kg = b.sb("kg", [128, NTILE * 128], BF16)
    vg = b.sb("vg", [128, NTILE * 128], BF16)
    mk_sb = b.sb("mk_sb", [128, NTILE * F], BF16)
    mkT_sb = b.sb("mkT_sb", [128, 12384], BF16)
    mksq_sb = b.sb("mksq_sb", [128, NTILE])
    om_sb = b.sb("om_sb", [128, 6 * 129], BF16)
    omf = b.sb("omf", [128, 6 * 129])
    mbh = b.sb("mbh", [128, NTILE * 128], BF16)
    num_sb = b.sb("num_sb", [128, 129])
    rr = b.sb("rr", [128, 1])
    pv_sb = b.sb("pv_sb", [128, 128])
    wmv = b.sb("wmv", [128, 128])
    coef = b.sb("coef", [128, 1])
    rhs129 = b.sb("rhs129", [128, 129], BF16)
    dd_sb = b.sb("dd_sb", [128, 6 * 129])

    for g in range(G):
        ksrc = ap3(k_d, g * 128, [[512, 128], [128 * 512, NTILE], [1, 128]])
        b.dma("sync", kg[:, :].rearrange("p (b c) -> p b c", c=128), ksrc, tag=f"kg{g}")
        vsrc = ap3(v_d, g * 128, [[512, 128], [128 * 512, NTILE], [1, 128]])
        b.dma("sync", vg[:, :].rearrange("p (b c) -> p b c", c=128), vsrc, tag=f"vg{g}")
        for si in range(NTILE):
            emit_dpfp(kg[:, si * 128:(si + 1) * 128], [kg],
                      mk_sb[:, si * F:(si + 1) * F], [mk_sb],
                      mksq_sb[:, si:si + 1], [mksq_sb], f"dk{g}.{si}")
            def emit_tr(e):
                last = None
                for t in range(6):
                    last = e.transpose(ptr_bf[:, (t % 4) * 128:(t % 4) * 128 + 128],
                                       mk_sb[:, si * F + t * 128: si * F + (t + 1) * 128],
                                       ident_bf[:, :]) if False else None
                return last
            for gg in range(2):
                def emit_tr2(e, gg=gg, si=si):
                    last = None
                    for t in range(3):
                        c = gg * 3 + t
                        last = e.transpose(ptr_bf[:, t * 128:(t + 1) * 128],
                                           mk_sb[:, si * F + c * 128: si * F + (c + 1) * 128],
                                           ident_bf[:, :])
                    return last
                p.op("tensor", emit_tr2, ins=[mk_sb, ident_bf], outs=[ptr_bf], tag=f"ktr{g}.{si}.{gg}")
                dst = ap3(mkT_sb, (gg * 3) * S + si * 128, [[12384, 128], [S, 3], [1, 128]])
                p.op("vector", lambda e, dst=dst: e.tensor_copy(
                    dst, ptr_bf[:, :384].rearrange("p (b c) -> p b c", c=128)),
                    ins=[ptr_bf], outs=[mkT_sb], tag=f"kev{g}.{si}.{gg}")
        for hq in range(4 * g, 4 * g + 4):
            b.dma("sync", omf[:, :].rearrange("p (b c) -> p b c", c=129),
                  ap3(memnorm, hq * F * 129, [[129, 128], [128 * 129, 6], [1, 129]]), tag=f"om{hq}")
            p.op("vector", lambda e: e.tensor_copy(om_sb[:, :], omf[:, :]),
                 ins=[omf], outs=[om_sb], tag=f"omb{hq}")
            b.dma("sync", mbh[:, :].rearrange("p (b c) -> p b c", c=128),
                  ap3(mb_d, hq * 128, [[2048, 128], [128 * 2048, NTILE], [1, 128]]), tag=f"mb{hq}")
            for si in range(NTILE):
                pairs = [(mkT_sb[:, fc * S + si * 128: fc * S + (si + 1) * 128],
                          om_sb[:, fc * 129:(fc + 1) * 129]) for fc in range(6)]
                b.mm(pN[:, :129], pairs, ins=[mkT_sb, om_sb], tag=f"num{hq}.{si}")
                p.op("vector", lambda e: e.tensor_copy(num_sb[:, :], pN[:, :129]),
                     ins=[pN], outs=[num_sb], tag=f"nev{hq}.{si}")
                p.op("vector", lambda e: e.tensor_single_scalar(num_sb[:, 128:129], num_sb[:, 128:129],
                                                                1e-8, ALU.add),
                     ins=[num_sb], outs=[num_sb], tag=f"den{hq}.{si}")
                p.op("vector", lambda e: e.reciprocal(rr[:, :], num_sb[:, 128:129]),
                     ins=[num_sb], outs=[rr], tag=f"rc{hq}.{si}")
                p.op("vector", lambda e: e.tensor_scalar_mul(pv_sb[:, :], num_sb[:, :128], rr[:, :]),
                     ins=[num_sb, rr], outs=[pv_sb], tag=f"pv{hq}.{si}")
                p.op("vector", lambda e, si=si: e.tensor_tensor(
                    wmv[:, :], vg[:, si * 128:(si + 1) * 128], pv_sb[:, :], ALU.subtract),
                    ins=[vg, pv_sb], outs=[wmv], tag=f"mv{hq}.{si}")
                p.op("vector", lambda e, si=si: e.tensor_tensor(
                    wmv[:, :], wmv[:, :], mbh[:, si * 128:(si + 1) * 128], ALU.mult),
                    ins=[wmv, mbh], outs=[wmv], tag=f"wm{hq}.{si}")
                p.op("vector", lambda e, si=si: e.reciprocal(coef[:, :], mksq_sb[:, si:si + 1]),
                     ins=[mksq_sb], outs=[coef], tag=f"cf0{hq}.{si}")
                p.op("vector", lambda e: e.tensor_tensor(coef[:, :], coef[:, :], num_sb[:, 128:129],
                                                         ALU.mult),
                     ins=[coef, num_sb], outs=[coef], tag=f"cf1{hq}.{si}")
                p.op("vector", lambda e: e.tensor_scalar(coef[:, :], coef[:, :], -1.0, 1.0,
                                                         ALU.mult, ALU.add),
                     ins=[coef], outs=[coef], tag=f"cf2{hq}.{si}")
                p.op("vector", lambda e: e.tensor_scalar(coef[:, :], coef[:, :], 1.0, 0.0,
                                                         ALU.min, ALU.max),
                     ins=[coef], outs=[coef], tag=f"cf3{hq}.{si}")
                p.op("vector", lambda e: e.tensor_copy(rhs129[:, :128], wmv[:, :]),
                     ins=[wmv], outs=[rhs129], tag=f"rh0{hq}.{si}")
                p.op("vector", lambda e: e.tensor_copy(rhs129[:, 128:129], coef[:, :]),
                     ins=[coef], outs=[rhs129], tag=f"rh1{hq}.{si}")
                def emit_dmm(e, si=si):
                    last = None
                    for fc in range(6):
                        last = e.matmul(pD[:, fc * 129:(fc + 1) * 129],
                                        mk_sb[:, si * F + fc * 128: si * F + (fc + 1) * 128],
                                        rhs129[:, :], start=(si == 0), stop=(si == NTILE - 1))
                    return last
                p.op("tensor", emit_dmm, ins=[mk_sb, rhs129, pD], outs=[pD], tag=f"dmm{hq}.{si}")
            p.op("vector", lambda e: e.tensor_copy(dd_sb[:, :774], pD[:, :774]),
                 ins=[pD], outs=[dd_sb], tag=f"dev{hq}")
            b.dma("scalar", ap3(cc_in, hq * F * 129, [[129, 128], [128 * 129, 6], [1, 129]]),
                  dd_sb[:, :774].rearrange("p (b c) -> p b c", c=129), tag=f"dst{hq}")

    p.op("gpsimd", lambda e: e.collective_compute(
        "AllReduce", ALU.add, replica_groups=[list(range(8))],
        ins=[cc_in[:, :]], outs=[cc_out[:, :]]), ins=[cc_in], outs=[cc_out], tag="ar")

    nm_sb = mkT_sb[:, :H * 6 * 129]
    for hq in range(H):
        b.dma("sync", omf[:, :].rearrange("p (b c) -> p b c", c=129),
              ap3(memnorm, hq * F * 129, [[129, 128], [128 * 129, 6], [1, 129]]), tag=f"nm0{hq}")
        b.dma("sync", dd_sb[:, :774].rearrange("p (b c) -> p b c", c=129),
              ap3(cc_out, hq * F * 129, [[129, 128], [128 * 129, 6], [1, 129]]), tag=f"nm1{hq}")
        p.op("vector", lambda e, hq=hq: e.tensor_tensor(
            nm_sb[:, hq * 774:(hq + 1) * 774], omf[:, :774], dd_sb[:, :774], ALU.add),
            ins=[omf, dd_sb], outs=[nm_sb], tag=f"nm2{hq}")

    qt = b.sb("qt", [128, 2048], BF16)
    gt = b.sb("gt", [128, 2048], BF16)
    mh_sb = b.sb("mh_sb", [128, 2048])
    mqT = b.sb("mqT", [128, 6 * 128], BF16)
    for si in range(NTILE):
        b.dma("sync", qt[:, :], q_d[si * 128:(si + 1) * 128, :], tag=f"q.l{si}")
        b.dma("sync", gt[:, :], g_d[si * 128:(si + 1) * 128, :], tag=f"q.g{si}")
        for hq in range(H):
            emit_dpfp(qt[:, hq * 128:(hq + 1) * 128], [qt],
                      mkt_bf[:, :], [mkt_bf], None, None, f"dq{si}.{hq}")
            for gg in range(2):
                def emit_tr2(e, gg=gg):
                    last = None
                    for t in range(3):
                        c = gg * 3 + t
                        last = e.transpose(ptr_bf[:, t * 128:(t + 1) * 128],
                                           mkt_bf[:, c * 128:(c + 1) * 128], ident_bf[:, :])
                    return last
                p.op("tensor", emit_tr2, ins=[mkt_bf, ident_bf], outs=[ptr_bf], tag=f"qtr{si}.{hq}.{gg}")
                dst = ap3(mqT, (gg * 3) * 128, [[6 * 128, 128], [128, 3], [1, 128]])
                p.op("vector", lambda e, dst=dst: e.tensor_copy(
                    dst, ptr_bf[:, :384].rearrange("p (b c) -> p b c", c=128)),
                    ins=[ptr_bf], outs=[mqT], tag=f"qev{si}.{hq}.{gg}")
            pairs = [(mqT[:, fc * 128:(fc + 1) * 128],
                      nm_sb[:, hq * 774 + fc * 129: hq * 774 + (fc + 1) * 129]) for fc in range(6)]
            b.mm(pN[:, :129], pairs, ins=[mqT, nm_sb], tag=f"qnum{si}.{hq}")
            p.op("vector", lambda e: e.tensor_copy(num_sb[:, :], pN[:, :129]),
                 ins=[pN], outs=[num_sb], tag=f"qnev{si}.{hq}")
            p.op("vector", lambda e: e.tensor_single_scalar(num_sb[:, 128:129], num_sb[:, 128:129],
                                                            1e-8, ALU.add),
                 ins=[num_sb], outs=[num_sb], tag=f"qden{si}.{hq}")
            p.op("vector", lambda e: e.reciprocal(rr[:, :], num_sb[:, 128:129]),
                 ins=[num_sb], outs=[rr], tag=f"qrc{si}.{hq}")
            p.op("vector", lambda e, hq=hq: e.tensor_scalar_mul(
                mh_sb[:, hq * 128:(hq + 1) * 128], num_sb[:, :128], rr[:, :]),
                ins=[num_sb, rr], outs=[mh_sb], tag=f"qmh{si}.{hq}")
        b.dma("sync", xn[:, :], xn_d[si * 128:(si + 1) * 128, :], tag=f"q.xn{si}")
        b.dma("sync", xt[:, :], x_in[si * 128:(si + 1) * 128, :], tag=f"q.x{si}")
        p.op("vector", lambda e: e.tensor_tensor(mh_sb[:, :], mh_sb[:, :], gt[:, :], ALU.mult),
             ins=[mh_sb, gt], outs=[mh_sb], tag=f"q.gm{si}")
        p.op("vector", lambda e: e.tensor_tensor(xn[:, :], xn[:, :], mh_sb[:, :], ALU.add),
             ins=[xn, mh_sb], outs=[xn], tag=f"q.a1{si}")
        p.op("vector", lambda e: e.tensor_tensor(xn[:, :], xn[:, :], xt[:, :], ALU.add),
             ins=[xn, xt], outs=[xn], tag=f"q.a2{si}")
        b.dma("scalar", x_out[si * 128:(si + 1) * 128, :], xn[:, :], tag=f"q.st{si}")

    p.op("sync", lambda e: e.nop(), ins=[x_out], tag="final")
    p.build()
    return nc, b


def l0att_inputs(x_core, Wq, bq, Wk, bk, Wv, bv, Wg, bg, Wmb, bmb, memory, norm):
    import ml_dtypes
    bias_all = np.concatenate([bq, bg, bmb, bk, bv]).reshape(1, -1)
    memnorm = np.concatenate([memory, norm[:, :, None]], axis=2).reshape(16 * 768, 129)
    return {
        "x_in": np.ascontiguousarray(x_core, np.float32),
        "WqT": np.ascontiguousarray(Wq.T, np.float32),
        "WkT": np.ascontiguousarray(Wk.T, np.float32),
        "WvT": np.ascontiguousarray(Wv.T, np.float32),
        "WgT": np.ascontiguousarray(Wg.T, np.float32),
        "WmbT": np.ascontiguousarray(Wmb.T, np.float32),
        "bias_all": np.ascontiguousarray(bias_all, np.float32),
        "memnorm": np.ascontiguousarray(memnorm, np.float32),
        "ident_bf": np.eye(128, dtype=ml_dtypes.bfloat16),
        "ident_f32": np.eye(128, dtype=np.float32),
        "ones1": np.ones((1, 128), np.float32),
    }


def build_l1att():
    H, G, DH = 16, 4, 128
    WIN, GLB = 256, 64
    SCALE = 1.0 / np.sqrt(128.0)
    closest = 2.0 ** np.floor(np.log2(H))
    base = 2.0 ** (-(2.0 ** (-(np.log2(closest) - 3.0))))
    slopes = [float(base ** k) for k in range(1, H + 1)]

    nc = bass.Bass("TRN2", target_bir_lowering=False)
    b = B(nc)
    p = b.p
    x_in = b.dram("x_in", [S, DM], kind="ExternalInput")
    Wq = b.dram("Wq", [DM, 2048], kind="ExternalInput")
    Wkv = b.dram("Wkv", [DM, 1024], kind="ExternalInput")
    Wo = b.dram("Wo", [DM, 2048], kind="ExternalInput")
    bo = b.dram("bo", [1, 2048], kind="ExternalInput")
    ident_in = b.dram("ident_bf", [128, 128], BF16, kind="ExternalInput")
    ones_in = b.dram("ones1", [1, 128], kind="ExternalInput")
    x_out = b.dram("x_out", [S, DM], kind="ExternalOutput")

    xnT_d = b.dram("xnT_d", [DM, S], BF16)
    q_d = b.dram("q_d", [S, 2048], BF16)
    ao_d = b.dram("ao_d", [S, 2048], BF16)
    aoT_d = b.dram("aoT_d", [DM, S], BF16)
    wq_b = b.dram("wq_b", [DM, 2048], BF16)
    wkv_b = b.dram("wkv_b", [DM, 1024], BF16)
    wo_b = b.dram("wo_b", [DM, 2048], BF16)

    ident_bf = b.sb("identb", [128, 128], BF16)
    ones_f = b.sb("ones_f", [1, 128])
    ones_bf = b.sb("ones_bf", [1, 128], BF16)
    borow = b.sb("borow", [1, 2048])
    borow_bf = b.sb("borow_bf", [1, 2048], BF16)
    b.dma("sync", ident_bf[:, :], ident_in[:, :], tag="c0")
    b.dma("sync", ones_f[:, :], ones_in[:, :], tag="c1")
    b.dma("sync", borow[:, :], bo[:, :], tag="c2")
    p.op("vector", lambda e: e.tensor_copy(ones_bf[:1, :], ones_f[:1, :]),
         ins=[ones_f], outs=[ones_bf], tag="c3")
    p.op("vector", lambda e: e.tensor_copy(borow_bf[:1, :], borow[:1, :]),
         ins=[borow], outs=[borow_bf], tag="c4")

    ptr_bf = b.ps("ptr_bf", [128, 1024], BF16)
    pS = b.ps("pS", [128, 2048])
    pP = b.ps("pP", [128, 512])
    pAV = b.ps("pAV", [128, 512])

    wrow = b.sb("wrow", [128, DM])
    wbf = b.sb("wbf", [128, DM], BF16)
    for (src, dst, O, tg) in [(Wq, wq_b, 2048, "wq"), (Wkv, wkv_b, 1024, "wkv"), (Wo, wo_b, 2048, "wo")]:
        for r in range(DM // 128):
            b.dma("sync", wrow[:, :O], src[r * 128:(r + 1) * 128, :], tag=f"{tg}.l{r}")
            p.op("vector", lambda e: e.tensor_copy(wbf[:, :O], wrow[:, :O]),
                 ins=[wrow], outs=[wbf], tag=f"{tg}.c{r}")
            b.dma("scalar", dst[r * 128:(r + 1) * 128, :], wbf[:, :O], tag=f"{tg}.s{r}")

    xt = b.sb("xt", [128, DM])
    xn = b.sb("xn", [128, DM])
    xnb = b.sb("xnb", [128, DM], BF16)
    sq = b.sb("sq", [128, DM])
    stage = b.sb("stage", [128, 1024], BF16)
    stats = dict(m=b.sb("m", [128, 1]), ssq=b.sb("ssq", [128, 1]),
                 var=b.sb("var", [128, 1]), rstd=b.sb("rstd", [128, 1]),
                 eps=b.sb("epsc", [128, 1]))
    cw_sb = b.sb("cw_sb", [128, NTILE])

    def cw_cb(si):
        p.op("vector", lambda e: e.reduce_sum(stats["m"][:, :], xn[:, :], AX.X,
                                              apply_absolute_value=True),
             ins=[xn], outs=[stats["m"]], tag=f"cw.s{si}")
        p.op("scalar", lambda e, si=si: e.activation(cw_sb[:, si:si + 1], stats["m"][:, :], AF.Sigmoid),
             ins=[stats["m"]], outs=[cw_sb], tag=f"cw.g{si}")

    emit_ln_xnT(b, x_in, xnT_d, xt, xn, xnb, sq, stats, ptr_bf, stage, ident_bf,
                extra_tile_cb=cw_cb)

    xnTt = b.sb("xnTt", [128, NTILE * 128], BF16)
    wsl = b.sb("wsl", [128, NTILE * 512], BF16)
    poutb = b.sb("poutb", [128, 512], BF16)
    kT_sb = b.sb("kT_sb", [128, G * S], BF16)
    v_sb = b.sb("v_sb", [128, NTILE * 512], BF16)
    for os_ in range(4):
        b.dma("sync", wsl[:, :].rearrange("p (b c) -> p b c", c=512),
              ap3(wq_b, os_ * 512, [[2048, 128], [128 * 2048, NTILE], [1, 512]]), tag=f"q.w{os_}")
        for si in range(NTILE):
            b.dma("sync", xnTt[:, :].rearrange("p (b c) -> p b c", c=128),
                  ap3(xnT_d, si * 128, [[S, 128], [128 * S, NTILE], [1, 128]]), tag=f"q.x{os_}.{si}")
            pairs = [(xnTt[:, kc * 128:(kc + 1) * 128], wsl[:, kc * 512:(kc + 1) * 512])
                     for kc in range(NTILE)]
            b.mm(pP[:, :], pairs, ins=[xnTt, wsl], tag=f"q.m{os_}.{si}")
            p.op("vector", lambda e, si=si: e.tensor_scalar(poutb[:, :], pP[:, :],
                                                            cw_sb[:, si:si + 1], SCALE,
                                                            ALU.mult, ALU.mult),
                 ins=[pP, cw_sb], outs=[poutb], tag=f"q.e{os_}.{si}")
            b.dma("scalar", q_d[si * 128:(si + 1) * 128, os_ * 512:(os_ + 1) * 512],
                  poutb[:, :], tag=f"q.s{os_}.{si}")
    for os_ in range(2):
        b.dma("sync", wsl[:, :].rearrange("p (b c) -> p b c", c=512),
              ap3(wkv_b, os_ * 512, [[1024, 128], [128 * 1024, NTILE], [1, 512]]), tag=f"kv.w{os_}")
        for si in range(NTILE):
            b.dma("sync", xnTt[:, :].rearrange("p (b c) -> p b c", c=128),
                  ap3(xnT_d, si * 128, [[S, 128], [128 * S, NTILE], [1, 128]]), tag=f"kv.x{os_}.{si}")
            pairs = [(xnTt[:, kc * 128:(kc + 1) * 128], wsl[:, kc * 512:(kc + 1) * 512])
                     for kc in range(NTILE)]
            b.mm(pP[:, :], pairs, ins=[xnTt, wsl], tag=f"kv.m{os_}.{si}")
            p.op("vector", lambda e: e.tensor_copy(poutb[:, :], pP[:, :]),
                 ins=[pP], outs=[poutb], tag=f"kv.e{os_}.{si}")
            if os_ == 1:
                p.op("vector", lambda e, si=si: e.tensor_copy(v_sb[:, si * 512:(si + 1) * 512],
                                                              poutb[:, :]),
                     ins=[poutb], outs=[v_sb], tag=f"kv.v{si}")
            else:
                for g in range(G):
                    def emit_tr(e, g=g):
                        return e.transpose(ptr_bf[:, g * 128:(g + 1) * 128],
                                           poutb[:, g * 128:(g + 1) * 128], ident_bf[:, :])
                    p.op("tensor", emit_tr, ins=[poutb, ident_bf], outs=[ptr_bf], tag=f"kv.t{si}.{g}")
                dst = ap3(kT_sb, si * 128, [[G * S, 128], [S, 4], [1, 128]])
                p.op("vector", lambda e, dst=dst: e.tensor_copy(
                    dst, ptr_bf[:, :512].rearrange("p (b c) -> p b c", c=128)),
                    ins=[ptr_bf], outs=[kT_sb], tag=f"kv.kt{si}")

    jrow = b.sb("jrow", [128, S])
    ppcol = b.sb("ppcol", [128, 1])
    globj = b.sb("globj", [128, S], I32)
    neg9 = b.sb("neg9", [128, S])
    ji = b.sb("ji", [128, S], I32)
    p.op("gpsimd", lambda e: e.iota(ji[:, :], [[1, S]], channel_multiplier=0),
         ins=[], outs=[ji], tag="iotaj")
    p.op("vector", lambda e: e.tensor_copy(jrow[:, :], ji[:, :]), ins=[ji], outs=[jrow], tag="jrow")
    p.op("gpsimd", lambda e: e.iota(ji[:, :1], [[0, 1]], channel_multiplier=1),
         ins=[], outs=[ji], tag="iotap")
    p.op("vector", lambda e: e.tensor_copy(ppcol[:, :], ji[:, :1]), ins=[ji], outs=[ppcol], tag="pcol")
    p.op("vector", lambda e: e.tensor_single_scalar(globj[:, :], jrow[:, :], 64.0, ALU.is_lt),
         ins=[jrow], outs=[globj], tag="globj")
    p.op("vector", lambda e: e.memset(neg9[:, :], -1e9), ins=[], outs=[neg9], tag="neg9")

    qt = b.sb("qt", [128, 2048], BF16)
    dd = b.sb("dd", [128, S])
    am = b.sb("am", [128, S], I32)
    u_sb = b.sb("u_sb", [128, S])
    eu = b.sb("eu", [128, S], BF16)
    pb_sb = b.sb("pb_sb", [128, S], BF16)
    pT = b.sb("pT", [128, NTILE * 128], BF16)
    ao_sb = b.sb("ao_sb", [128, 2048], BF16)
    mx = b.sb("mxr", [128, 1])
    sm = b.sb("smr", [128, 1])
    rr = b.sb("rrr", [128, 1])
    qTt = b.sb("qTt", [128, 128], BF16)
    for si in range(NTILE):
        b.dma("sync", qt[:, :], q_d[si * 128:(si + 1) * 128, :], tag=f"a.q{si}")
        p.op("vector", lambda e, si=si: e.tensor_scalar(dd[:, :], jrow[:, :], ppcol[:, :],
                                                        float(si * 128), ALU.subtract,
                                                        ALU.subtract),
             ins=[jrow, ppcol], outs=[dd], tag=f"a.d0{si}")
        p.op("scalar", lambda e: e.activation(dd[:, :], dd[:, :], AF.Abs),
             ins=[dd], outs=[dd], tag=f"a.d1{si}")
        p.op("vector", lambda e: e.tensor_single_scalar(am[:, :], dd[:, :], float(WIN), ALU.is_le),
             ins=[dd], outs=[am], tag=f"a.m0{si}")
        p.op("vector", lambda e: e.tensor_tensor(am[:, :], am[:, :], globj[:, :], ALU.max),
             ins=[am, globj], outs=[am], tag=f"a.m1{si}")
        if si == 0:
            p.op("vector", lambda e: e.memset(am[:64, :], 1), ins=[], outs=[am], tag="a.m2")
        for h in range(H):
            g = h // 4
            def emit_qT(e, h=h):
                return e.transpose(ptr_bf[:, :128], qt[:, h * 128:(h + 1) * 128], ident_bf[:, :])
            p.op("tensor", emit_qT, ins=[qt, ident_bf], outs=[ptr_bf], tag=f"a.qT{si}.{h}")
            p.op("vector", lambda e: e.tensor_copy(qTt[:, :], ptr_bf[:, :128]),
                 ins=[ptr_bf], outs=[qTt], tag=f"a.qTe{si}.{h}")
            def emit_sc(e, g=g):
                last = None
                for n in range(4):
                    last = e.matmul(pS[:, n * 512:(n + 1) * 512], qTt[:, :],
                                    kT_sb[:, g * S + n * 512: g * S + (n + 1) * 512],
                                    start=True, stop=True)
                return last
            p.op("tensor", emit_sc, ins=[qTt, kT_sb], outs=[pS], tag=f"a.sc{si}.{h}")
            p.op("vector", lambda e, h=h: e.scalar_tensor_tensor(
                u_sb[:, :], dd[:, :], -slopes[h], pS[:, :], ALU.mult, ALU.add),
                ins=[dd, pS], outs=[u_sb], tag=f"a.u{si}.{h}")
            p.op("vector", lambda e: e.copy_predicated(u_sb[:, :], am[:, :], neg9[:, :]),
                 ins=[am, neg9], outs=[u_sb], tag=f"a.cp{si}.{h}")
            p.op("vector", lambda e: e.reduce_max(mx[:, :], u_sb[:, :], AX.X, negate=True),
                 ins=[u_sb], outs=[mx], tag=f"a.mx{si}.{h}")
            p.op("scalar", lambda e: e.activation(eu[:, :], u_sb[:, :], AF.Exp, bias=mx[:, :],
                                                  accum_out=sm[:, :]),
                 ins=[u_sb, mx], outs=[eu, sm], tag=f"a.ex{si}.{h}")
            p.op("vector", lambda e: e.reciprocal(rr[:, :], sm[:, :]),
                 ins=[sm], outs=[rr], tag=f"a.rc{si}.{h}")
            p.op("vector", lambda e: e.tensor_scalar_mul(pb_sb[:, :], eu[:, :], rr[:, :]),
                 ins=[eu, rr], outs=[pb_sb], tag=f"a.pb{si}.{h}")
            for gg in range(2):
                def emit_tr(e, gg=gg):
                    last = None
                    for t in range(8):
                        c = gg * 8 + t
                        last = e.transpose(ptr_bf[:, t * 128:(t + 1) * 128],
                                           pb_sb[:, c * 128:(c + 1) * 128], ident_bf[:, :])
                    return last
                p.op("tensor", emit_tr, ins=[pb_sb, ident_bf], outs=[ptr_bf], tag=f"a.pt{si}.{h}.{gg}")
                dst = ap3(pT, (gg * 8) * 128, [[NTILE * 128, 128], [128, 8], [1, 128]])
                p.op("vector", lambda e, dst=dst: e.tensor_copy(
                    dst, ptr_bf[:, :1024].rearrange("p (b c) -> p b c", c=128)),
                    ins=[ptr_bf], outs=[pT], tag=f"a.pte{si}.{h}.{gg}")
            def emit_av(e, g=g):
                last = None
                for jc in range(NTILE):
                    last = e.matmul(pAV[:, :128], pT[:, jc * 128:(jc + 1) * 128],
                                    v_sb[:, jc * 512 + g * 128: jc * 512 + (g + 1) * 128],
                                    start=(jc == 0), stop=(jc == NTILE - 1))
                return last
            p.op("tensor", emit_av, ins=[pT, v_sb, pAV], outs=[pAV], tag=f"a.av{si}.{h}")
            p.op("vector", lambda e, h=h: e.tensor_copy(ao_sb[:, h * 128:(h + 1) * 128], pAV[:, :128]),
                 ins=[pAV], outs=[ao_sb], tag=f"a.ae{si}.{h}")
        b.dma("scalar", ao_d[si * 128:(si + 1) * 128, :], ao_sb[:, :], tag=f"a.st{si}")
        for gg in range(2):
            def emit_tr(e, gg=gg):
                last = None
                for t in range(8):
                    c = gg * 8 + t
                    last = e.transpose(ptr_bf[:, t * 128:(t + 1) * 128],
                                       ao_sb[:, c * 128:(c + 1) * 128], ident_bf[:, :])
                return last
            p.op("tensor", emit_tr, ins=[ao_sb, ident_bf], outs=[ptr_bf], tag=f"a.ot{si}.{gg}")
            p.op("vector", lambda e: e.tensor_copy(stage[:, :1024], ptr_bf[:, :1024]),
                 ins=[ptr_bf], outs=[stage], tag=f"a.oe{si}.{gg}")
            dst = ap3(aoT_d, (gg * 8) * 128 * S + si * 128, [[S, 128], [128 * S, 8], [1, 128]])
            b.dma("scalar", dst, stage[:, :1024].rearrange("p (b c) -> p b c", c=128),
                  tag=f"a.os{si}.{gg}")

    outf = b.sb("outf", [128, 512])
    for os_ in range(4):
        b.dma("sync", wsl[:, :].rearrange("p (b c) -> p b c", c=512),
              ap3(wo_b, os_ * 512, [[2048, 128], [128 * 2048, NTILE], [1, 512]]), tag=f"o.w{os_}")
        for si in range(NTILE):
            b.dma("sync", xnTt[:, :].rearrange("p (b c) -> p b c", c=128),
                  ap3(aoT_d, si * 128, [[S, 128], [128 * S, NTILE], [1, 128]]), tag=f"o.x{os_}.{si}")
            pairs = [(xnTt[:, kc * 128:(kc + 1) * 128], wsl[:, kc * 512:(kc + 1) * 512])
                     for kc in range(NTILE)]
            b.mm(pP[:, :], pairs, ins=[xnTt, wsl, ones_bf, borow_bf], tag=f"o.m{os_}.{si}",
                 extra=(ones_bf[:1, :], borow_bf[:1, os_ * 512:(os_ + 1) * 512]))
            b.dma("sync", xt[:, :512], x_in[si * 128:(si + 1) * 128, os_ * 512:(os_ + 1) * 512],
                  tag=f"o.r{os_}.{si}")
            p.op("vector", lambda e: e.tensor_tensor(outf[:, :], pP[:, :], xt[:, :512], ALU.add),
                 ins=[pP, xt], outs=[outf], tag=f"o.e{os_}.{si}")
            b.dma("scalar", x_out[si * 128:(si + 1) * 128, os_ * 512:(os_ + 1) * 512],
                  outf[:, :], tag=f"o.s{os_}.{si}")

    p.op("sync", lambda e: e.nop(), ins=[x_out], tag="final")
    p.build()
    return nc, b


def l1att_inputs(x_core, Wq, Wkv, Wo, bo):
    import ml_dtypes
    return {
        "x_in": np.ascontiguousarray(x_core, np.float32),
        "Wq": np.ascontiguousarray(Wq, np.float32),
        "Wkv": np.ascontiguousarray(Wkv, np.float32),
        "Wo": np.ascontiguousarray(Wo, np.float32),
        "bo": np.ascontiguousarray(bo.reshape(1, -1), np.float32),
        "ident_bf": np.eye(128, dtype=ml_dtypes.bfloat16),
        "ones1": np.ones((1, 128), np.float32),
    }


_PROGS = {}


def _get_prog(name):
    if name not in _PROGS:
        _PROGS[name] = {"l0ffn": build_l0ffn, "l0att": build_l0att,
                        "l1ffn": build_l1ffn, "l1att": build_l1att}[name]()
    return _PROGS[name]


def kernel(L0_attn_hidden_states, L0_attn_Wq, L0_attn_bq, L0_attn_Wk, L0_attn_bk,
           L0_attn_Wv, L0_attn_bv, L0_attn_Wg, L0_attn_bg, L0_attn_Wmb, L0_attn_bmb,
           L0_attn_memory, L0_attn_norm, L0_ffn_gate_w, L0_ffn_w13, L0_ffn_w2,
           L0_ffn_sw1, L0_ffn_sw2, L0_ffn_sw3, L0_ffn_expert_bias,
           L1_attn_Wq, L1_attn_Wkv, L1_attn_Wo, L1_attn_bo,
           L1_ffn_gate_w, L1_ffn_up_w, L1_ffn_down_w):
    from concourse.bass_utils import run_bass_kernel_spmd

    x = np.ascontiguousarray(np.asarray(L0_attn_hidden_states, np.float32))
    n = 8

    def launch(name, mk_inputs):
        nc, _ = _get_prog(name)
        in_maps = [mk_inputs(c) for c in range(n)]
        res = run_bass_kernel_spmd(nc, in_maps, core_ids=list(range(n)))
        return np.stack([res.results[c]["x_out"] for c in range(n)], axis=0)

    x = launch("l0ffn", lambda c: l0ffn_inputs(x[c], L0_ffn_gate_w, L0_ffn_expert_bias,
                                               L0_ffn_w13, L0_ffn_w2, L0_ffn_sw1,
                                               L0_ffn_sw2, L0_ffn_sw3))
    x = launch("l0att", lambda c: l0att_inputs(x[c], L0_attn_Wq, L0_attn_bq, L0_attn_Wk,
                                               L0_attn_bk, L0_attn_Wv, L0_attn_bv,
                                               L0_attn_Wg, L0_attn_bg, L0_attn_Wmb,
                                               L0_attn_bmb, L0_attn_memory, L0_attn_norm))
    x = launch("l0ffn", lambda c: l0ffn_inputs(x[c], L0_ffn_gate_w, L0_ffn_expert_bias,
                                               L0_ffn_w13, L0_ffn_w2, L0_ffn_sw1,
                                               L0_ffn_sw2, L0_ffn_sw3))
    x = launch("l1ffn", lambda c: l1ffn_inputs(x[c], L1_ffn_gate_w, L1_ffn_up_w, L1_ffn_down_w))
    x = launch("l1att", lambda c: l1att_inputs(x[c], L1_attn_Wq, L1_attn_Wkv, L1_attn_Wo, L1_attn_bo))
    x = launch("l1ffn", lambda c: l1ffn_inputs(x[c], L1_ffn_gate_w, L1_ffn_up_w, L1_ffn_down_w))
    return x.astype(np.float32)


# version 03
# speedup vs baseline: 1.3478x; 1.3478x over previous
import numpy as np
from contextlib import ExitStack

import concourse.bass as bass
import concourse.mybir as mybir

F32 = mybir.dt.float32
BF16 = mybir.dt.bfloat16
I32 = mybir.dt.int32
U32 = mybir.dt.uint32
FP8 = mybir.dt.float8e4
AF = mybir.ActivationFunctionType
ALU = mybir.AluOpType
AX = mybir.AxisListType

ENGINES = ("sync", "scalar", "vector", "tensor", "gpsimd")
S = 2048
DM = 2048
NTILE = S // 128


class Op:
    __slots__ = ("eng", "emit", "ins", "outs", "dma", "tag", "waits", "inc", "tick",
                 "is_barrier")

    def __init__(self, eng, emit, ins, outs, dma, tag):
        self.eng, self.emit, self.ins, self.outs = eng, emit, list(ins), list(outs)
        self.dma, self.tag = dma, tag
        self.inc = 16 if dma else 1
        self.waits = None
        self.tick = None
        self.is_barrier = False


def _key(x):
    if isinstance(x, str):
        return x
    if hasattr(x, "tensor"):
        return x.tensor.name
    if hasattr(x, "name"):
        return x.name
    raise TypeError(f"bad buffer key {x!r}")


class Prog:

    def __init__(self, nc, dma_slots=6):
        self.nc = nc
        self.ops = []
        self.dma_slots = dma_slots
        self.segments = []

    def op(self, eng, emit, ins=(), outs=(), dma=False, tag=""):
        assert eng in ENGINES, eng
        self.ops.append(Op(eng, emit, [_key(i) for i in ins], [_key(o) for o in outs], dma, tag))

    def barrier(self):
        o = Op("sync", lambda e: e.nop(), [], [], False, "barrier")
        o.is_barrier = True
        self.ops.append(o)
        self.segments.append(len(self.ops))

    def plan(self):
        state = {}
        counts = {}
        memo = {e: {} for e in ENGINES}
        dma_idx = {}
        for op in self.ops:
            if op.dma:
                i = dma_idx.get(op.eng, 0)
                dma_idx[op.eng] = i + 1
                sk = f"dma_{op.eng}{i % self.dma_slots}"
            else:
                sk = "tick_" + op.eng
            needs = {}
            for b in op.ins:
                st = state.get(b)
                if st and st["w"]:
                    wsk, wv = st["w"]
                    needs[wsk] = max(needs.get(wsk, 0), wv)
            for b in op.outs:
                st = state.get(b)
                if st:
                    if st["w"]:
                        wsk, wv = st["w"]
                        needs[wsk] = max(needs.get(wsk, 0), wv)
                    for rsk, rv in st["r"].items():
                        needs[rsk] = max(needs.get(rsk, 0), rv)
            if op.dma:
                prev = counts.get(sk, 0)
                if prev:
                    needs[sk] = max(needs.get(sk, 0), prev)
            if getattr(op, "is_barrier", False):
                for k2, v2 in counts.items():
                    if k2.startswith("dma_"):
                        needs[k2] = max(needs.get(k2, 0), v2)
            if op.eng == "tensor" and not op.dma:
                needs.pop("tick_tensor", None)
            waits = []
            m = memo[op.eng]
            for wsk, wv in sorted(needs.items()):
                if m.get(wsk, 0) >= wv:
                    continue
                waits.append((wsk, wv))
                m[wsk] = wv
            op.waits = waits
            counts[sk] = counts.get(sk, 0) + op.inc
            op.tick = counts[sk]
            for b in op.ins:
                st = state.setdefault(b, {"w": None, "r": {}})
                st["r"][sk] = op.tick
            for b in op.outs:
                state[b] = {"w": (sk, op.tick), "r": {}}
        self.final_counts = counts

    def build(self):
        self.plan()
        nc = self.nc
        import contextlib
        self._es = contextlib.ExitStack()
        sems = {k: self._es.enter_context(nc.semaphore(k)) for k in sorted(self.final_counts)}
        self.sems = sems
        dma_slots = self.dma_slots
        bounds = list(self.segments)
        if not bounds or bounds[-1] != len(self.ops):
            bounds.append(len(self.ops))
        start = 0
        dma_i = {e: 0 for e in ENGINES}
        for end in bounds:
            seg = self.ops[start:end]
            start = end
            if not seg:
                continue

            def section(eng_name, eng_obj, seg=seg):
                for op in seg:
                    if op.eng != eng_name:
                        continue
                    if op.dma:
                        sk = f"dma_{eng_name}{dma_i[eng_name] % dma_slots}"
                        dma_i[eng_name] += 1
                    else:
                        sk = "tick_" + eng_name
                    for wsk, wv in op.waits:
                        eng_obj.wait_ge(sems[wsk], wv)
                    ins = op.emit(eng_obj)
                    if ins is None:
                        raise RuntimeError(f"op {op.tag} returned None")
                    ins.then_inc(sems[sk], op.inc)

            engs = sorted({o.eng for o in seg})
            with nc.Block() as block:
                if "sync" in engs:
                    @block.sync
                    def _(e):
                        section("sync", e)
                if "scalar" in engs:
                    @block.scalar
                    def _(e):
                        section("scalar", e)
                if "vector" in engs:
                    @block.vector
                    def _(e):
                        section("vector", e)
                if "tensor" in engs:
                    @block.tensor
                    def _(e):
                        section("tensor", e)
                if "gpsimd" in engs:
                    @block.gpsimd
                    def _(e):
                        section("gpsimd", e)


def ap3(t, offset, dims):
    return bass.AP(t, offset, dims)


class B:

    def __init__(self, nc):
        self.nc = nc
        self.p = Prog(nc)
        self.es = ExitStack()
        self._scopes = [self.es]
        self.pfx = ""
        self._drams = {}
        self._uid = 0

    def uid(self):
        self._uid += 1
        return self._uid

    def push_scope(self):
        self._scopes.append(ExitStack())

    def end_phase(self):
        self.p.barrier()
        self._scopes.pop().close()

    def sb(self, name, shape, dt=F32):
        return self._scopes[-1].enter_context(self.nc.sbuf_tensor(self.pfx + name, shape, dt))

    def ps(self, name, shape, dt=F32):
        return self._scopes[-1].enter_context(self.nc.psum_tensor(self.pfx + name, shape, dt))

    def dram(self, name, shape, dt=F32, kind=None, shared=False):
        nm = name if (shared or kind) else self.pfx + name
        if nm in self._drams:
            return self._drams[nm]
        t = self.nc.dram_tensor(nm, shape, dt, kind=kind) if kind else self.nc.dram_tensor(nm, shape, dt)
        self._drams[nm] = t
        return t

    def dma(self, q, dst, src, tag=""):
        self.p.op(q, lambda e, d=dst, s=src: e.dma_start(d, s), ins=[src], outs=[dst], dma=True, tag=tag)

    def mm(self, psum_ap, pairs, ins, tag="", extra=None):
        def emit(e):
            last = None
            n = len(pairs)
            for i, (l, r) in enumerate(pairs):
                last = e.matmul(psum_ap, l, r, start=(i == 0),
                                stop=(i == n - 1 and extra is None))
            if extra is not None:
                last = e.matmul(psum_ap, extra[0], extra[1], start=(n == 0), stop=True)
            return last
        self.p.op("tensor", emit, ins=ins, outs=[psum_ap], tag=tag)

    def transpose_blocks(self, src_tile, cols, psum_bf, dst_writer, ins, tag=""):
        raise NotImplementedError


def qdq_emit(b, src_ap, dst_ap, w, scr, ins, tag, mode="pow2"):
    p = b.p
    nb = w // 32
    amax, ibits, scl, inv, t, q8 = scr["amax"], scr["ibits"], scr["scl"], scr["inv"], scr["t"], scr["q8"]
    src3 = src_ap.rearrange("p (b t) -> p b t", t=32)
    p.op("vector", lambda e: e.tensor_reduce(amax[:, :nb, None], src3, AX.X, ALU.max,
                                             apply_absolute_value=True),
         ins=ins, outs=[amax], tag=tag + ".amax")
    if mode == "pow2":
        p.op("vector", lambda e: e.tensor_single_scalar(ibits[:, :nb], amax[:, :nb].bitcast(I32),
                                                        23, ALU.logical_shift_right),
             ins=[amax], outs=[ibits], tag=tag + ".e")
        p.op("vector", lambda e: e.tensor_single_scalar(ibits[:, :nb], ibits[:, :nb], 9, ALU.max),
             ins=[ibits], outs=[ibits], tag=tag + ".clampe")
        p.op("vector", lambda e: e.tensor_single_scalar(scl[:, :nb].bitcast(I32), ibits[:, :nb],
                                                        -8, ALU.add),
             ins=[ibits], outs=[scl], tag=tag + ".scl0")
        p.op("vector", lambda e: e.tensor_single_scalar(scl[:, :nb].bitcast(I32), scl[:, :nb].bitcast(I32),
                                                        23, ALU.logical_shift_left),
             ins=[scl], outs=[scl], tag=tag + ".scl")
        p.op("vector", lambda e: e.tensor_scalar(inv[:, :nb].bitcast(I32), ibits[:, :nb], -1, 262,
                                                 ALU.mult, ALU.add),
             ins=[ibits], outs=[inv], tag=tag + ".inv0")
        p.op("vector", lambda e: e.tensor_single_scalar(inv[:, :nb].bitcast(I32), inv[:, :nb].bitcast(I32),
                                                        23, ALU.logical_shift_left),
             ins=[inv], outs=[inv], tag=tag + ".inv")
    else:
        p.op("vector", lambda e: e.tensor_scalar(scl[:, :nb], amax[:, :nb], 1.0 / 448.0, 1e-30,
                                                 ALU.mult, ALU.max),
             ins=[amax], outs=[scl], tag=tag + ".scl")
        p.op("vector", lambda e: e.reciprocal(inv[:, :nb], scl[:, :nb]),
             ins=[scl], outs=[inv], tag=tag + ".inv")
    t3 = t[:, :w].rearrange("p (b t) -> p b t", t=32)
    p.op("vector", lambda e: e.tensor_tensor(t3, src3, inv[:, :nb, None].broadcast_to([128, nb, 32]),
                                             ALU.mult),
         ins=ins + [inv], outs=[t], tag=tag + ".mul")
    if mode == "pow2":
        p.op("vector", lambda e: e.tensor_scalar(t[:, :w], t[:, :w], 448.0, -448.0, ALU.min, ALU.max),
             ins=[t], outs=[t], tag=tag + ".clip")
    p.op("scalar", lambda e: e.activation(q8[:, :w], t[:, :w], AF.Copy, scale=0.5),
         ins=[t], outs=[q8], tag=tag + ".q8")
    p.op("vector", lambda e: e.tensor_scalar_mul(scl[:, :nb], scl[:, :nb], 2.0),
         ins=[scl], outs=[scl], tag=tag + ".s2")
    q83 = q8[:, :w].rearrange("p (b t) -> p b t", t=32)
    d3 = dst_ap.rearrange("p (b t) -> p b t", t=32)
    p.op("vector", lambda e: e.tensor_tensor(d3, q83, scl[:, :nb, None].broadcast_to([128, nb, 32]),
                                             ALU.mult),
         ins=[q8, scl], outs=[dst_ap], tag=tag + ".dq")


def mk_qdq_scratch(b, w, name):
    nb = w // 32
    return dict(
        amax=b.sb(f"{name}_amax", [128, nb]),
        ibits=b.sb(f"{name}_ib", [128, nb], I32),
        scl=b.sb(f"{name}_scl", [128, nb]),
        inv=b.sb(f"{name}_inv", [128, nb]),
        t=b.sb(f"{name}_t", [128, w]),
        q8=b.sb(f"{name}_q8", [128, w], FP8),
    )


def emit_weight_qdqT(b, w_d, qT_d, O, K, scr, wrow, wbf, ptr_bf, stage, ident_bf, tag, mode="pow2"):
    p = b.p
    KS = 2048 if K >= 2048 else K
    for r in range(O // 128):
        for ks in range(K // KS):
            b.dma("sync", wrow[:, :KS], w_d[r * 128:(r + 1) * 128, ks * KS:(ks + 1) * KS],
                  tag=f"{tag}.ld{r}.{ks}")
            qdq_emit(b, wrow[:, :KS], wbf[:, :KS], KS, scr, ins=[wrow], tag=f"{tag}.q{r}.{ks}", mode=mode)
            for g in range(KS // 1024):
                def emit_tr(e, g=g):
                    last = None
                    for t in range(8):
                        c = g * 8 + t
                        last = e.transpose(ptr_bf[:, t * 128:(t + 1) * 128],
                                           wbf[:, c * 128:(c + 1) * 128], ident_bf[:, :])
                    return last
                p.op("tensor", emit_tr, ins=[wbf, ident_bf], outs=[ptr_bf], tag=f"{tag}.tr{r}.{ks}.{g}")
                p.op("vector", lambda e: e.tensor_copy(stage[:, :1024], ptr_bf[:, :1024]),
                     ins=[ptr_bf], outs=[stage], tag=f"{tag}.ev{r}.{ks}.{g}")
                c0 = ks * (KS // 128) + g * 8
                dst = ap3(qT_d, c0 * 128 * O + r * 128, [[O, 128], [128 * O, 8], [1, 128]])
                b.dma("scalar", dst, stage[:, :1024].rearrange("p (b c) -> p b c", c=128),
                      tag=f"{tag}.st{r}.{ks}.{g}")


def emit_ln_xnT(b, x_in, xnT_d, xt, xn, xnb, sq, stats, ptr_bf, stage, ident_bf,
                xn_d=None, tag="ln", extra_tile_cb=None, qdq_scr=None, qdq_mode="div448"):
    p = b.p
    m, ssq, var, rstd = stats["m"], stats["ssq"], stats["var"], stats["rstd"]
    eps = stats["eps"]
    p.op("vector", lambda e: e.memset(eps[:, :], 1e-5), ins=[], outs=[eps], tag=tag + ".eps")
    for si in range(NTILE):
        b.dma("sync", xt[:, :], x_in[si * 128:(si + 1) * 128, :], tag=f"{tag}.ld{si}")
        p.op("vector", lambda e: e.reduce_sum(m[:, :], xt[:, :], AX.X),
             ins=[xt], outs=[m], tag=f"{tag}.sum{si}")
        p.op("scalar", lambda e: e.activation(sq[:, :], xt[:, :], AF.Square, accum_out=ssq[:, :]),
             ins=[xt], outs=[sq, ssq], tag=f"{tag}.sq{si}")
        p.op("vector", lambda e: e.tensor_scalar_mul(m[:, :], m[:, :], 1.0 / DM),
             ins=[m], outs=[m], tag=f"{tag}.m{si}")
        p.op("vector", lambda e: e.tensor_tensor(var[:, :], m[:, :], m[:, :], ALU.mult),
             ins=[m], outs=[var], tag=f"{tag}.m2{si}")
        p.op("vector", lambda e: e.scalar_tensor_tensor(var[:, :], ssq[:, :], 1.0 / DM, var[:, :],
                                                        ALU.mult, ALU.subtract),
             ins=[ssq, var], outs=[var], tag=f"{tag}.var{si}")
        p.op("scalar", lambda e: e.activation(rstd[:, :], var[:, :], AF.Sqrt, bias=eps[:, :]),
             ins=[var, eps], outs=[rstd], tag=f"{tag}.sd{si}")
        p.op("vector", lambda e: e.reciprocal(rstd[:, :], rstd[:, :]),
             ins=[rstd], outs=[rstd], tag=f"{tag}.rs{si}")
        p.op("vector", lambda e: e.tensor_scalar(xn[:, :], xt[:, :], m[:, :], rstd[:, :],
                                                 ALU.subtract, ALU.mult),
             ins=[xt, m, rstd], outs=[xn], tag=f"{tag}.xn{si}")
        if xn_d is not None:
            b.dma("scalar", xn_d[si * 128:(si + 1) * 128, :], xn[:, :], tag=f"{tag}.stxn{si}")
        if qdq_scr is not None:
            qdq_emit(b, xn[:, :], xnb[:, :], DM, qdq_scr, ins=[xn], tag=f"{tag}.qdq{si}",
                     mode=qdq_mode)
        else:
            p.op("vector", lambda e: e.tensor_copy(xnb[:, :], xn[:, :]),
                 ins=[xn], outs=[xnb], tag=f"{tag}.bf{si}")
        for g in range(2):
            def emit_tr(e, g=g):
                last = None
                for t in range(8):
                    c = g * 8 + t
                    last = e.transpose(ptr_bf[:, t * 128:(t + 1) * 128],
                                       xnb[:, c * 128:(c + 1) * 128], ident_bf[:, :])
                return last
            p.op("tensor", emit_tr, ins=[xnb, ident_bf], outs=[ptr_bf], tag=f"{tag}.tr{si}.{g}")
            p.op("vector", lambda e: e.tensor_copy(stage[:, :1024], ptr_bf[:, :1024]),
                 ins=[ptr_bf], outs=[stage], tag=f"{tag}.ev{si}.{g}")
            dst = ap3(xnT_d, (g * 8) * 128 * S + si * 128, [[S, 128], [128 * S, 8], [1, 128]])
            b.dma("scalar", dst, stage[:, :1024].rearrange("p (b c) -> p b c", c=128),
                  tag=f"{tag}.st{si}.{g}")
        if extra_tile_cb is not None:
            extra_tile_cb(si)


def emit_l1ffn(b, x_in, x_out, P, prep=True):
    I = 8192
    SUP, NSUB = 256, 2
    NSUP = S // SUP
    b.push_scope()
    gate_w, up_w, down_w = P["gate_w1"], P["up_w1"], P["down_w1"]
    ident_in = P["ident_bf"]

    xnT_d = b.dram("xnT_d", [DM, S], BF16)
    gq_d = b.dram("gq_d", [DM, I], BF16, shared=True)
    uq_d = b.dram("uq_d", [DM, I], BF16, shared=True)
    dq_d = b.dram("dq_d", [I, DM], BF16, shared=True)

    p = b.p
    ident_bf = b.sb("identb", [128, 128], BF16)
    b.dma("sync", ident_bf[:, :], ident_in[:, :], tag="ident")

    wrow = b.sb("wrow", [128, DM])
    wbf = b.sb("wbf", [128, DM], BF16)
    ptr_bf = b.ps("ptr_bf", [128, 1024], BF16)
    stage = b.sb("stage", [128, 1024], BF16)
    scrW = mk_qdq_scratch(b, DM, "scrW")
    if prep:
        emit_weight_qdqT(b, gate_w, gq_d, I, DM, scrW, wrow, wbf, ptr_bf, stage, ident_bf, "gw", mode="div448")
        emit_weight_qdqT(b, up_w, uq_d, I, DM, scrW, wrow, wbf, ptr_bf, stage, ident_bf, "uw", mode="div448")
        emit_weight_qdqT(b, down_w, dq_d, DM, I, scrW, wrow, wbf, ptr_bf, stage, ident_bf, "dw", mode="div448")

    xt = b.sb("xt", [128, DM])
    xn = b.sb("xn", [128, DM])
    xnb = b.sb("xnb", [128, DM], BF16)
    sq = b.sb("sq", [128, DM])
    stats = dict(m=b.sb("m", [128, 1]), ssq=b.sb("ssq", [128, 1]),
                 var=b.sb("var", [128, 1]), rstd=b.sb("rstd", [128, 1]),
                 eps=b.sb("epsc", [128, 1]))
    emit_ln_xnT(b, x_in, xnT_d, xt, xn, xnb, sq, stats, ptr_bf, stage, ident_bf,
                qdq_scr=scrW, qdq_mode="div448")

    xnT_s = b.sb("xnT_s", [128, NTILE * SUP], BF16)
    gsl = [b.sb(f"gsl{i}", [128, NTILE * 512], BF16) for i in range(2)]
    usl = [b.sb(f"usl{i}", [128, NTILE * 512], BF16) for i in range(2)]
    hh = b.sb("hh", [128, 512])
    t1 = b.sb("t1", [128, 512])
    hqb = b.sb("hqb", [128, 512], BF16)
    hqT = b.sb("hqT", [128, (I // 128) * SUP], BF16)
    dsl = [b.sb(f"dsl{i}", [128, DM], BF16) for i in range(2)]
    acc = b.sb("acc", [128, NSUB * DM])
    scrH = mk_qdq_scratch(b, 512, "scrH")
    pA = b.ps("pA", [128, 512])
    pB = b.ps("pB", [128, 512])
    pQ = b.ps("pQ", [128, DM])
    xres = b.sb("xres", [128, DM])
    outt = b.sb("outt", [128, DM])

    for u in range(NSUP):
        src = ap3(xnT_d, u * SUP, [[S, 128], [128 * S, NTILE], [1, SUP]])
        b.dma("sync", xnT_s[:, :].rearrange("p (b c) -> p b c", c=SUP), src, tag=f"u{u}.xnT")
        for j in range(I // 512):
            g_b, u_b = gsl[j % 2], usl[j % 2]
            gsrc = ap3(gq_d, j * 512, [[I, 128], [128 * I, NTILE], [1, 512]])
            usrc = ap3(uq_d, j * 512, [[I, 128], [128 * I, NTILE], [1, 512]])
            b.dma("sync", g_b[:, :].rearrange("p (b c) -> p b c", c=512), gsrc, tag=f"u{u}.g{j}")
            b.dma("sync", u_b[:, :].rearrange("p (b c) -> p b c", c=512), usrc, tag=f"u{u}.u{j}")
            for sub in range(NSUB):
                pairs_g = [(xnT_s[:, kc * SUP + sub * 128: kc * SUP + sub * 128 + 128],
                            g_b[:, kc * 512:(kc + 1) * 512]) for kc in range(NTILE)]
                pairs_u = [(xnT_s[:, kc * SUP + sub * 128: kc * SUP + sub * 128 + 128],
                            u_b[:, kc * 512:(kc + 1) * 512]) for kc in range(NTILE)]
                b.mm(pA[:, :], pairs_g, ins=[xnT_s, g_b], tag=f"u{u}.j{j}.s{sub}.g")
                b.mm(pB[:, :], pairs_u, ins=[xnT_s, u_b], tag=f"u{u}.j{j}.s{sub}.u")
                p.op("scalar", lambda e: e.activation(t1[:, :], pA[:, :], AF.Silu),
                     ins=[pA], outs=[t1], tag=f"u{u}.j{j}.s{sub}.si")
                p.op("vector", lambda e: e.tensor_tensor(hh[:, :], t1[:, :], pB[:, :], ALU.mult),
                     ins=[t1, pB], outs=[hh], tag=f"u{u}.j{j}.s{sub}.h")
                qdq_emit(b, hh[:, :], hqb[:, :], 512, scrH, ins=[hh],
                         tag=f"u{u}.j{j}.s{sub}.q", mode="div448")
                def emit_tr(e):
                    last = None
                    for t in range(4):
                        last = e.transpose(ptr_bf[:, t * 128:(t + 1) * 128],
                                           hqb[:, t * 128:(t + 1) * 128], ident_bf[:, :])
                    return last
                p.op("tensor", emit_tr, ins=[hqb, ident_bf], outs=[ptr_bf], tag=f"u{u}.j{j}.s{sub}.tr")
                dst = ap3(hqT, (4 * j) * SUP + sub * 128, [[(I // 128) * SUP, 128], [SUP, 4], [1, 128]])
                p.op("vector", lambda e, dst=dst: e.tensor_copy(
                    dst, ptr_bf[:, :512].rearrange("p (b c) -> p b c", c=128)),
                    ins=[ptr_bf], outs=[hqT], tag=f"u{u}.j{j}.s{sub}.sc")
        for sub in range(NSUB):
            for kc in range(I // 128):
                d_b = dsl[kc % 2]
                b.dma("sync", d_b[:, :], dq_d[kc * 128:(kc + 1) * 128, :], tag=f"u{u}.s{sub}.d{kc}")
                def emit_mm(e, kc=kc, sub=sub, d_b=d_b):
                    last = None
                    for n in range(4):
                        last = e.matmul(pQ[:, n * 512:(n + 1) * 512],
                                        hqT[:, kc * SUP + sub * 128: kc * SUP + sub * 128 + 128],
                                        d_b[:, n * 512:(n + 1) * 512],
                                        start=(kc == 0), stop=(kc == I // 128 - 1))
                    return last
                p.op("tensor", emit_mm, ins=[hqT, d_b, pQ], outs=[pQ], tag=f"u{u}.s{sub}.mm{kc}")
            p.op("vector", lambda e, sub=sub: e.tensor_copy(acc[:, sub * DM:(sub + 1) * DM], pQ[:, :]),
                 ins=[pQ], outs=[acc], tag=f"u{u}.s{sub}.acc")
        for sub in range(NSUB):
            r0 = u * SUP + sub * 128
            b.dma("sync", xres[:, :], x_in[r0:r0 + 128, :], tag=f"u{u}.s{sub}.lx")
            p.op("vector", lambda e, sub=sub: e.scalar_tensor_tensor(
                outt[:, :], acc[:, sub * DM:(sub + 1) * DM], 0.5, xres[:, :], ALU.mult, ALU.add),
                ins=[acc, xres], outs=[outt], tag=f"u{u}.s{sub}.res")
            b.dma("scalar", x_out[r0:r0 + 128, :], outt[:, :], tag=f"u{u}.s{sub}.st")


def l1ffn_inputs(x_full_core, gate_w, up_w, down_w):
    import ml_dtypes
    return {
        "x_in": np.ascontiguousarray(x_full_core, np.float32),
        "gate_w": np.ascontiguousarray(gate_w, np.float32),
        "up_w": np.ascontiguousarray(up_w, np.float32),
        "down_w": np.ascontiguousarray(down_w, np.float32),
        "ident_bf": np.eye(128, dtype=ml_dtypes.bfloat16),
    }


def emit_l0ffn(b, x_in, x_out, P, prep=True):
    E, HID = 8, 2048
    SUP_P = 256
    NUP = (2 * S + E * (SUP_P - 1)) // SUP_P + 1
    NUP = 6144 // SUP_P
    ROWS = NUP * SUP_P
    b.push_scope()
    gate_wT, ebias, w13, w2 = P["gwT0"], P["eb0"], P["w13"], P["w2"]
    sw1T, sw3T, sw2T = P["sw1T"], P["sw3T"], P["sw2T"]
    ident_in, identf_in = P["ident_bf"], P["ident_f32"]
    ones_in, iota_in, tri_in = P["ones1"], P["iota8"], P["tri0_bf"]

    xnT_d = b.dram("xnT_d", [DM, S], BF16)
    A_d = b.dram("A_d", [S, DM], BF16)
    Bq_d = b.dram("B_d", [S, DM], BF16)
    Xp_d = b.dram("Xp_d", [ROWS, DM], BF16)
    Op_d = b.dram("Op_d", [ROWS, DM])
    sh_d = b.dram("sh_d", [S, DM])
    w13qT_d = b.dram("w13qT_d", [E * DM, 2 * HID], BF16, shared=True)
    w2qT_d = b.dram("w2qT_d", [E * HID, DM], BF16, shared=True)
    sw1b_d = b.dram("sw1b_d", [DM, HID], BF16, shared=True)
    sw3b_d = b.dram("sw3b_d", [DM, HID], BF16, shared=True)
    sw2b_d = b.dram("sw2b_d", [HID, DM], BF16, shared=True)

    p = b.p
    ident_bf = b.sb("identb", [128, 128], BF16)
    ident_f = b.sb("identf", [128, 128], F32)
    ones1 = b.sb("ones1s", [1, 128])
    iota8 = b.sb("iota8s", [1, E])
    ebias_s = b.sb("ebias_s", [1, E])
    tri0 = b.sb("tri0", [128, 128], BF16)
    b.dma("sync", ident_bf[:, :], ident_in[:, :], tag="c0")
    b.dma("sync", ident_f[:, :], identf_in[:, :], tag="c1")
    b.dma("sync", ones1[:, :], ones_in[:, :], tag="c2")
    b.dma("sync", iota8[:, :], iota_in[:, :], tag="c3")
    b.dma("sync", tri0[:, :], tri_in[:, :], tag="c3b")
    b.dma("sync", ebias_s[:, :], ebias[:, :], tag="c4")
    gwT_sb = b.sb("gwT_sb", [128, NTILE * E])
    b.dma("sync", gwT_sb[:, :].rearrange("p (b c) -> p b c", c=E),
          ap3(gate_wT, 0, [[E, 128], [128 * E, NTILE], [1, E]]), tag="gw.ld")

    ptr_bf = b.ps("ptr_bf", [128, 1024], BF16)
    pA = b.ps("pA", [128, 512])
    pB = b.ps("pB", [128, 512])
    pQ = b.ps("pQ", [128, DM])
    stage = b.sb("stage", [128, 1024], BF16)

    bias_b = b.sb("bias_b", [128, E])
    iota_b = b.sb("iota_b", [128, E])
    b.mm(pA[:, :E], [(ones1[:1, :], ebias_s[:1, :])], ins=[ones1, ebias_s], tag="bb")
    p.op("vector", lambda e: e.tensor_copy(bias_b[:, :], pA[:, :E]), ins=[pA], outs=[bias_b], tag="bbe")
    b.mm(pB[:, :E], [(ones1[:1, :], iota8[:1, :])], ins=[ones1, iota8], tag="ib")
    p.op("vector", lambda e: e.tensor_copy(iota_b[:, :], pB[:, :E]), ins=[pB], outs=[iota_b], tag="ibe")

    acc = b.sb("acc", [128, 4 * DM])
    wrow = acc[:, :DM]
    wbf = b.sb("wbf", [128, DM], BF16)
    scrW = mk_qdq_scratch(b, DM, "scrW")

    def wprep(w_src, q_dst, row0, O, K, tag, colmap=None):
        for r in range(O // 128):
            b.dma("sync", wrow[:, :K], w_src[row0 + r * 128: row0 + (r + 1) * 128, :],
                  tag=f"{tag}.ld{r}")
            qdq_emit(b, wrow[:, :K], wbf[:, :K], K, scrW, ins=[wrow],
                     tag=f"{tag}.q{r}", mode="pow2")
            col0 = colmap(r) if colmap else r * 128
            for g in range(K // 1024):
                def emit_tr(e2, g=g):
                    last = None
                    for t in range(8):
                        c = g * 8 + t
                        last = e2.transpose(ptr_bf[:, t * 128:(t + 1) * 128],
                                            wbf[:, c * 128:(c + 1) * 128], ident_bf[:, :])
                    return last
                p.op("tensor", emit_tr, ins=[wbf, ident_bf], outs=[ptr_bf], tag=f"{tag}.tr{r}.{g}")
                p.op("vector", lambda e2: e2.tensor_copy(stage[:, :1024], ptr_bf[:, :1024]),
                     ins=[ptr_bf], outs=[stage], tag=f"{tag}.ev{r}.{g}")
                c0 = g * 8
                dst = ap3(q_dst[0], q_dst[1] + c0 * 128 * q_dst[2] + col0,
                          [[q_dst[2], 128], [128 * q_dst[2], 8], [1, 128]])
                b.dma("scalar", dst, stage[:, :1024].rearrange("p (b c) -> p b c", c=128),
                      tag=f"{tag}.st{r}.{g}")

    def w13cols(r):
        if r < 16:
            return (r // 4) * 1024 + (r % 4) * 128
        rr = r - 16
        return (rr // 4) * 1024 + 512 + (rr % 4) * 128

    if prep:
        for e in range(E):
            wprep(w13, (w13qT_d, e * DM * 2 * HID, 2 * HID), e * 2 * HID, 2 * HID, DM,
                  f"w13.{e}", colmap=w13cols)
            wprep(w2, (w2qT_d, e * HID * DM, DM), e * DM, DM, HID, f"w2.{e}")
        for (src, dst, R, C, tg) in [(sw1T, sw1b_d, DM, HID, "s1"), (sw3T, sw3b_d, DM, HID, "s3"),
                                     (sw2T, sw2b_d, HID, DM, "s2")]:
            for r in range(R // 128):
                b.dma("sync", wrow[:, :C], src[r * 128:(r + 1) * 128, :], tag=f"{tg}.ld{r}")
                p.op("vector", lambda e2: e2.tensor_copy(wbf[:, :C], wrow[:, :C]),
                     ins=[wrow], outs=[wbf], tag=f"{tg}.c{r}")
                b.dma("scalar", dst[r * 128:(r + 1) * 128, :], wbf[:, :C], tag=f"{tg}.st{r}")

    xt = b.sb("xt", [128, DM])
    xn = b.sb("xn", [128, DM])
    ab_raw = b.sb("ab_raw", [128, DM])
    sq = ab_raw
    stats = dict(m=b.sb("m", [128, 1]), ssq=b.sb("ssq", [128, 1]),
                 var=b.sb("var", [128, 1]), rstd=b.sb("rstd", [128, 1]),
                 eps=b.sb("epsc", [128, 1]))
    xnTf = b.sb("xnTf", [128, NTILE * 128])
    xnTb = b.sb("xnTb", [128, NTILE * 128], BF16)
    sc = b.sb("sc", [128, E])
    scb = b.sb("scb", [128, E])
    mx8 = b.sb("mx8", [128, E])
    mi8 = b.sb("mi8", [128, E], U32)
    mif = b.sb("mif", [128, E])
    eqt = b.sb("eqt", [128, E])
    tvals = b.sb("tvals", [128, 4])
    wts = b.sb("wts", [128, 2])
    eq_sb = b.sb("eq_sb", [128, NTILE * 2 * E])
    ab_bf = b.sb("ab_bf", [128, DM], BF16)
    m_, ssq_, var_, rstd_, eps_ = stats["m"], stats["ssq"], stats["var"], stats["rstd"], stats["eps"]
    p.op("vector", lambda e: e.memset(eps_[:, :], 1e-5), ins=[], outs=[eps_], tag="eps")
    for si in range(NTILE):
        b.dma("sync", xt[:, :], x_in[si * 128:(si + 1) * 128, :], tag=f"r.ld{si}")
        p.op("vector", lambda e: e.reduce_sum(m_[:, :], xt[:, :], AX.X), ins=[xt], outs=[m_], tag=f"r.s{si}")
        p.op("scalar", lambda e: e.activation(sq[:, :], xt[:, :], AF.Square, accum_out=ssq_[:, :]),
             ins=[xt], outs=[ab_raw, ssq_], tag=f"r.q{si}")
        p.op("vector", lambda e: e.tensor_scalar_mul(m_[:, :], m_[:, :], 1.0 / DM),
             ins=[m_], outs=[m_], tag=f"r.m{si}")
        p.op("vector", lambda e: e.tensor_tensor(var_[:, :], m_[:, :], m_[:, :], ALU.mult),
             ins=[m_], outs=[var_], tag=f"r.m2{si}")
        p.op("vector", lambda e: e.scalar_tensor_tensor(var_[:, :], ssq_[:, :], 1.0 / DM, var_[:, :],
                                                        ALU.mult, ALU.subtract),
             ins=[ssq_, var_], outs=[var_], tag=f"r.v{si}")
        p.op("scalar", lambda e: e.activation(rstd_[:, :], var_[:, :], AF.Sqrt, bias=eps_[:, :]),
             ins=[var_, eps_], outs=[rstd_], tag=f"r.sd{si}")
        p.op("vector", lambda e: e.reciprocal(rstd_[:, :], rstd_[:, :]), ins=[rstd_], outs=[rstd_], tag=f"r.r{si}")
        p.op("vector", lambda e: e.tensor_scalar(xn[:, :], xt[:, :], m_[:, :], rstd_[:, :],
                                                 ALU.subtract, ALU.mult),
             ins=[xt, m_, rstd_], outs=[xn], tag=f"r.xn{si}")
        for g in range(4):
            def emit_tr(e2, g=g):
                last = None
                for t in range(4):
                    c = g * 4 + t
                    last = e2.transpose(pA[:, t * 128:(t + 1) * 128],
                                        xn[:, c * 128:(c + 1) * 128], ident_f[:, :])
                return last
            p.op("tensor", emit_tr, ins=[xn, ident_f], outs=[pA], tag=f"r.tr{si}.{g}")
            p.op("vector", lambda e2, g=g: e2.tensor_copy(xnTf[:, g * 512:(g + 1) * 512], pA[:, :]),
                 ins=[pA], outs=[xnTf], tag=f"r.ev{si}.{g}")
        p.op("vector", lambda e2: e2.tensor_copy(xnTb[:, :], xnTf[:, :]),
             ins=[xnTf], outs=[xnTb], tag=f"r.bf{si}")
        dst = ap3(xnT_d, si * 128, [[S, 128], [128 * S, NTILE], [1, 128]])
        b.dma("scalar", dst, xnTb[:, :].rearrange("p (b c) -> p b c", c=128), tag=f"r.stT{si}")
        def emit_gate(e2):
            last = None
            for kc in range(NTILE):
                last = e2.matmul(pB[:, :E], xnTf[:, kc * 128:(kc + 1) * 128],
                                 gwT_sb[:, kc * E:(kc + 1) * E], start=(kc == 0), stop=(kc == NTILE - 1))
            return last
        p.op("tensor", emit_gate, ins=[xnTf, gwT_sb], outs=[pB], tag=f"r.g{si}")
        p.op("scalar", lambda e2: e2.activation(sc[:, :], pB[:, :E], AF.Sigmoid),
             ins=[pB], outs=[sc], tag=f"r.sig{si}")
        p.op("vector", lambda e2: e2.tensor_tensor(scb[:, :], sc[:, :], bias_b[:, :], ALU.add),
             ins=[sc, bias_b], outs=[scb], tag=f"r.sb{si}")
        p.op("vector", lambda e2: e2.max(mx8[:, :], scb[:, :]), ins=[scb], outs=[mx8], tag=f"r.mx{si}")
        p.op("vector", lambda e2: e2.max_index(mi8[:, :], mx8[:, :], scb[:, :]),
             ins=[mx8, scb], outs=[mi8], tag=f"r.mi{si}")
        p.op("vector", lambda e2: e2.tensor_copy(mif[:, :], mi8[:, :]), ins=[mi8], outs=[mif], tag=f"r.mf{si}")
        for k2, col in ((0, 0), (1, 1)):
            eqdst = eq_sb[:, si * 2 * E + k2 * E: si * 2 * E + (k2 + 1) * E]
            p.op("vector", lambda e2, col=col, eqdst=eqdst: e2.tensor_single_scalar(
                eqdst, iota_b[:, :], mif[:, col:col + 1], ALU.is_equal),
                ins=[iota_b, mif], outs=[eq_sb], tag=f"r.eq{k2}.{si}")
            p.op("vector", lambda e2, k2=k2, eqdst=eqdst: e2.tensor_tensor(
                eqt[:, :], eqdst, sc[:, :], ALU.mult),
                ins=[eq_sb, sc], outs=[eqt], tag=f"r.eqm{k2}.{si}")
            p.op("vector", lambda e2, k2=k2: e2.reduce_sum(tvals[:, k2:k2 + 1], eqt[:, :], AX.X),
                 ins=[eqt], outs=[tvals], tag=f"r.t{k2}.{si}")
        p.op("vector", lambda e2: e2.tensor_tensor(tvals[:, 2:3], tvals[:, 0:1], tvals[:, 1:2], ALU.add),
             ins=[tvals], outs=[tvals], tag=f"r.su{si}")
        p.op("vector", lambda e2: e2.tensor_single_scalar(tvals[:, 2:3], tvals[:, 2:3], 1e-20, ALU.add),
             ins=[tvals], outs=[tvals], tag=f"r.su2{si}")
        p.op("vector", lambda e2: e2.reciprocal(tvals[:, 3:4], tvals[:, 2:3]),
             ins=[tvals], outs=[tvals], tag=f"r.rc{si}")
        p.op("vector", lambda e2: e2.tensor_tensor(wts[:, 0:2], tvals[:, 0:2],
                                                   tvals[:, 3:4].broadcast_to([128, 2]), ALU.mult),
             ins=[tvals], outs=[wts], tag=f"r.w{si}")
        for k2, (dst_d, wcol) in enumerate(((A_d, 0), (Bq_d, 1))):
            p.op("vector", lambda e2, wcol=wcol: e2.tensor_scalar_mul(ab_raw[:, :], xn[:, :],
                                                                      wts[:, wcol:wcol + 1]),
                 ins=[xn, wts], outs=[ab_raw], tag=f"r.ab{k2}.{si}")
            qdq_emit(b, ab_raw[:, :], ab_bf[:, :], DM, scrW, ins=[ab_raw],
                     tag=f"r.abq{k2}.{si}", mode="pow2")
            b.dma("scalar", dst_d[si * 128:(si + 1) * 128, :], ab_bf[:, :], tag=f"r.abs{k2}.{si}")

    eqs_bf = b.sb("eqs_bf", [128, NTILE * E], BF16)
    for si in range(NTILE):
        p.op("vector", lambda e2, si=si: e2.tensor_tensor(
            eqs_bf[:, si * E:(si + 1) * E], eq_sb[:, si * 2 * E: si * 2 * E + E],
            eq_sb[:, si * 2 * E + E: si * 2 * E + 2 * E], ALU.add),
            ins=[eq_sb], outs=[eqs_bf], tag=f"t.es{si}")
    ones_colb = b.sb("ones_colb", [128, 1], BF16)
    ones128b = b.sb("ones128b", [128, 128], BF16)
    p.op("vector", lambda e2: e2.memset(ones_colb[:, :], 1.0), ins=[], outs=[ones_colb], tag="t.oc")
    p.op("vector", lambda e2: e2.memset(ones128b[:, :], 1.0), ins=[], outs=[ones128b], tag="t.o1")
    pairs = [(ones_colb[:, :], eqs_bf[:, si * E:(si + 1) * E]) for si in range(NTILE)]
    b.mm(pA[:1, :E], pairs, ins=[ones_colb, eqs_bf], tag="t.cnt")
    crow = b.sb("crow", [1, 3 * E + 2])
    p.op("vector", lambda e2: e2.tensor_copy(crow[:1, :E], pA[:1, :E]), ins=[pA], outs=[crow], tag="t.ce")
    ci = b.sb("ci", [1, 4 * E], I32)
    p.op("vector", lambda e2: e2.tensor_copy(ci[:1, :E], crow[:1, :E]), ins=[crow], outs=[ci], tag="t.i0")
    p.op("vector", lambda e2: e2.tensor_single_scalar(ci[:1, E:2 * E], ci[:1, :E], 255, ALU.add),
         ins=[ci], outs=[ci], tag="t.i1")
    p.op("vector", lambda e2: e2.tensor_single_scalar(ci[:1, E:2 * E], ci[:1, E:2 * E], -256,
                                                      ALU.bitwise_and),
         ins=[ci], outs=[ci], tag="t.i2")
    p.op("vector", lambda e2: e2.tensor_copy(ci[:1, 2 * E:3 * E], ci[:1, E:2 * E]),
         ins=[ci], outs=[ci], tag="t.s0")
    for sh in (1, 2, 4):
        p.op("vector", lambda e2, sh=sh: e2.tensor_tensor(
            ci[:1, 3 * E + sh - 1: 3 * E + sh - 1 + 1] if False else
            ci[:1, 2 * E + sh: 3 * E], ci[:1, 2 * E + sh: 3 * E],
            ci[:1, 2 * E: 3 * E - sh], ALU.add),
            ins=[ci], outs=[ci], tag=f"t.s{sh}")
    p.op("vector", lambda e2: e2.tensor_tensor(ci[:1, 3 * E:4 * E], ci[:1, 2 * E:3 * E],
                                               ci[:1, E:2 * E], ALU.subtract),
         ins=[ci], outs=[ci], tag="t.base")
    basef = b.sb("basef", [1, 2 * E])
    p.op("vector", lambda e2: e2.tensor_copy(basef[:1, :E], ci[:1, 3 * E:4 * E]),
         ins=[ci], outs=[basef], tag="t.bf")
    p.op("vector", lambda e2: e2.tensor_copy(basef[:1, E:2 * E], ci[:1, 2 * E:3 * E]),
         ins=[ci], outs=[basef], tag="t.ef")
    base_b = b.sb("base_b", [128, E])
    b.mm(pB[:, :E], [(ones1[:1, :], basef[:1, :E])], ins=[ones1, basef], tag="t.bb")
    p.op("vector", lambda e2: e2.tensor_copy(base_b[:, :], pB[:, :E]), ins=[pB], outs=[base_b], tag="t.bbe")
    urow = b.sb("urow", [1, NUP])
    uj = b.sb("uj", [1, NUP], I32)
    p.op("gpsimd", lambda e2: e2.iota(uj[:1, :], [[1, NUP]], channel_multiplier=0),
         ins=[], outs=[uj], tag="t.uio")
    p.op("vector", lambda e2: e2.tensor_copy(urow[:1, :], uj[:1, :]), ins=[uj], outs=[urow], tag="t.uc")
    p.op("vector", lambda e2: e2.tensor_single_scalar(urow[:1, :], urow[:1, :], float(SUP_P), ALU.mult),
         ins=[urow], outs=[urow], tag="t.um")
    e_u = b.sb("e_u", [1, NUP])
    tmp_u = b.sb("tmp_u", [1, NUP])
    p.op("vector", lambda e2: e2.memset(e_u[:1, :], 0.0), ins=[], outs=[e_u], tag="t.ez")
    for e in range(E - 1):
        p.op("vector", lambda e2, e=e: e2.tensor_single_scalar(
            tmp_u[:1, :], urow[:1, :], basef[:1, E + e:E + e + 1], ALU.is_ge),
            ins=[urow, basef], outs=[tmp_u], tag=f"t.ge{e}")
        p.op("vector", lambda e2: e2.tensor_tensor(e_u[:1, :], e_u[:1, :], tmp_u[:1, :], ALU.add),
             ins=[e_u, tmp_u], outs=[e_u], tag=f"t.ea{e}")
    PRE = b.sb("PRE", [128, NTILE * E])
    for ti in range(NTILE):
        pairs = [(ones128b[:, :], eqs_bf[:, kc * E:(kc + 1) * E]) for kc in range(ti)]
        pairs.append((tri0[:, :], eqs_bf[:, ti * E:(ti + 1) * E]))
        b.mm(pA[:, :E], pairs, ins=[ones128b, tri0, eqs_bf], tag=f"t.pre{ti}")
        p.op("vector", lambda e2, ti=ti: e2.tensor_copy(PRE[:, ti * E:(ti + 1) * E], pA[:, :E]),
             ins=[pA], outs=[PRE], tag=f"t.pree{ti}")
    slotAB = b.sb("slotAB", [128, NTILE * 2], I32)
    slt = b.sb("slt", [128, E])
    slf = b.sb("slf", [128, 1])
    for ti in range(NTILE):
        p.op("vector", lambda e2, ti=ti: e2.tensor_tensor(
            slt[:, :], PRE[:, ti * E:(ti + 1) * E], base_b[:, :], ALU.add),
            ins=[PRE, base_b], outs=[slt], tag=f"t.sl0{ti}")
        for k2 in range(2):
            p.op("vector", lambda e2, ti=ti, k2=k2: e2.tensor_tensor(
                eqt[:, :], slt[:, :], eq_sb[:, ti * 2 * E + k2 * E: ti * 2 * E + (k2 + 1) * E],
                ALU.mult),
                ins=[slt, eq_sb], outs=[eqt], tag=f"t.sl1{ti}.{k2}")
            p.op("vector", lambda e2: e2.reduce_sum(slf[:, :], eqt[:, :], AX.X),
                 ins=[eqt], outs=[slf], tag=f"t.sl2{ti}.{k2}")
            p.op("vector", lambda e2, ti=ti, k2=k2: e2.tensor_copy(
                slotAB[:, ti * 2 + k2: ti * 2 + k2 + 1], slf[:, :]),
                ins=[slf], outs=[slotAB], tag=f"t.sl3{ti}.{k2}")

    zt = xnTb[:, :DM]
    p.op("vector", lambda e2: e2.memset(zt[:, :], 0.0), ins=[], outs=[zt], tag="x.z")
    for r in range(ROWS // 128):
        b.dma("sync", Xp_d[r * 128:(r + 1) * 128, :], zt[:, :], tag=f"x.z{r}")
    xnT_s = b.sb("xnT_s", [128, NTILE * 512], BF16)
    at = xnT_s[:, :DM]
    bt = xnT_s[:, DM:2 * DM]
    for ti in range(NTILE):
        b.dma("sync", at[:, :], A_d[ti * 128:(ti + 1) * 128, :], tag=f"x.la{ti}")
        b.dma("sync", bt[:, :], Bq_d[ti * 128:(ti + 1) * 128, :], tag=f"x.lb{ti}")
        for k2, src in ((0, at), (1, bt)):
            idx = slotAB[:, ti * 2 + k2: ti * 2 + k2 + 1]
            out_ap = ap3(Xp_d, 0, [[DM, ROWS], [1, DM]])
            p.op("gpsimd", lambda e2, out_ap=out_ap, idx=idx, src=src: e2.indirect_dma_start(
                out_ap, bass.IndirectOffsetOnAxis(idx, 0), src[:, :], None),
                ins=[src, slotAB], outs=[Xp_d], dma=True, tag=f"x.sc{ti}.{k2}")

    SUPS = 512
    wA = b.sb("wA0", [128, NTILE * 512], BF16)
    wB = b.sb("wB0", [128, NTILE * 512], BF16)
    w2sl = b.sb("w2sl0", [128, 2 * DM], BF16)
    hqT = b.sb("hqT", [128, NTILE * SUPS], BF16)
    hh = b.sb("hh", [128, 512])
    t1 = b.sb("t1", [128, 512])
    hqb = b.sb("hqb", [128, 512], BF16)
    scrH = dict(amax=scrW["amax"], ibits=scrW["ibits"], scl=scrW["scl"], inv=scrW["inv"],
                t=scrW["t"][:, :512], q8=scrW["q8"][:, :512])
    for u in range(S // SUPS):
        src = ap3(xnT_d, u * SUPS, [[S, 128], [128 * S, NTILE], [1, SUPS]])
        b.dma("sync", xnT_s[:, :].rearrange("p (b c) -> p b c", c=SUPS), src, tag=f"u{u}.xnT")
        for j in range(HID // 512):
            b.dma("sync", wA[:, :].rearrange("p (b c) -> p b c", c=512),
                  ap3(sw1b_d, j * 512, [[HID, 128], [128 * HID, NTILE], [1, 512]]), tag=f"u{u}.s1.{j}")
            b.dma("sync", wB[:, :].rearrange("p (b c) -> p b c", c=512),
                  ap3(sw3b_d, j * 512, [[HID, 128], [128 * HID, NTILE], [1, 512]]), tag=f"u{u}.s3.{j}")
            for sub in range(4):
                pairs_a = [(xnT_s[:, kc * SUPS + sub * 128: kc * SUPS + sub * 128 + 128],
                            wA[:, kc * 512:(kc + 1) * 512]) for kc in range(NTILE)]
                pairs_b = [(xnT_s[:, kc * SUPS + sub * 128: kc * SUPS + sub * 128 + 128],
                            wB[:, kc * 512:(kc + 1) * 512]) for kc in range(NTILE)]
                b.mm(pA[:, :], pairs_a, ins=[xnT_s, wA], tag=f"u{u}.sh{j}.{sub}.a")
                b.mm(pB[:, :], pairs_b, ins=[xnT_s, wB], tag=f"u{u}.sh{j}.{sub}.b")
                p.op("scalar", lambda e2: e2.activation(t1[:, :], pA[:, :], AF.Silu),
                     ins=[pA], outs=[t1], tag=f"u{u}.sh{j}.{sub}.si")
                p.op("vector", lambda e2: e2.tensor_tensor(hh[:, :], t1[:, :], pB[:, :], ALU.mult),
                     ins=[t1, pB], outs=[hh], tag=f"u{u}.sh{j}.{sub}.h")
                p.op("vector", lambda e2: e2.tensor_copy(hqb[:, :], hh[:, :]),
                     ins=[hh], outs=[hqb], tag=f"u{u}.sh{j}.{sub}.c")
                def emit_tr(e2):
                    last = None
                    for t in range(4):
                        last = e2.transpose(ptr_bf[:, t * 128:(t + 1) * 128],
                                            hqb[:, t * 128:(t + 1) * 128], ident_bf[:, :])
                    return last
                p.op("tensor", emit_tr, ins=[hqb, ident_bf], outs=[ptr_bf], tag=f"u{u}.sh{j}.{sub}.tr")
                dst = ap3(hqT, (4 * j) * SUPS + sub * 128, [[NTILE * SUPS, 128], [SUPS, 4], [1, 128]])
                p.op("vector", lambda e2, dst=dst: e2.tensor_copy(
                    dst, ptr_bf[:, :512].rearrange("p (b c) -> p b c", c=128)),
                    ins=[ptr_bf], outs=[hqT], tag=f"u{u}.sh{j}.{sub}.sc")
        for sub in range(4):
            for kg in range(8):
                b.dma("sync", w2sl[:, :].rearrange("p (b c) -> p b c", c=DM),
                      ap3(sw2b_d, kg * 2 * 128 * DM, [[DM, 128], [128 * DM, 2], [1, DM]]),
                      tag=f"u{u}.w2s.{sub}.{kg}")
                def emit_mm(e2, kg=kg, sub=sub):
                    last = None
                    for h2 in range(2):
                        kc = kg * 2 + h2
                        for n in range(4):
                            last = e2.matmul(pQ[:, n * 512:(n + 1) * 512],
                                             hqT[:, kc * SUPS + sub * 128: kc * SUPS + sub * 128 + 128],
                                             w2sl[:, h2 * DM + n * 512: h2 * DM + (n + 1) * 512],
                                             start=(kc == 0), stop=(kc == 15))
                    return last
                p.op("tensor", emit_mm, ins=[hqT, w2sl, pQ], outs=[pQ], tag=f"u{u}.w2m.{sub}.{kg}")
            p.op("vector", lambda e2, sub=sub: e2.tensor_copy(acc[:, sub * DM:(sub + 1) * DM], pQ[:, :]),
                 ins=[pQ], outs=[acc], tag=f"u{u}.w2acc{sub}")
            r0 = u * SUPS + sub * 128
            b.dma("scalar", sh_d[r0:r0 + 128, :], acc[:, sub * DM:(sub + 1) * DM], tag=f"u{u}.shs{sub}")

    iota2d = b.sb("iota2d", [128, NTILE], I32)
    p.op("gpsimd", lambda e2: e2.iota(iota2d[:, :], [[128, NTILE]], channel_multiplier=1),
         ins=[], outs=[iota2d], tag="e.iota")
    iota2f = b.sb("iota2f", [128, NTILE])
    p.op("vector", lambda e2: e2.tensor_copy(iota2f[:, :], iota2d[:, :]), ins=[iota2d], outs=[iota2f], tag="e.iof")
    ebc = b.sb("ebc", [128, NUP])
    b.mm(pB[:, :NUP], [(ones1[:1, :], e_u[:1, :])], ins=[ones1, e_u], tag="e.eb")
    p.op("vector", lambda e2: e2.tensor_copy(ebc[:, :], pB[:, :NUP]), ins=[pB], outs=[ebc], tag="e.ebe")
    idxf = b.sb("idxf", [128, NTILE])
    idxi = b.sb("idxi", [128, NTILE], I32)
    xp = w2sl[:, :DM]
    ceT = xnT_s[:, :NTILE * SUP_P]
    hqTp = hqT[:, :NTILE * SUP_P]
    wfull = [wA[:, :DM], wB[:, :DM]]
    outp_t = acc[:, :DM]
    for u in range(NUP):
        p.op("vector", lambda e2, u=u: e2.scalar_tensor_tensor(
            idxf[:, :], ebc[:, u:u + 1].broadcast_to([128, NTILE]), 2048.0, iota2f[:, :],
            ALU.mult, ALU.add),
            ins=[ebc, iota2f], outs=[idxf], tag=f"e{u}.ix0")
        p.op("vector", lambda e2: e2.tensor_copy(idxi[:, :], idxf[:, :]), ins=[idxf], outs=[idxi], tag=f"e{u}.ix1")
        for sub in range(SUP_P // 128):
            r0 = u * SUP_P + sub * 128
            b.dma("sync", xp[:, :], Xp_d[r0:r0 + 128, :], tag=f"e{u}.{sub}.lx")
            for g in range(2):
                def emit_tr(e2, g=g):
                    last = None
                    for t in range(8):
                        c = g * 8 + t
                        last = e2.transpose(ptr_bf[:, t * 128:(t + 1) * 128],
                                            xp[:, c * 128:(c + 1) * 128], ident_bf[:, :])
                    return last
                p.op("tensor", emit_tr, ins=[xp, ident_bf], outs=[ptr_bf], tag=f"e{u}.{sub}.tr{g}")
                dst = ap3(xnT_s, (g * 8) * SUP_P + sub * 128, [[NTILE * 512, 128], [SUP_P, 8], [1, 128]])
                p.op("vector", lambda e2, dst=dst: e2.tensor_copy(
                    dst, ptr_bf[:, :1024].rearrange("p (b c) -> p b c", c=128)),
                    ins=[ptr_bf], outs=[ceT], tag=f"e{u}.{sub}.sc{g}")
        for sub in range(SUP_P // 128):
            for half in range(2):
                for kc in range(NTILE):
                    wf = wfull[kc % 2]
                    src = ap3(w13qT_d, 0, [[2 * HID, E * DM], [1, 2 * HID]])
                    p.op("gpsimd", lambda e2, src=src, kc=kc, wf=wf, half=half: e2.indirect_dma_start(
                        wf[:, :], None, src, bass.IndirectOffsetOnAxis(idxi[:, kc:kc + 1], 0),
                        element_offset=half * 2048),
                        ins=[w13qT_d, idxi], outs=[wf], dma=True, tag=f"e{u}.{sub}.w{half}.{kc}")
                    def emit_mm(e2, kc=kc, wf=wf, sub=sub):
                        last = None
                        for n in range(4):
                            last = e2.matmul(pQ[:, n * 512:(n + 1) * 512],
                                             ceT[:, kc * SUP_P + sub * 128: kc * SUP_P + sub * 128 + 128],
                                             wf[:, n * 512:(n + 1) * 512],
                                             start=(kc == 0), stop=(kc == NTILE - 1))
                        return last
                    p.op("tensor", emit_mm, ins=[ceT, wf, pQ], outs=[pQ], tag=f"e{u}.{sub}.m{half}.{kc}")
                for jj in range(2):
                    j = half * 2 + jj
                    p.op("scalar", lambda e2, jj=jj: e2.activation(
                        t1[:, :], pQ[:, jj * 1024: jj * 1024 + 512], AF.Silu),
                        ins=[pQ], outs=[t1], tag=f"e{u}.{sub}.si{j}")
                    p.op("vector", lambda e2, jj=jj: e2.tensor_tensor(
                        hh[:, :], t1[:, :], pQ[:, jj * 1024 + 512: jj * 1024 + 1024], ALU.mult),
                        ins=[t1, pQ], outs=[hh], tag=f"e{u}.{sub}.h{j}")
                    qdq_emit(b, hh[:, :], hqb[:, :], 512, scrH, ins=[hh],
                             tag=f"e{u}.{sub}.q{j}", mode="pow2")
                    def emit_tr(e2):
                        last = None
                        for t in range(4):
                            last = e2.transpose(ptr_bf[:, t * 128:(t + 1) * 128],
                                                hqb[:, t * 128:(t + 1) * 128], ident_bf[:, :])
                        return last
                    p.op("tensor", emit_tr, ins=[hqb, ident_bf], outs=[ptr_bf], tag=f"e{u}.{sub}.tr{j}")
                    dst = ap3(hqT, (4 * j) * SUP_P + sub * 128,
                              [[NTILE * 512, 128], [SUP_P, 4], [1, 128]])
                    p.op("vector", lambda e2, dst=dst: e2.tensor_copy(
                        dst, ptr_bf[:, :512].rearrange("p (b c) -> p b c", c=128)),
                        ins=[ptr_bf], outs=[hqTp], tag=f"e{u}.{sub}.hs{j}")
            for kc in range(NTILE):
                wf = wfull[kc % 2]
                src = ap3(w2qT_d, 0, [[DM, E * HID], [1, DM]])
                p.op("gpsimd", lambda e2, src=src, kc=kc, wf=wf: e2.indirect_dma_start(
                    wf[:, :], None, src, bass.IndirectOffsetOnAxis(idxi[:, kc:kc + 1], 0)),
                    ins=[w2qT_d, idxi], outs=[wf], dma=True, tag=f"e{u}.{sub}.v{kc}")
                def emit_mm(e2, kc=kc, wf=wf, sub=sub):
                    last = None
                    for n in range(4):
                        last = e2.matmul(pQ[:, n * 512:(n + 1) * 512],
                                         hqTp[:, kc * SUP_P + sub * 128: kc * SUP_P + sub * 128 + 128],
                                         wf[:, n * 512:(n + 1) * 512],
                                         start=(kc == 0), stop=(kc == NTILE - 1))
                    return last
                p.op("tensor", emit_mm, ins=[hqTp, wf, pQ], outs=[pQ], tag=f"e{u}.{sub}.w2m{kc}")
            p.op("vector", lambda e2: e2.tensor_copy(outp_t[:, :], pQ[:, :]),
                 ins=[pQ], outs=[outp_t], tag=f"e{u}.{sub}.oe")
            r0 = u * SUP_P + sub * 128
            b.dma("scalar", Op_d[r0:r0 + 128, :], outp_t[:, :], tag=f"e{u}.{sub}.os")

    oa = acc[:, :DM]
    ob = acc[:, DM:2 * DM]
    sht = acc[:, 2 * DM:3 * DM]
    for ti in range(NTILE):
        srcO = ap3(Op_d, 0, [[DM, ROWS], [1, DM]])
        idxa = slotAB[:, ti * 2: ti * 2 + 1]
        idxb = slotAB[:, ti * 2 + 1: ti * 2 + 2]
        p.op("gpsimd", lambda e2, idxa=idxa: e2.indirect_dma_start(
            oa[:, :], None, srcO, bass.IndirectOffsetOnAxis(idxa, 0)),
            ins=[Op_d, slotAB], outs=[oa], dma=True, tag=f"f.ga{ti}")
        p.op("gpsimd", lambda e2, idxb=idxb: e2.indirect_dma_start(
            ob[:, :], None, srcO, bass.IndirectOffsetOnAxis(idxb, 0)),
            ins=[Op_d, slotAB], outs=[ob], dma=True, tag=f"f.gb{ti}")
        b.dma("sync", sht[:, :], sh_d[ti * 128:(ti + 1) * 128, :], tag=f"f.sh{ti}")
        b.dma("sync", xt[:, :], x_in[ti * 128:(ti + 1) * 128, :], tag=f"f.x{ti}")
        p.op("vector", lambda e2: e2.tensor_tensor(oa[:, :], oa[:, :], ob[:, :], ALU.add),
             ins=[oa, ob], outs=[oa], tag=f"f.a0{ti}")
        p.op("vector", lambda e2: e2.tensor_tensor(oa[:, :], sht[:, :], oa[:, :], ALU.add),
             ins=[oa, sht], outs=[oa], tag=f"f.a1{ti}")
        p.op("vector", lambda e2: e2.scalar_tensor_tensor(oa[:, :], oa[:, :], 0.5, xt[:, :],
                                                          ALU.mult, ALU.add),
             ins=[oa, xt], outs=[oa], tag=f"f.a2{ti}")
        b.dma("scalar", x_out[ti * 128:(ti + 1) * 128, :], oa[:, :], tag=f"f.st{ti}")


def l0ffn_inputs(x_core, gate_w, ebias, w13, w2, sw1, sw2, sw3):
    import ml_dtypes
    E, HID = 8, 2048
    return {
        "x_in": np.ascontiguousarray(x_core, np.float32),
        "gate_wT": np.ascontiguousarray(gate_w.T, np.float32),
        "ebias": np.ascontiguousarray(ebias.reshape(1, E), np.float32),
        "w13": np.ascontiguousarray(w13.reshape(E * 2 * HID, DM), np.float32),
        "w2": np.ascontiguousarray(w2.reshape(E * DM, HID), np.float32),
        "sw1T": np.ascontiguousarray(sw1.T, np.float32),
        "sw3T": np.ascontiguousarray(sw3.T, np.float32),
        "sw2T": np.ascontiguousarray(sw2.T, np.float32),
        "ident_bf": np.eye(128, dtype=ml_dtypes.bfloat16),
        "ident_f32": np.eye(128, dtype=np.float32),
        "ones1": np.ones((1, 128), np.float32),
        "iota8": np.arange(E, dtype=np.float32).reshape(1, E),
    }


def emit_l0att(b, x_in, x_out, P):
    H, G, DH, F = 16, 4, 128, 768
    b.push_scope()
    p = b.p
    WqT, WkT, WvT = P["WqT0"], P["WkT0"], P["WvT0"]
    WgT, WmbT = P["WgT0"], P["WmbT0"]
    bias_all, memnorm = P["bias_all0"], P["memnorm0"]
    ident_in, ident_fin, ones_in = P["ident_bf"], P["ident_f32"], P["ones1"]

    xnT_d = b.dram("xnT_d", [DM, S], BF16)
    xn_d = b.dram("xn_d", [S, DM])
    q_d = b.dram("q_d", [S, 2048], BF16)
    k_d = b.dram("k_d", [S, 512], BF16)
    v_d = b.dram("v_d", [S, 512], BF16)
    mb_d = b.dram("mb_d", [S, 2048], BF16)
    g_d = b.dram("g_d", [S, 2048], BF16)
    wq_b = b.dram("wq_b", [DM, 2048], BF16)
    wk_b = b.dram("wk_b", [DM, 512], BF16)
    wv_b = b.dram("wv_b", [DM, 512], BF16)
    wg_b = b.dram("wg_b", [DM, 2048], BF16)
    wmb_b = b.dram("wmb_b", [DM, 2048], BF16)
    cc_in = b.dram("cc_in", [H * F, 129])
    cc_out = b.dram("cc_out", [H * F, 129])

    ident_bf = b.sb("identb", [128, 128], BF16)
    ident_f = b.sb("identf", [128, 128], F32)
    ones_bf = b.sb("ones_bf", [1, 128], BF16)
    ones_f = b.sb("ones_f", [1, 128])
    brow = b.sb("brow", [1, 2048 * 3 + 512 * 2])
    brow_bf = b.sb("brow_bf", [1, 2048 * 3 + 512 * 2], BF16)
    b.dma("sync", ident_bf[:, :], ident_in[:, :], tag="c0")
    b.dma("sync", ident_f[:, :], ident_fin[:, :], tag="c1")
    b.dma("sync", ones_f[:, :], ones_in[:, :], tag="c2")
    b.dma("sync", brow[:, :], bias_all[:, :], tag="c3")
    p.op("vector", lambda e: e.tensor_copy(ones_bf[:1, :], ones_f[:1, :]),
         ins=[ones_f], outs=[ones_bf], tag="c4")
    p.op("vector", lambda e: e.tensor_copy(brow_bf[:1, :], brow[:1, :]),
         ins=[brow], outs=[brow_bf], tag="c5")
    OFF_BQ, OFF_BG, OFF_BMB, OFF_BK, OFF_BV = 0, 2048, 4096, 6144, 6656

    ptr_bf = b.ps("ptr_bf", [128, 1024], BF16)
    pP = b.ps("pP", [128, 512])
    pN = b.ps("pN", [128, 512])
    pD = b.ps("pD", [128, 1024])

    wrow = b.sb("wrow", [128, DM])
    wbf = b.sb("wbf", [128, DM], BF16)
    for (src, dst, O, tg) in [(WqT, wq_b, 2048, "wq"), (WkT, wk_b, 512, "wk"),
                              (WvT, wv_b, 512, "wv"), (WgT, wg_b, 2048, "wg"),
                              (WmbT, wmb_b, 2048, "wmb")]:
        for r in range(DM // 128):
            b.dma("sync", wrow[:, :O], src[r * 128:(r + 1) * 128, :], tag=f"{tg}.l{r}")
            p.op("vector", lambda e: e.tensor_copy(wbf[:, :O], wrow[:, :O]),
                 ins=[wrow], outs=[wbf], tag=f"{tg}.c{r}")
            b.dma("scalar", dst[r * 128:(r + 1) * 128, :], wbf[:, :O], tag=f"{tg}.s{r}")

    xt = b.sb("xt", [128, DM])
    xn = b.sb("xn", [128, DM])
    xnb = b.sb("xnb", [128, DM], BF16)
    sq = b.sb("sq", [128, DM])
    stage = b.sb("stage", [128, 1024], BF16)
    stats = dict(m=b.sb("m", [128, 1]), ssq=b.sb("ssq", [128, 1]),
                 var=b.sb("var", [128, 1]), rstd=b.sb("rstd", [128, 1]),
                 eps=b.sb("epsc", [128, 1]))
    emit_ln_xnT(b, x_in, xnT_d, xt, xn, xnb, sq, stats, ptr_bf, stage, ident_bf, xn_d=xn_d)

    xnTt = b.sb("xnTt", [128, NTILE * 128], BF16)
    wsl = b.sb("wsl", [128, NTILE * 512], BF16)
    pout = b.sb("pout", [128, 512])
    poutb = b.sb("poutb", [128, 512], BF16)
    projs = [
        (wk_b, 512, OFF_BK, k_d, None, "pk"),
        (wv_b, 512, OFF_BV, v_d, None, "pv"),
        (wmb_b, 2048, OFF_BMB, mb_d, AF.Sigmoid, "pmb"),
        (wq_b, 2048, OFF_BQ, q_d, None, "pq"),
        (wg_b, 2048, OFF_BG, g_d, AF.Sigmoid, "pg"),
    ]
    for (w_b, O, boff, out_d, act, tg) in projs:
        for os_ in range(O // 512):
            wsrc = ap3(w_b, os_ * 512, [[O, 128], [128 * O, NTILE], [1, 512]])
            b.dma("sync", wsl[:, :].rearrange("p (b c) -> p b c", c=512), wsrc, tag=f"{tg}.w{os_}")
            for si in range(NTILE):
                xsrc = ap3(xnT_d, si * 128, [[S, 128], [128 * S, NTILE], [1, 128]])
                b.dma("sync", xnTt[:, :].rearrange("p (b c) -> p b c", c=128), xsrc, tag=f"{tg}.x{os_}.{si}")
                pairs = [(xnTt[:, kc * 128:(kc + 1) * 128], wsl[:, kc * 512:(kc + 1) * 512])
                         for kc in range(NTILE)]
                b.mm(pP[:, :], pairs, ins=[xnTt, wsl, ones_bf, brow_bf], tag=f"{tg}.m{os_}.{si}",
                     extra=(ones_bf[:1, :], brow_bf[:1, boff + os_ * 512: boff + (os_ + 1) * 512]))
                if act is None:
                    p.op("vector", lambda e: e.tensor_copy(poutb[:, :], pP[:, :]),
                         ins=[pP], outs=[poutb], tag=f"{tg}.e{os_}.{si}")
                else:
                    p.op("scalar", lambda e, act=act: e.activation(poutb[:, :], pP[:, :], act),
                         ins=[pP], outs=[poutb], tag=f"{tg}.e{os_}.{si}")
                b.dma("scalar", out_d[si * 128:(si + 1) * 128, os_ * 512:(os_ + 1) * 512],
                      poutb[:, :], tag=f"{tg}.s{os_}.{si}")

    cbuf = b.sb("cbuf", [128, 256])
    mkraw = b.sb("mkraw", [128, F])
    dpsq = b.sb("dpsq", [128, F])
    dnorm = b.sb("dnorm", [128, 1])
    drr = b.sb("drr", [128, 1])
    mkt_bf = b.sb("mkt_bf", [128, F], BF16)

    def emit_dpfp(src_ap, ins_src, dst_bf_ap, dst_bufs, mksq_ap, mksq_bufs, tg):
        p.op("scalar", lambda e: e.activation(cbuf[:, :128], src_ap, AF.Relu),
             ins=ins_src, outs=[cbuf], tag=f"{tg}.r0")
        p.op("scalar", lambda e: e.activation(cbuf[:, 128:], src_ap, AF.Relu, scale=-1.0),
             ins=ins_src, outs=[cbuf], tag=f"{tg}.r1")
        for r in range(3):
            j = r + 1
            p.op("vector", lambda e, r=r, j=j: e.tensor_tensor(
                mkraw[:, r * 256 + j: r * 256 + 256], cbuf[:, j:256], cbuf[:, 0:256 - j], ALU.mult),
                ins=[cbuf], outs=[mkraw], tag=f"{tg}.ro{r}a")
            p.op("vector", lambda e, r=r, j=j: e.tensor_tensor(
                mkraw[:, r * 256: r * 256 + j], cbuf[:, 0:j], cbuf[:, 256 - j:256], ALU.mult),
                ins=[cbuf], outs=[mkraw], tag=f"{tg}.ro{r}b")
        p.op("scalar", lambda e: e.activation(dpsq[:, :], mkraw[:, :], AF.Square, accum_out=dnorm[:, :]),
             ins=[mkraw], outs=[dpsq, dnorm], tag=f"{tg}.ss")
        p.op("scalar", lambda e: e.activation(dnorm[:, :], dnorm[:, :], AF.Sqrt),
             ins=[dnorm], outs=[dnorm], tag=f"{tg}.sq")
        p.op("vector", lambda e: e.tensor_single_scalar(dnorm[:, :], dnorm[:, :], 1e-12, ALU.max),
             ins=[dnorm], outs=[dnorm], tag=f"{tg}.mx")
        p.op("vector", lambda e: e.reciprocal(drr[:, :], dnorm[:, :]),
             ins=[dnorm], outs=[drr], tag=f"{tg}.rc")
        p.op("vector", lambda e: e.tensor_scalar_mul(mkraw[:, :], mkraw[:, :], drr[:, :]),
             ins=[mkraw, drr], outs=[mkraw], tag=f"{tg}.nm")
        if mksq_ap is not None:
            p.op("scalar", lambda e: e.activation(dpsq[:, :], mkraw[:, :], AF.Square,
                                                  accum_out=mksq_ap),
                 ins=[mkraw], outs=[dpsq] + mksq_bufs, tag=f"{tg}.s2")
        p.op("vector", lambda e: e.tensor_copy(dst_bf_ap, mkraw[:, :]),
             ins=[mkraw], outs=dst_bufs, tag=f"{tg}.bf")

    kg = b.sb("kg", [128, NTILE * 128], BF16)
    vg = b.sb("vg", [128, NTILE * 128], BF16)
    mk_sb = b.sb("mk_sb", [128, NTILE * F], BF16)
    mkT_sb = b.sb("mkT_sb", [128, 12384], BF16)
    mksq_sb = b.sb("mksq_sb", [128, NTILE])
    om_sb = b.sb("om_sb", [128, 6 * 129], BF16)
    omf = b.sb("omf", [128, 6 * 129])
    mbh = b.sb("mbh", [128, NTILE * 128], BF16)
    num_sb = b.sb("num_sb", [128, 129])
    rr = b.sb("rr", [128, 1])
    pv_sb = b.sb("pv_sb", [128, 128])
    wmv = b.sb("wmv", [128, 128])
    coef = b.sb("coef", [128, 1])
    rhs129 = b.sb("rhs129", [128, 129], BF16)
    dd_sb = b.sb("dd_sb", [128, 6 * 129])

    for g in range(G):
        ksrc = ap3(k_d, g * 128, [[512, 128], [128 * 512, NTILE], [1, 128]])
        b.dma("sync", kg[:, :].rearrange("p (b c) -> p b c", c=128), ksrc, tag=f"kg{g}")
        vsrc = ap3(v_d, g * 128, [[512, 128], [128 * 512, NTILE], [1, 128]])
        b.dma("sync", vg[:, :].rearrange("p (b c) -> p b c", c=128), vsrc, tag=f"vg{g}")
        for si in range(NTILE):
            emit_dpfp(kg[:, si * 128:(si + 1) * 128], [kg],
                      mk_sb[:, si * F:(si + 1) * F], [mk_sb],
                      mksq_sb[:, si:si + 1], [mksq_sb], f"dk{g}.{si}")
            def emit_tr(e):
                last = None
                for t in range(6):
                    last = e.transpose(ptr_bf[:, (t % 4) * 128:(t % 4) * 128 + 128],
                                       mk_sb[:, si * F + t * 128: si * F + (t + 1) * 128],
                                       ident_bf[:, :]) if False else None
                return last
            for gg in range(2):
                def emit_tr2(e, gg=gg, si=si):
                    last = None
                    for t in range(3):
                        c = gg * 3 + t
                        last = e.transpose(ptr_bf[:, t * 128:(t + 1) * 128],
                                           mk_sb[:, si * F + c * 128: si * F + (c + 1) * 128],
                                           ident_bf[:, :])
                    return last
                p.op("tensor", emit_tr2, ins=[mk_sb, ident_bf], outs=[ptr_bf], tag=f"ktr{g}.{si}.{gg}")
                dst = ap3(mkT_sb, (gg * 3) * S + si * 128, [[12384, 128], [S, 3], [1, 128]])
                p.op("vector", lambda e, dst=dst: e.tensor_copy(
                    dst, ptr_bf[:, :384].rearrange("p (b c) -> p b c", c=128)),
                    ins=[ptr_bf], outs=[mkT_sb], tag=f"kev{g}.{si}.{gg}")
        for hq in range(4 * g, 4 * g + 4):
            b.dma("sync", omf[:, :].rearrange("p (b c) -> p b c", c=129),
                  ap3(memnorm, hq * F * 129, [[129, 128], [128 * 129, 6], [1, 129]]), tag=f"om{hq}")
            p.op("vector", lambda e: e.tensor_copy(om_sb[:, :], omf[:, :]),
                 ins=[omf], outs=[om_sb], tag=f"omb{hq}")
            b.dma("sync", mbh[:, :].rearrange("p (b c) -> p b c", c=128),
                  ap3(mb_d, hq * 128, [[2048, 128], [128 * 2048, NTILE], [1, 128]]), tag=f"mb{hq}")
            for si in range(NTILE):
                pairs = [(mkT_sb[:, fc * S + si * 128: fc * S + (si + 1) * 128],
                          om_sb[:, fc * 129:(fc + 1) * 129]) for fc in range(6)]
                b.mm(pN[:, :129], pairs, ins=[mkT_sb, om_sb], tag=f"num{hq}.{si}")
                p.op("vector", lambda e: e.tensor_copy(num_sb[:, :], pN[:, :129]),
                     ins=[pN], outs=[num_sb], tag=f"nev{hq}.{si}")
                p.op("vector", lambda e: e.tensor_single_scalar(num_sb[:, 128:129], num_sb[:, 128:129],
                                                                1e-8, ALU.add),
                     ins=[num_sb], outs=[num_sb], tag=f"den{hq}.{si}")
                p.op("vector", lambda e: e.reciprocal(rr[:, :], num_sb[:, 128:129]),
                     ins=[num_sb], outs=[rr], tag=f"rc{hq}.{si}")
                p.op("vector", lambda e: e.tensor_scalar_mul(pv_sb[:, :], num_sb[:, :128], rr[:, :]),
                     ins=[num_sb, rr], outs=[pv_sb], tag=f"pv{hq}.{si}")
                p.op("vector", lambda e, si=si: e.tensor_tensor(
                    wmv[:, :], vg[:, si * 128:(si + 1) * 128], pv_sb[:, :], ALU.subtract),
                    ins=[vg, pv_sb], outs=[wmv], tag=f"mv{hq}.{si}")
                p.op("vector", lambda e, si=si: e.tensor_tensor(
                    wmv[:, :], wmv[:, :], mbh[:, si * 128:(si + 1) * 128], ALU.mult),
                    ins=[wmv, mbh], outs=[wmv], tag=f"wm{hq}.{si}")
                p.op("vector", lambda e, si=si: e.reciprocal(coef[:, :], mksq_sb[:, si:si + 1]),
                     ins=[mksq_sb], outs=[coef], tag=f"cf0{hq}.{si}")
                p.op("vector", lambda e: e.tensor_tensor(coef[:, :], coef[:, :], num_sb[:, 128:129],
                                                         ALU.mult),
                     ins=[coef, num_sb], outs=[coef], tag=f"cf1{hq}.{si}")
                p.op("vector", lambda e: e.tensor_scalar(coef[:, :], coef[:, :], -1.0, 1.0,
                                                         ALU.mult, ALU.add),
                     ins=[coef], outs=[coef], tag=f"cf2{hq}.{si}")
                p.op("vector", lambda e: e.tensor_scalar(coef[:, :], coef[:, :], 1.0, 0.0,
                                                         ALU.min, ALU.max),
                     ins=[coef], outs=[coef], tag=f"cf3{hq}.{si}")
                p.op("vector", lambda e: e.tensor_copy(rhs129[:, :128], wmv[:, :]),
                     ins=[wmv], outs=[rhs129], tag=f"rh0{hq}.{si}")
                p.op("vector", lambda e: e.tensor_copy(rhs129[:, 128:129], coef[:, :]),
                     ins=[coef], outs=[rhs129], tag=f"rh1{hq}.{si}")
                def emit_dmm(e, si=si):
                    last = None
                    for fc in range(6):
                        last = e.matmul(pD[:, fc * 129:(fc + 1) * 129],
                                        mk_sb[:, si * F + fc * 128: si * F + (fc + 1) * 128],
                                        rhs129[:, :], start=(si == 0), stop=(si == NTILE - 1))
                    return last
                p.op("tensor", emit_dmm, ins=[mk_sb, rhs129, pD], outs=[pD], tag=f"dmm{hq}.{si}")
            p.op("vector", lambda e: e.tensor_copy(dd_sb[:, :774], pD[:, :774]),
                 ins=[pD], outs=[dd_sb], tag=f"dev{hq}")
            b.dma("scalar", ap3(cc_in, hq * F * 129, [[129, 128], [128 * 129, 6], [1, 129]]),
                  dd_sb[:, :774].rearrange("p (b c) -> p b c", c=129), tag=f"dst{hq}")

    p.op("gpsimd", lambda e: e.collective_compute(
        "AllReduce", ALU.add, replica_groups=[list(range(8))],
        ins=[cc_in[:, :]], outs=[cc_out[:, :]]), ins=[cc_in], outs=[cc_out], tag="ar")

    nm_sb = mkT_sb[:, :H * 6 * 129]
    for hq in range(H):
        b.dma("sync", omf[:, :].rearrange("p (b c) -> p b c", c=129),
              ap3(memnorm, hq * F * 129, [[129, 128], [128 * 129, 6], [1, 129]]), tag=f"nm0{hq}")
        b.dma("sync", dd_sb[:, :774].rearrange("p (b c) -> p b c", c=129),
              ap3(cc_out, hq * F * 129, [[129, 128], [128 * 129, 6], [1, 129]]), tag=f"nm1{hq}")
        p.op("vector", lambda e, hq=hq: e.tensor_tensor(
            nm_sb[:, hq * 774:(hq + 1) * 774], omf[:, :774], dd_sb[:, :774], ALU.add),
            ins=[omf, dd_sb], outs=[nm_sb], tag=f"nm2{hq}")

    qt = b.sb("qt", [128, 2048], BF16)
    gt = b.sb("gt", [128, 2048], BF16)
    mh_sb = b.sb("mh_sb", [128, 2048])
    mqT = b.sb("mqT", [128, 6 * 128], BF16)
    for si in range(NTILE):
        b.dma("sync", qt[:, :], q_d[si * 128:(si + 1) * 128, :], tag=f"q.l{si}")
        b.dma("sync", gt[:, :], g_d[si * 128:(si + 1) * 128, :], tag=f"q.g{si}")
        for hq in range(H):
            emit_dpfp(qt[:, hq * 128:(hq + 1) * 128], [qt],
                      mkt_bf[:, :], [mkt_bf], None, None, f"dq{si}.{hq}")
            for gg in range(2):
                def emit_tr2(e, gg=gg):
                    last = None
                    for t in range(3):
                        c = gg * 3 + t
                        last = e.transpose(ptr_bf[:, t * 128:(t + 1) * 128],
                                           mkt_bf[:, c * 128:(c + 1) * 128], ident_bf[:, :])
                    return last
                p.op("tensor", emit_tr2, ins=[mkt_bf, ident_bf], outs=[ptr_bf], tag=f"qtr{si}.{hq}.{gg}")
                dst = ap3(mqT, (gg * 3) * 128, [[6 * 128, 128], [128, 3], [1, 128]])
                p.op("vector", lambda e, dst=dst: e.tensor_copy(
                    dst, ptr_bf[:, :384].rearrange("p (b c) -> p b c", c=128)),
                    ins=[ptr_bf], outs=[mqT], tag=f"qev{si}.{hq}.{gg}")
            pairs = [(mqT[:, fc * 128:(fc + 1) * 128],
                      nm_sb[:, hq * 774 + fc * 129: hq * 774 + (fc + 1) * 129]) for fc in range(6)]
            b.mm(pN[:, :129], pairs, ins=[mqT, nm_sb], tag=f"qnum{si}.{hq}")
            p.op("vector", lambda e: e.tensor_copy(num_sb[:, :], pN[:, :129]),
                 ins=[pN], outs=[num_sb], tag=f"qnev{si}.{hq}")
            p.op("vector", lambda e: e.tensor_single_scalar(num_sb[:, 128:129], num_sb[:, 128:129],
                                                            1e-8, ALU.add),
                 ins=[num_sb], outs=[num_sb], tag=f"qden{si}.{hq}")
            p.op("vector", lambda e: e.reciprocal(rr[:, :], num_sb[:, 128:129]),
                 ins=[num_sb], outs=[rr], tag=f"qrc{si}.{hq}")
            p.op("vector", lambda e, hq=hq: e.tensor_scalar_mul(
                mh_sb[:, hq * 128:(hq + 1) * 128], num_sb[:, :128], rr[:, :]),
                ins=[num_sb, rr], outs=[mh_sb], tag=f"qmh{si}.{hq}")
        b.dma("sync", xn[:, :], xn_d[si * 128:(si + 1) * 128, :], tag=f"q.xn{si}")
        b.dma("sync", xt[:, :], x_in[si * 128:(si + 1) * 128, :], tag=f"q.x{si}")
        p.op("vector", lambda e: e.tensor_tensor(mh_sb[:, :], mh_sb[:, :], gt[:, :], ALU.mult),
             ins=[mh_sb, gt], outs=[mh_sb], tag=f"q.gm{si}")
        p.op("vector", lambda e: e.tensor_tensor(xn[:, :], xn[:, :], mh_sb[:, :], ALU.add),
             ins=[xn, mh_sb], outs=[xn], tag=f"q.a1{si}")
        p.op("vector", lambda e: e.tensor_tensor(xn[:, :], xn[:, :], xt[:, :], ALU.add),
             ins=[xn, xt], outs=[xn], tag=f"q.a2{si}")
        b.dma("scalar", x_out[si * 128:(si + 1) * 128, :], xn[:, :], tag=f"q.st{si}")


def l0att_inputs(x_core, Wq, bq, Wk, bk, Wv, bv, Wg, bg, Wmb, bmb, memory, norm):
    import ml_dtypes
    bias_all = np.concatenate([bq, bg, bmb, bk, bv]).reshape(1, -1)
    memnorm = np.concatenate([memory, norm[:, :, None]], axis=2).reshape(16 * 768, 129)
    return {
        "x_in": np.ascontiguousarray(x_core, np.float32),
        "WqT": np.ascontiguousarray(Wq.T, np.float32),
        "WkT": np.ascontiguousarray(Wk.T, np.float32),
        "WvT": np.ascontiguousarray(Wv.T, np.float32),
        "WgT": np.ascontiguousarray(Wg.T, np.float32),
        "WmbT": np.ascontiguousarray(Wmb.T, np.float32),
        "bias_all": np.ascontiguousarray(bias_all, np.float32),
        "memnorm": np.ascontiguousarray(memnorm, np.float32),
        "ident_bf": np.eye(128, dtype=ml_dtypes.bfloat16),
        "ident_f32": np.eye(128, dtype=np.float32),
        "ones1": np.ones((1, 128), np.float32),
    }


def emit_l1att(b, x_in, x_out, P):
    H, G, DH = 16, 4, 128
    WIN, GLB = 256, 64
    SCALE = 1.0 / np.sqrt(128.0)
    closest = 2.0 ** np.floor(np.log2(H))
    base = 2.0 ** (-(2.0 ** (-(np.log2(closest) - 3.0))))
    slopes = [float(base ** k) for k in range(1, H + 1)]

    b.push_scope()
    p = b.p
    Wq, Wkv, Wo, bo = P["Wq1"], P["Wkv1"], P["Wo1"], P["bo1"]
    ident_in, ones_in = P["ident_bf"], P["ones1"]

    xnT_d = b.dram("xnT_d", [DM, S], BF16)
    q_d = b.dram("q_d", [S, 2048], BF16)
    ao_d = b.dram("ao_d", [S, 2048], BF16)
    aoT_d = b.dram("aoT_d", [DM, S], BF16)
    wq_b = b.dram("wq_b", [DM, 2048], BF16)
    wkv_b = b.dram("wkv_b", [DM, 1024], BF16)
    wo_b = b.dram("wo_b", [DM, 2048], BF16)

    ident_bf = b.sb("identb", [128, 128], BF16)
    ones_f = b.sb("ones_f", [1, 128])
    ones_bf = b.sb("ones_bf", [1, 128], BF16)
    borow = b.sb("borow", [1, 2048])
    borow_bf = b.sb("borow_bf", [1, 2048], BF16)
    b.dma("sync", ident_bf[:, :], ident_in[:, :], tag="c0")
    b.dma("sync", ones_f[:, :], ones_in[:, :], tag="c1")
    b.dma("sync", borow[:, :], bo[:, :], tag="c2")
    p.op("vector", lambda e: e.tensor_copy(ones_bf[:1, :], ones_f[:1, :]),
         ins=[ones_f], outs=[ones_bf], tag="c3")
    p.op("vector", lambda e: e.tensor_copy(borow_bf[:1, :], borow[:1, :]),
         ins=[borow], outs=[borow_bf], tag="c4")

    ptr_bf = b.ps("ptr_bf", [128, 1024], BF16)
    pS = b.ps("pS", [128, 2048])
    pP = b.ps("pP", [128, 512])
    pAV = b.ps("pAV", [128, 512])

    wrow = b.sb("wrow", [128, DM])
    wbf = b.sb("wbf", [128, DM], BF16)
    for (src, dst, O, tg) in [(Wq, wq_b, 2048, "wq"), (Wkv, wkv_b, 1024, "wkv"), (Wo, wo_b, 2048, "wo")]:
        for r in range(DM // 128):
            b.dma("sync", wrow[:, :O], src[r * 128:(r + 1) * 128, :], tag=f"{tg}.l{r}")
            p.op("vector", lambda e: e.tensor_copy(wbf[:, :O], wrow[:, :O]),
                 ins=[wrow], outs=[wbf], tag=f"{tg}.c{r}")
            b.dma("scalar", dst[r * 128:(r + 1) * 128, :], wbf[:, :O], tag=f"{tg}.s{r}")

    xt = b.sb("xt", [128, DM])
    xn = b.sb("xn", [128, DM])
    xnb = b.sb("xnb", [128, DM], BF16)
    sq = b.sb("sq", [128, DM])
    stage = b.sb("stage", [128, 1024], BF16)
    stats = dict(m=b.sb("m", [128, 1]), ssq=b.sb("ssq", [128, 1]),
                 var=b.sb("var", [128, 1]), rstd=b.sb("rstd", [128, 1]),
                 eps=b.sb("epsc", [128, 1]))
    cw_sb = b.sb("cw_sb", [128, NTILE])

    def cw_cb(si):
        p.op("vector", lambda e: e.reduce_sum(stats["m"][:, :], xn[:, :], AX.X,
                                              apply_absolute_value=True),
             ins=[xn], outs=[stats["m"]], tag=f"cw.s{si}")
        p.op("scalar", lambda e, si=si: e.activation(cw_sb[:, si:si + 1], stats["m"][:, :], AF.Sigmoid),
             ins=[stats["m"]], outs=[cw_sb], tag=f"cw.g{si}")

    emit_ln_xnT(b, x_in, xnT_d, xt, xn, xnb, sq, stats, ptr_bf, stage, ident_bf,
                extra_tile_cb=cw_cb)

    xnTt = b.sb("xnTt", [128, NTILE * 128], BF16)
    wsl = b.sb("wsl", [128, NTILE * 512], BF16)
    poutb = b.sb("poutb", [128, 512], BF16)
    kT_sb = b.sb("kT_sb", [128, G * S], BF16)
    v_sb = b.sb("v_sb", [128, NTILE * 512], BF16)
    for os_ in range(4):
        b.dma("sync", wsl[:, :].rearrange("p (b c) -> p b c", c=512),
              ap3(wq_b, os_ * 512, [[2048, 128], [128 * 2048, NTILE], [1, 512]]), tag=f"q.w{os_}")
        for si in range(NTILE):
            b.dma("sync", xnTt[:, :].rearrange("p (b c) -> p b c", c=128),
                  ap3(xnT_d, si * 128, [[S, 128], [128 * S, NTILE], [1, 128]]), tag=f"q.x{os_}.{si}")
            pairs = [(xnTt[:, kc * 128:(kc + 1) * 128], wsl[:, kc * 512:(kc + 1) * 512])
                     for kc in range(NTILE)]
            b.mm(pP[:, :], pairs, ins=[xnTt, wsl], tag=f"q.m{os_}.{si}")
            p.op("vector", lambda e, si=si: e.tensor_scalar(poutb[:, :], pP[:, :],
                                                            cw_sb[:, si:si + 1], SCALE,
                                                            ALU.mult, ALU.mult),
                 ins=[pP, cw_sb], outs=[poutb], tag=f"q.e{os_}.{si}")
            b.dma("scalar", q_d[si * 128:(si + 1) * 128, os_ * 512:(os_ + 1) * 512],
                  poutb[:, :], tag=f"q.s{os_}.{si}")
    for os_ in range(2):
        b.dma("sync", wsl[:, :].rearrange("p (b c) -> p b c", c=512),
              ap3(wkv_b, os_ * 512, [[1024, 128], [128 * 1024, NTILE], [1, 512]]), tag=f"kv.w{os_}")
        for si in range(NTILE):
            b.dma("sync", xnTt[:, :].rearrange("p (b c) -> p b c", c=128),
                  ap3(xnT_d, si * 128, [[S, 128], [128 * S, NTILE], [1, 128]]), tag=f"kv.x{os_}.{si}")
            pairs = [(xnTt[:, kc * 128:(kc + 1) * 128], wsl[:, kc * 512:(kc + 1) * 512])
                     for kc in range(NTILE)]
            b.mm(pP[:, :], pairs, ins=[xnTt, wsl], tag=f"kv.m{os_}.{si}")
            p.op("vector", lambda e: e.tensor_copy(poutb[:, :], pP[:, :]),
                 ins=[pP], outs=[poutb], tag=f"kv.e{os_}.{si}")
            if os_ == 1:
                p.op("vector", lambda e, si=si: e.tensor_copy(v_sb[:, si * 512:(si + 1) * 512],
                                                              poutb[:, :]),
                     ins=[poutb], outs=[v_sb], tag=f"kv.v{si}")
            else:
                for g in range(G):
                    def emit_tr(e, g=g):
                        return e.transpose(ptr_bf[:, g * 128:(g + 1) * 128],
                                           poutb[:, g * 128:(g + 1) * 128], ident_bf[:, :])
                    p.op("tensor", emit_tr, ins=[poutb, ident_bf], outs=[ptr_bf], tag=f"kv.t{si}.{g}")
                dst = ap3(kT_sb, si * 128, [[G * S, 128], [S, 4], [1, 128]])
                p.op("vector", lambda e, dst=dst: e.tensor_copy(
                    dst, ptr_bf[:, :512].rearrange("p (b c) -> p b c", c=128)),
                    ins=[ptr_bf], outs=[kT_sb], tag=f"kv.kt{si}")

    jrow = b.sb("jrow", [128, S])
    ppcol = b.sb("ppcol", [128, 1])
    globj = b.sb("globj", [128, S], I32)
    neg9 = b.sb("neg9", [128, S])
    ji = b.sb("ji", [128, S], I32)
    p.op("gpsimd", lambda e: e.iota(ji[:, :], [[1, S]], channel_multiplier=0),
         ins=[], outs=[ji], tag="iotaj")
    p.op("vector", lambda e: e.tensor_copy(jrow[:, :], ji[:, :]), ins=[ji], outs=[jrow], tag="jrow")
    p.op("gpsimd", lambda e: e.iota(ji[:, :1], [[0, 1]], channel_multiplier=1),
         ins=[], outs=[ji], tag="iotap")
    p.op("vector", lambda e: e.tensor_copy(ppcol[:, :], ji[:, :1]), ins=[ji], outs=[ppcol], tag="pcol")
    p.op("vector", lambda e: e.tensor_single_scalar(globj[:, :], jrow[:, :], 64.0, ALU.is_lt),
         ins=[jrow], outs=[globj], tag="globj")
    p.op("vector", lambda e: e.memset(neg9[:, :], -1e9), ins=[], outs=[neg9], tag="neg9")

    qt = b.sb("qt", [128, 2048], BF16)
    dd = b.sb("dd", [128, S])
    am = b.sb("am", [128, S], I32)
    u_sb = b.sb("u_sb", [128, S])
    eu = b.sb("eu", [128, S], BF16)
    pb_sb = b.sb("pb_sb", [128, S], BF16)
    pT = b.sb("pT", [128, NTILE * 128], BF16)
    ao_sb = b.sb("ao_sb", [128, 2048], BF16)
    mx = b.sb("mxr", [128, 1])
    sm = b.sb("smr", [128, 1])
    rr = b.sb("rrr", [128, 1])
    qTt = b.sb("qTt", [128, 128], BF16)
    for si in range(NTILE):
        b.dma("sync", qt[:, :], q_d[si * 128:(si + 1) * 128, :], tag=f"a.q{si}")
        p.op("vector", lambda e, si=si: e.tensor_scalar(dd[:, :], jrow[:, :], ppcol[:, :],
                                                        float(si * 128), ALU.subtract,
                                                        ALU.subtract),
             ins=[jrow, ppcol], outs=[dd], tag=f"a.d0{si}")
        p.op("scalar", lambda e: e.activation(dd[:, :], dd[:, :], AF.Abs),
             ins=[dd], outs=[dd], tag=f"a.d1{si}")
        p.op("vector", lambda e: e.tensor_single_scalar(am[:, :], dd[:, :], float(WIN), ALU.is_le),
             ins=[dd], outs=[am], tag=f"a.m0{si}")
        p.op("vector", lambda e: e.tensor_tensor(am[:, :], am[:, :], globj[:, :], ALU.max),
             ins=[am, globj], outs=[am], tag=f"a.m1{si}")
        if si == 0:
            p.op("vector", lambda e: e.memset(am[:64, :], 1), ins=[], outs=[am], tag="a.m2")
        for h in range(H):
            g = h // 4
            def emit_qT(e, h=h):
                return e.transpose(ptr_bf[:, :128], qt[:, h * 128:(h + 1) * 128], ident_bf[:, :])
            p.op("tensor", emit_qT, ins=[qt, ident_bf], outs=[ptr_bf], tag=f"a.qT{si}.{h}")
            p.op("vector", lambda e: e.tensor_copy(qTt[:, :], ptr_bf[:, :128]),
                 ins=[ptr_bf], outs=[qTt], tag=f"a.qTe{si}.{h}")
            def emit_sc(e, g=g):
                last = None
                for n in range(4):
                    last = e.matmul(pS[:, n * 512:(n + 1) * 512], qTt[:, :],
                                    kT_sb[:, g * S + n * 512: g * S + (n + 1) * 512],
                                    start=True, stop=True)
                return last
            p.op("tensor", emit_sc, ins=[qTt, kT_sb], outs=[pS], tag=f"a.sc{si}.{h}")
            p.op("vector", lambda e, h=h: e.scalar_tensor_tensor(
                u_sb[:, :], dd[:, :], -slopes[h], pS[:, :], ALU.mult, ALU.add),
                ins=[dd, pS], outs=[u_sb], tag=f"a.u{si}.{h}")
            p.op("vector", lambda e: e.copy_predicated(u_sb[:, :], am[:, :], neg9[:, :]),
                 ins=[am, neg9], outs=[u_sb], tag=f"a.cp{si}.{h}")
            p.op("vector", lambda e: e.reduce_max(mx[:, :], u_sb[:, :], AX.X, negate=True),
                 ins=[u_sb], outs=[mx], tag=f"a.mx{si}.{h}")
            p.op("scalar", lambda e: e.activation(eu[:, :], u_sb[:, :], AF.Exp, bias=mx[:, :],
                                                  accum_out=sm[:, :]),
                 ins=[u_sb, mx], outs=[eu, sm], tag=f"a.ex{si}.{h}")
            p.op("vector", lambda e: e.reciprocal(rr[:, :], sm[:, :]),
                 ins=[sm], outs=[rr], tag=f"a.rc{si}.{h}")
            p.op("vector", lambda e: e.tensor_scalar_mul(pb_sb[:, :], eu[:, :], rr[:, :]),
                 ins=[eu, rr], outs=[pb_sb], tag=f"a.pb{si}.{h}")
            for gg in range(2):
                def emit_tr(e, gg=gg):
                    last = None
                    for t in range(8):
                        c = gg * 8 + t
                        last = e.transpose(ptr_bf[:, t * 128:(t + 1) * 128],
                                           pb_sb[:, c * 128:(c + 1) * 128], ident_bf[:, :])
                    return last
                p.op("tensor", emit_tr, ins=[pb_sb, ident_bf], outs=[ptr_bf], tag=f"a.pt{si}.{h}.{gg}")
                dst = ap3(pT, (gg * 8) * 128, [[NTILE * 128, 128], [128, 8], [1, 128]])
                p.op("vector", lambda e, dst=dst: e.tensor_copy(
                    dst, ptr_bf[:, :1024].rearrange("p (b c) -> p b c", c=128)),
                    ins=[ptr_bf], outs=[pT], tag=f"a.pte{si}.{h}.{gg}")
            def emit_av(e, g=g):
                last = None
                for jc in range(NTILE):
                    last = e.matmul(pAV[:, :128], pT[:, jc * 128:(jc + 1) * 128],
                                    v_sb[:, jc * 512 + g * 128: jc * 512 + (g + 1) * 128],
                                    start=(jc == 0), stop=(jc == NTILE - 1))
                return last
            p.op("tensor", emit_av, ins=[pT, v_sb, pAV], outs=[pAV], tag=f"a.av{si}.{h}")
            p.op("vector", lambda e, h=h: e.tensor_copy(ao_sb[:, h * 128:(h + 1) * 128], pAV[:, :128]),
                 ins=[pAV], outs=[ao_sb], tag=f"a.ae{si}.{h}")
        b.dma("scalar", ao_d[si * 128:(si + 1) * 128, :], ao_sb[:, :], tag=f"a.st{si}")
        for gg in range(2):
            def emit_tr(e, gg=gg):
                last = None
                for t in range(8):
                    c = gg * 8 + t
                    last = e.transpose(ptr_bf[:, t * 128:(t + 1) * 128],
                                       ao_sb[:, c * 128:(c + 1) * 128], ident_bf[:, :])
                return last
            p.op("tensor", emit_tr, ins=[ao_sb, ident_bf], outs=[ptr_bf], tag=f"a.ot{si}.{gg}")
            p.op("vector", lambda e: e.tensor_copy(stage[:, :1024], ptr_bf[:, :1024]),
                 ins=[ptr_bf], outs=[stage], tag=f"a.oe{si}.{gg}")
            dst = ap3(aoT_d, (gg * 8) * 128 * S + si * 128, [[S, 128], [128 * S, 8], [1, 128]])
            b.dma("scalar", dst, stage[:, :1024].rearrange("p (b c) -> p b c", c=128),
                  tag=f"a.os{si}.{gg}")

    outf = b.sb("outf", [128, 512])
    for os_ in range(4):
        b.dma("sync", wsl[:, :].rearrange("p (b c) -> p b c", c=512),
              ap3(wo_b, os_ * 512, [[2048, 128], [128 * 2048, NTILE], [1, 512]]), tag=f"o.w{os_}")
        for si in range(NTILE):
            b.dma("sync", xnTt[:, :].rearrange("p (b c) -> p b c", c=128),
                  ap3(aoT_d, si * 128, [[S, 128], [128 * S, NTILE], [1, 128]]), tag=f"o.x{os_}.{si}")
            pairs = [(xnTt[:, kc * 128:(kc + 1) * 128], wsl[:, kc * 512:(kc + 1) * 512])
                     for kc in range(NTILE)]
            b.mm(pP[:, :], pairs, ins=[xnTt, wsl, ones_bf, borow_bf], tag=f"o.m{os_}.{si}",
                 extra=(ones_bf[:1, :], borow_bf[:1, os_ * 512:(os_ + 1) * 512]))
            b.dma("sync", xt[:, :512], x_in[si * 128:(si + 1) * 128, os_ * 512:(os_ + 1) * 512],
                  tag=f"o.r{os_}.{si}")
            p.op("vector", lambda e: e.tensor_tensor(outf[:, :], pP[:, :], xt[:, :512], ALU.add),
                 ins=[pP, xt], outs=[outf], tag=f"o.e{os_}.{si}")
            b.dma("scalar", x_out[si * 128:(si + 1) * 128, os_ * 512:(os_ + 1) * 512],
                  outf[:, :], tag=f"o.s{os_}.{si}")


def l1att_inputs(x_core, Wq, Wkv, Wo, bo):
    import ml_dtypes
    return {
        "x_in": np.ascontiguousarray(x_core, np.float32),
        "Wq": np.ascontiguousarray(Wq, np.float32),
        "Wkv": np.ascontiguousarray(Wkv, np.float32),
        "Wo": np.ascontiguousarray(Wo, np.float32),
        "bo": np.ascontiguousarray(bo.reshape(1, -1), np.float32),
        "ident_bf": np.eye(128, dtype=ml_dtypes.bfloat16),
        "ones1": np.ones((1, 128), np.float32),
    }


def make_params(b):
    E, HID, I, F, H = 8, 2048, 8192, 768, 16
    names = {
        "gwT0": [DM, E], "eb0": [1, E], "w13": [E * 2 * HID, DM], "w2": [E * DM, HID],
        "sw1T": [DM, HID], "sw3T": [DM, HID], "sw2T": [HID, DM],
        "WqT0": [DM, 2048], "WkT0": [DM, 512], "WvT0": [DM, 512],
        "WgT0": [DM, 2048], "WmbT0": [DM, 2048],
        "bias_all0": [1, 2048 * 3 + 512 * 2], "memnorm0": [H * F, 129],
        "gate_w1": [I, DM], "up_w1": [I, DM], "down_w1": [DM, I],
        "Wq1": [DM, 2048], "Wkv1": [DM, 1024], "Wo1": [DM, 2048], "bo1": [1, 2048],
        "ident_f32": [128, 128], "ones1": [1, 128], "iota8": [1, E],
    }
    P = {k: b.dram(k, sh, F32, kind="ExternalInput") for k, sh in names.items()}
    P["ident_bf"] = b.dram("ident_bf", [128, 128], BF16, kind="ExternalInput")
    P["tri0_bf"] = b.dram("tri0_bf", [128, 128], BF16, kind="ExternalInput")
    return P


def build_fused():
    nc = bass.Bass("TRN2", target_bir_lowering=False)
    b = B(nc)
    P = make_params(b)
    x0 = b.dram("x0", [S, DM], F32, kind="ExternalInput")
    x_out = b.dram("x_out", [S, DM], F32, kind="ExternalOutput")
    xc = [b.dram(f"xch{i}", [S, DM], F32, shared=True) for i in range(5)]

    b.pfx = "f0."
    emit_l0ffn(b, x0, xc[0], P, prep=True)
    b.end_phase()
    b.pfx = "a0."
    emit_l0att(b, xc[0], xc[1], P)
    b.end_phase()
    b.pfx = "f0b."
    emit_l0ffn(b, xc[1], xc[2], P, prep=False)
    b.end_phase()
    b.pfx = "f1."
    emit_l1ffn(b, xc[2], xc[3], P, prep=True)
    b.end_phase()
    b.pfx = "a1."
    emit_l1att(b, xc[3], xc[4], P)
    b.end_phase()
    b.pfx = "f1b."
    emit_l1ffn(b, xc[4], x_out, P, prep=False)
    b.end_phase()
    b.pfx = ""
    b.p.op("sync", lambda e: e.nop(), ins=[x_out], tag="final")
    b.p.build()
    return nc, b


def stage_params(L0_attn_Wq, L0_attn_bq, L0_attn_Wk, L0_attn_bk, L0_attn_Wv, L0_attn_bv,
                 L0_attn_Wg, L0_attn_bg, L0_attn_Wmb, L0_attn_bmb, L0_attn_memory,
                 L0_attn_norm, L0_ffn_gate_w, L0_ffn_w13, L0_ffn_w2, L0_ffn_sw1,
                 L0_ffn_sw2, L0_ffn_sw3, L0_ffn_expert_bias, L1_attn_Wq, L1_attn_Wkv,
                 L1_attn_Wo, L1_attn_bo, L1_ffn_gate_w, L1_ffn_up_w, L1_ffn_down_w):
    import ml_dtypes
    E, HID = 8, 2048
    c = lambda a: np.ascontiguousarray(a, np.float32)
    bias_all = np.concatenate([L0_attn_bq, L0_attn_bg, L0_attn_bmb, L0_attn_bk,
                               L0_attn_bv]).reshape(1, -1)
    memnorm = np.concatenate([L0_attn_memory, L0_attn_norm[:, :, None]], axis=2)
    return {
        "gwT0": c(L0_ffn_gate_w.T), "eb0": c(L0_ffn_expert_bias.reshape(1, E)),
        "w13": c(L0_ffn_w13.reshape(E * 2 * HID, DM)), "w2": c(L0_ffn_w2.reshape(E * DM, HID)),
        "sw1T": c(L0_ffn_sw1.T), "sw3T": c(L0_ffn_sw3.T), "sw2T": c(L0_ffn_sw2.T),
        "WqT0": c(L0_attn_Wq.T), "WkT0": c(L0_attn_Wk.T), "WvT0": c(L0_attn_Wv.T),
        "WgT0": c(L0_attn_Wg.T), "WmbT0": c(L0_attn_Wmb.T),
        "bias_all0": c(bias_all), "memnorm0": c(memnorm.reshape(16 * 768, 129)),
        "gate_w1": c(L1_ffn_gate_w), "up_w1": c(L1_ffn_up_w), "down_w1": c(L1_ffn_down_w),
        "Wq1": c(L1_attn_Wq), "Wkv1": c(L1_attn_Wkv), "Wo1": c(L1_attn_Wo),
        "bo1": c(L1_attn_bo.reshape(1, -1)),
        "ident_f32": np.eye(128, dtype=np.float32),
        "ones1": np.ones((1, 128), np.float32),
        "iota8": np.arange(E, dtype=np.float32).reshape(1, E),
        "ident_bf": np.eye(128, dtype=ml_dtypes.bfloat16),
        "tri0_bf": np.triu(np.ones((128, 128), ml_dtypes.bfloat16), 1),
    }


_FUSED = []


def build_single(phase):
    nc = bass.Bass("TRN2", target_bir_lowering=False)
    b = B(nc)
    P = make_params(b)
    x_in = b.dram("x0", [S, DM], F32, kind="ExternalInput")
    x_out = b.dram("x_out", [S, DM], F32, kind="ExternalOutput")
    b.pfx = "s."
    if phase == "l0ffn":
        emit_l0ffn(b, x_in, x_out, P, prep=True)
    elif phase == "l0att":
        emit_l0att(b, x_in, x_out, P)
    elif phase == "l1ffn":
        emit_l1ffn(b, x_in, x_out, P, prep=True)
    elif phase == "l1att":
        emit_l1att(b, x_in, x_out, P)
    b.end_phase()
    b.pfx = ""
    b.p.op("sync", lambda e: e.nop(), ins=[x_out], tag="final")
    b.p.build()
    return nc, b


def kernel(L0_attn_hidden_states, **params):
    from concourse.bass_utils import run_bass_kernel_spmd

    if not _FUSED:
        _FUSED.append(build_fused())
    nc, _b = _FUSED[0]
    x = np.ascontiguousarray(np.asarray(L0_attn_hidden_states, np.float32))
    pmap = stage_params(**params)
    in_maps = [dict(pmap, x0=x[c]) for c in range(8)]
    res = run_bass_kernel_spmd(nc, in_maps, core_ids=list(range(8)))
    return np.stack([res.results[c]["x_out"] for c in range(8)], axis=0).astype(np.float32)
